# version 03
# baseline (speedup 1.0000x reference)
import math
from contextlib import ExitStack

import numpy as np
import ml_dtypes

import concourse.bass as bass
import concourse.mybir as mybir
from concourse.bass_utils import run_bass_kernel_spmd

F32 = mybir.dt.float32
BF16 = mybir.dt.bfloat16
I32 = mybir.dt.int32
AF = mybir.ActivationFunctionType
ALU = mybir.AluOpType
AX = mybir.AxisListType
bfdt = ml_dtypes.bfloat16

B, S, D = 4, 4096, 2048
HID = 5632
NH, NKV, HD = 16, 4, 128
TOK = 2048
CH = 512
NTC = TOK // CH
NFC = D // 128
NHC = HID // 128
SQD = math.sqrt(128.0)
ISQD = 1.0 / SQD
ACONST = -16.0
KTN = [12, 16, 28, 32]
NEG = -1.0e30
LN_EPS = 1e-5
Q_EPS = 1e-8
NGRP = 8
FFN_GRPS = [list(range(g, min(g + 6, NHC))) for g in range(0, NHC, 6)]

PHASES_ALL = ("P0", "PW", "PG", "P1", "P2", "P3", "P4", "P9")
_PHASES = PHASES_ALL


def chunk_starts(core):
    return [0, 512, 3072, 3584] if core % 2 == 0 else [1024, 1536, 2048, 2560]


def _maskpos():
    re_ = [(t0 + CH) // 128 for t0 in chunk_starts(0)]
    ro_ = [(t0 + CH) // 128 for t0 in chunk_starts(1)]
    out = []
    for s_i in range(NTC):
        out.append(sorted(set(range(re_[s_i] - 4, re_[s_i])) |
                          set(range(ro_[s_i] - 4, ro_[s_i]))))
    return out


MASKPOS = _maskpos()

def g2rank_idx(g):
    if g < 1024:
        return 0, g
    if g < 3072:
        return 1, g - 1024
    return 0, g - 2048


def _slopes16():
    start = 2.0 ** (-(2.0 ** -(math.log2(16) - 3)))
    return np.array([start * start ** i for i in range(16)], dtype=np.float32)


def host_consts(core):
    c = {}
    pos = np.concatenate([np.arange(t0, t0 + CH) for t0 in chunk_starts(core)])
    ar = np.arange(0, HD, 2, dtype=np.float32) / np.float32(HD)
    inv = (np.float32(1.0) / (np.float32(10000.0) ** ar)).astype(np.float32)
    t = np.arange(S, dtype=np.float32)
    freqs = (t[:, None] * inv[None, :]).astype(np.float32)
    emb = np.concatenate([freqs, freqs], axis=1)
    c["ropecos"] = np.ascontiguousarray(np.cos(emb).astype(np.float32).T[:, pos])
    c["ropesin"] = np.ascontiguousarray(np.sin(emb).astype(np.float32).T[:, pos])
    R = np.zeros((HD, HD), dtype=np.float32)
    for d in range(64):
        R[d, d + 64] = -1.0
        R[64 + d, d] = 1.0
    c["rotT"] = np.ascontiguousarray(R.T).astype(bfdt)
    c["identf"] = np.eye(128, dtype=np.float32)
    c["identb"] = np.eye(128, dtype=bfdt)
    c["onesb"] = np.ones((128, 128), dtype=bfdt)
    slopes = _slopes16()
    j = np.arange(CH, dtype=np.float32)
    c["crow"] = (-(SQD) * slopes[:, None] * j[None, :]).astype(bfdt)
    eh = np.zeros((128, NH * 128), dtype=np.float32)
    for h in range(NH):
        eh[h, h * 128:(h + 1) * 128] = 1.0
    c["ehmat"] = eh.astype(bfdt)
    kk = np.arange(128)[:, None]
    jj = np.arange(CH)[None, :]
    masks = np.concatenate([(kk + d > jj).astype(np.float32) * NEG
                            for d in (0, 128, 256, 384)], axis=1)
    c["masks"] = np.ascontiguousarray(masks.astype(np.float32))
    starts = chunk_starts(core)
    kb = np.zeros((NTC, 36, 128, NH), dtype=np.float32)
    kkv = np.arange(128, dtype=np.float32)
    for s_i, t0 in enumerate(starts):
        R_real = (t0 + CH) // 128
        for it in range(KTN[s_i]):
            if it < R_real:
                gk = 128 * it + kkv
                kb[s_i, it, :, :] = slopes[None, :] * (gk[:, None] - t0)
            else:
                kb[s_i, it] = NEG
    c["kbias"] = kb
    c["maskpos"] = MASKPOS
    mdyn = np.zeros((NTC, 8, 128, CH), dtype=np.float32)
    kk2 = np.arange(128)[:, None]
    jj2 = np.arange(CH)[None, :]
    for s_i, t0 in enumerate(starts):
        R_real = (t0 + CH) // 128
        for j, it in enumerate(MASKPOS[s_i]):
            if R_real - 4 <= it < R_real:
                d = 128 * it - t0
                mdyn[s_i, j] = (kk2 + d > jj2).astype(np.float32) * NEG
    c["mdyn"] = mdyn.astype(bfdt)
    off = np.zeros((NTC, 36, 2), dtype=np.int64)
    c["kvoff"] = off.astype(np.int32)
    c["self4"] = np.concatenate(
        [np.tile((np.arange(4) == w).astype(np.float32)[:, None], (1, 128))
         for w in range(3)], axis=1)
    return c


ENGS = ("tensor", "vector", "scalar", "gpsimd", "sync")


class Sem:
    __slots__ = ("h", "n", "name")

    def __init__(self, h, name):
        self.h, self.n, self.name = h, 0, name

    def bump(self, k=1):
        self.n += k
        return self.n


class Ctx:
    def __init__(self, nc, st):
        self.nc = nc
        self.st = st
        self.q = {e: [] for e in ENGS}
        self._semn = 0
        self.chain = {e: None for e in ("vector", "scalar", "gpsimd")}
        self.last = {e: None for e in ("vector", "scalar", "gpsimd")}

    def sem(self, name=None):
        self._semn += 1
        h = self.st.enter_context(self.nc.semaphore(name or f"s{self._semn}"))
        return Sem(h, name)

    def sb(self, name, shape, dt):
        return self.st.enter_context(self.nc.sbuf_tensor(name, shape, dt))

    def emit(self, eng, fn):
        self.q[eng].append(fn)

    def wait(self, eng, sem, val):
        if val > 0:
            self.emit(eng, lambda e, s=sem.h, v=val: e.wait_ge(s, v))

    def op(self, eng, fn, sem=None, k=1, drain=False):
        tgt = sem.bump(k) if sem is not None else None
        if eng in self.chain:
            if self.chain[eng] is None:
                self.chain[eng] = self.sem("ch_" + eng)
            ch = self.chain[eng]
            prev = self.last[eng]
            if sem is not None:
                semh, tv, delta = sem.h, tgt, k
            else:
                tv = ch.bump(1)
                semh, delta = ch.h, 1
            if prev is not None:
                self.emit(eng, lambda e, s=prev[0], v=prev[1]: e.wait_ge(s, v))

            def run(e, fn=fn, semh=semh, delta=delta):
                ins = fn(e)
                (ins if ins is not None else e.nop()).then_inc(semh, delta)
            self.emit(eng, run)
            self.last[eng] = (semh, tv)
        else:
            def run(e, fn=fn, sem=sem, k=k, drain=drain):
                ins = fn(e)
                if sem is not None:
                    if drain:
                        e.drain().then_inc(sem.h, k)
                    elif ins is not None:
                        ins.then_inc(sem.h, k)
                    else:
                        e.nop().then_inc(sem.h, k)
            self.emit(eng, run)
        return tgt

    def dma(self, eng, out, in_, sem, **kw):
        tgt = sem.bump(16)
        self.emit(eng, lambda e, out=out, in_=in_, kw=kw, s=sem.h:
                  e.dma_start(out=out, in_=in_, **kw).then_inc(s, 16))
        return tgt

    def barrier(self, sems_vals, engines=ENGS):
        for e in engines:
            for sem, v in sems_vals:
                self.wait(e, sem, v)


def ts(i, n=128):
    return slice(i * n, (i + 1) * n)


class Guard:

    def __init__(self, C):
        self.C = C
        self.m = {}

    def use(self, key, sem):
        self.m.setdefault(key, {})[sem] = sem.n

    def acquire(self, eng, key):
        for sem, v in self.m.pop(key, {}).items():
            self.C.wait(eng, sem, v)


def _ag(C, in_ap, out_ap, groups, sem):
    tgt = sem.bump(1)
    C.emit("gpsimd", lambda e, i=in_ap, o=out_ap, g=groups, s=sem.h:
           e.collective_compute("AllGather", ALU.bypass, replica_groups=g,
                                ins=[i.opt()], outs=[o.opt()]).then_inc(s, 1))
    return tgt


G8 = [[0, 1, 2, 3, 4, 5, 6, 7]]
G4 = [[0, 1, 2, 3], [4, 5, 6, 7]]
G2 = [[0, 1], [2, 3], [4, 5], [6, 7]]


def build_nc(phases=None):
    if phases is None:
        phases = _PHASES
    nc = bass.Bass("TRN2", target_bir_lowering=False)
    st = ExitStack()
    C = Ctx(nc, st)

    ein = {}

    def EIN(name, shape, dt):
        ein[name] = nc.dram_tensor(name, shape, dt, kind="ExternalInput")
        return ein[name]

    xin = EIN("xin", [TOK, D], F32)
    for nm, shape in [("aWq", [256, D]), ("aWk", [128, D]), ("aWv", [128, D]),
                      ("aWo", [256, D]), ("bWq", [256, D]), ("bWk", [256, D]),
                      ("bWv", [256, D]), ("bWo", [256, D]),
                      ("fw1", [768, D]), ("fw2", [768, D]), ("fw3", [D, 768]),
                      ("bbq", [16, 128]), ("bbk", [16, 128]), ("bbv", [16, 128]),
                      ("bbo", [16, 128]),
                      ("ropecos", [128, TOK]), ("ropesin", [128, TOK]),
                      ("identf", [128, 128]), ("masks", [128, 4 * CH]),
                      ("kbias", [NTC, 36, 128, NH]), ("self4", [4, 384])]:
        EIN(nm, shape, F32)
    for nm, shape in [("rotT", [128, 128]), ("identb", [128, 128]),
                      ("onesb", [128, 128]), ("crow", [16, CH]),
                      ("ehmat", [128, NH * 128]), ("mdyn", [NTC, 8, 128, CH])]:
        EIN(nm, shape, BF16)
    EIN("kvoff", [NTC, 36, 2], I32)

    xout = nc.dram_tensor("xout", [TOK, D], F32, kind="ExternalOutput")

    DR = lambda name, shape, dt: nc.dram_tensor(name, shape, dt)
    scr = {}
    for nm, shape, dt in [
        ("aWqT", [D, D], BF16), ("aWkT", [512, D], BF16),
        ("aWvT", [512, D], BF16), ("aWoT", [D, D], BF16),
        ("bWqb", [D, D], BF16), ("bWkb", [D, D], BF16), ("bWvb", [D, D], BF16),
        ("bWob", [D, D], BF16),
        ("qw1", [6144, D], BF16), ("qw2", [6144, D], BF16), ("qw3", [6144, D], BF16),
        ("awq_in", [256, D], BF16), ("awk_in", [128, D], BF16),
        ("awv_in", [128, D], BF16), ("awo_in", [256, D], BF16),
        ("bwq_in", [256, D], BF16), ("bwk_in", [256, D], BF16),
        ("bwv_in", [256, D], BF16), ("bwo_in", [256, D], BF16),
        ("qw1_in", [768, D], BF16), ("qw2_in", [768, D], BF16),
        ("qw3_in", [768, D], BF16),
        ("gpart_in", [1, 4], F32), ("gpart_out", [8, 4], F32),
        ("h_scr", [NFC, 128, TOK], BF16),
        ("qa_scr", [D, TOK], BF16), ("qb_scr", [D, TOK], BF16),
        ("xspill", [NFC, 128, TOK], F32),
        ("kina", [512, TOK], BF16), ("vina", [TOK, 512], BF16),
        ("kga", [2 * 512, TOK], BF16), ("vga", [2 * TOK, 512], BF16),
        ("kinb", [D, TOK], BF16), ("vinb", [TOK, D], BF16),
        ("kgb", [2 * D, TOK], BF16), ("vgb", [2 * TOK, D], BF16),
    ]:
        scr[nm] = DR(nm, shape, dt)

    cb = {}
    for nm, shape, dt in [("rotT", [128, 128], BF16), ("identf", [128, 128], F32),
                          ("identb", [128, 128], BF16), ("onesb", [128, 128], BF16)]:
        cb[nm] = C.sb("c_" + nm, shape, dt)
    bias_sb = C.sb("bias_sb", [128, 4 * 16], F32)
    crow_sb = C.sb("crow_sb", [128, CH], BF16)

    arX = C.sb("arX", [128, NFC * TOK], F32)
    arXb = arX[:, :].bitcast(BF16)
    arA = C.sb("arA", [128, 8192], BF16)
    arAf = arA[:, :].bitcast(F32)
    arB = C.sb("arB", [128, 4096], BF16)
    arBf = arB[:, :].bitcast(F32)
    arC = C.sb("arC", [128, 24576], BF16)
    arCf = arC[:, :].bitcast(F32)
    psA = st.enter_context(nc.psum_tensor("psA", [128, 4096], F32))
    psAb = psA[:, :].bitcast(BF16)

    def bank(b, n=1):
        return psA[:, b * 512:(b + n) * 512]

    def bankb(b, n=1):
        return psAb[:, b * 1024:(b + n) * 1024]

    constv = C.sb("constv", [128, 4], F32)
    sCN = C.sem("cn")
    for i, val in enumerate((0.0, LN_EPS, ACONST)):
        C.op("vector", lambda e, i=i, val=val: e.memset(constv[:, i:i + 1], val),
             sem=(sCN if i == 2 else None), drain=(i == 2))
        nc.const_aps.register(F32, val, constv[:, i:i + 1])
    C.wait("scalar", sCN, 1)
    C.wait("tensor", sCN, 1)

    sLD = C.sem("ld")
    for nm in ("rotT", "identf", "identb", "onesb"):
        C.dma("gpsimd", cb[nm][:, :], ein[nm][:, :], sLD)
    sMS = C.sem("ms")
    vms = C.op("vector", lambda e: e.memset(crow_sb[:, :], 0.0), sem=sMS, drain=True)
    C.wait("gpsimd", sMS, vms)
    C.dma("gpsimd", crow_sb[0:16, :], ein["crow"][:, :], sLD)
    vLD = sLD.n

    S = dict(nc=nc, C=C, ein=ein, scr=scr, cb=cb, bias_sb=bias_sb,
             crow_sb=crow_sb, x_fm=arX, arX=arX, arXb=arXb, arA=arA, arAf=arAf,
             arB=arB, arBf=arBf, arC=arC, arCf=arCf, bank=bank, bankb=bankb,
             sLD=sLD, vLD=vLD, xin=xin, xout=xout)

    fence = phase_P0(S)
    fa, fb = phase_PW(S) if "PW" in phases else ([], [])
    fence_g = phase_PG(S) if "PG" in phases else []
    if "P1" in phases:
        fence = phase_P1(S, fence + fa)
    if "P2" in phases:
        fence = phase_P2(S, fence + fence_g, "f2")
    if "P3" in phases:
        fence = phase_P3(S, fence + fb)
    if "P4" in phases:
        fence = phase_P2(S, fence, "f4")
    phase_P9(S, fence)

    block = st.enter_context(nc.Block())

    def _mk(eng_name):
        def _run(e):
            for fn in C.q[eng_name]:
                fn(e)
        return _run

    block.tensor(_mk("tensor"))
    block.vector(_mk("vector"))
    block.scalar(_mk("scalar"))
    block.gpsimd(_mk("gpsimd"))
    block.sync(_mk("sync"))

    st.close()
    return nc


def phase_P0(S):
    C, bank = S["C"], S["bank"]
    x_fm, xin = S["x_fm"], S["xin"]
    identf = S["cb"]["identf"]
    sd = [C.sem(f"p0d{i}") for i in range(2)]
    sm = C.sem("p0m")
    sv = C.sem("p0v")
    xt = [S["arAf"][:, 0:2048], S["arAf"][:, 2048:4096]]
    C.wait("tensor", S["sLD"], S["vLD"])
    gi = 0
    for tt in range(16):
        bi = tt % 2
        buf = xt[bi]
        if tt >= 2:
            C.wait("sync", sm, 4 * (tt - 1))
        vland = C.dma("sync", buf, xin[ts(tt), :], sd[bi])
        C.wait("tensor", sd[bi], vland)
        for g in range(4):
            bk = bank(gi % 2)
            if gi >= 2:
                C.wait("tensor", sv, gi - 1)
            for f4 in range(4):
                fc = g * 4 + f4
                C.op("tensor", lambda e, bk=bk, buf=buf, fc=fc, f4=f4, identf=identf:
                     e.transpose(bk[:, ts(f4)], buf[:, ts(fc)], identf[:, :]),
                     sem=(sm if f4 == 3 else None))
            C.wait("vector", sm, sm.n)
            out_ap = x_fm[:, g * 4 * TOK:(g * 4 + 4) * TOK] \
                .rearrange("p (f l) -> p f l", f=4)[:, :, tt * 128:(tt + 1) * 128]
            in_ap = bk[:, :].rearrange("p (f l) -> p f l", f=4)
            C.op("vector", lambda e, o=out_ap, i=in_ap: e.tensor_copy(o, i),
                 sem=sv, drain=True)
            gi += 1
    return [(sv, sv.n)]


def phase_PW(S):
    C, ein, scr, cb, bankb = S["C"], S["ein"], S["scr"], S["cb"], S["bankb"]
    identb, identf = cb["identb"], cb["identf"]
    sdw = C.sem("pwd")
    smw = C.sem("pwm")
    svw = C.sem("pwv")
    sso = C.sem("pwo")
    sgw = C.sem("pwg")
    wb = S["arB"][:, 0:2048]
    tb = S["arB"][:, 2048:4096]
    bk = bankb(2)[:, 0:512]
    C.wait("tensor", S["sLD"], S["vLD"])

    def prep_tile(src_ap, dst_ap):
        vw = sso.n
        C.wait("gpsimd", smw, smw.n)
        vd = C.dma("gpsimd", wb, src_ap, sdw)
        C.wait("tensor", sdw, vd)
        for g in range(4):
            C.wait("tensor", svw, svw.n)
            for f4 in range(4):
                fc = g * 4 + f4
                C.op("tensor", lambda e, fc=fc, f4=f4:
                     e.transpose(bk[:, ts(f4)], wb[:, ts(fc)], identb[:, :]),
                     sem=(smw if f4 == 3 else None))
            C.wait("vector", smw, smw.n)
            if g == 0:
                C.wait("vector", sso, vw)
            C.op("vector", lambda e, g=g:
                 e.tensor_copy(tb[:, g * 512:(g + 1) * 512], bk), sem=svw, drain=True)
        C.wait("sync", svw, svw.n)
        return C.dma("sync", dst_ap, tb, sso)

    for nm, inn, agout, ntile, groups in [
        ("aWk", "awk_in", "aWkT", 1, G4), ("aWv", "awv_in", "aWvT", 1, G4),
        ("aWq", "awq_in", "aWqT", 2, G8), ("aWo", "awo_in", "aWoT", 2, G8),
    ]:
        for ot in range(ntile):
            vst = prep_tile(ein[nm][ts(ot), :], scr[inn][ts(ot), :])
        C.wait("gpsimd", sso, vst)
        _ag(C, scr[inn].ap(), scr[agout].ap(), groups, sgw)
    for nm, inn, agout in [("bWq", "bwq_in", "bWqb"), ("bWk", "bwk_in", "bWkb"),
                           ("bWv", "bwv_in", "bWvb"), ("bWo", "bwo_in", "bWob")]:
        vc = C.dma("gpsimd", scr[inn][:, :], ein[nm][:, :], sdw)
        C.wait("gpsimd", sdw, vc)
        _ag(C, scr[inn].ap(), scr[agout].ap(), G8, sgw)
    sbb = C.sem("pwb")
    ssb = C.sem("pwbs")
    bbs = S["arBf"][0:16, 0:128]
    pb = S["bank"](2)[:, 0:16]
    for wi, nm in enumerate(("bbq", "bbk", "bbv", "bbo")):
        C.wait("sync", smw, smw.n)
        vb = C.dma("sync", bbs, ein[nm][:, :], sbb)
        C.wait("tensor", sbb, vb)
        C.wait("tensor", ssb, ssb.n)
        C.op("tensor", lambda e: e.transpose(pb, bbs, identf[0:16, 0:16]), sem=smw)
        C.wait("scalar", smw, smw.n)
        C.op("scalar", lambda e, wi=wi:
             e.copy(S["bias_sb"][:, wi * 16:(wi + 1) * 16], pb), sem=ssb, drain=True)
    fa = [(sgw, sgw.n)]
    fb = [(sgw, sgw.n), (ssb, ssb.n)]
    S["pw_done"] = [(sgw, sgw.n), (ssb, ssb.n), (svw, svw.n), (smw, smw.n)]
    return fa, fb


def phase_PG(S):
    C, ein, scr, cb = S["C"], S["ein"], S["scr"], S["cb"]
    identb, identf = cb["identb"], cb["identf"]
    bank, bankb = S["bank"], S["bankb"]
    sdq = C.sem("pgd")
    ssc = C.sem("pgs")
    svq = C.sem("pgv")
    sgp = C.sem("pgp")
    smq = C.sem("pgm")
    sgg = C.sem("pgg")
    sst = C.sem("pgt")
    wf = S["arCf"][:, 0:2048]
    mp = S["arC"][:, 4096:6144]
    mn = S["arC"][:, 6144:8192]
    qt = S["arC"][:, 8192:10240]
    tb = S["arC"][:, 10240:12288]
    acc = C.sb("pg_acc", [128, 32], F32)
    red = C.sb("pg_red", [128, 4], F32)
    gp_sb = C.sb("pg_gp", [8, 4], F32)
    tot = C.sb("pg_tot", [4, 1], F32)
    thr4 = C.sb("pg_thr4", [4, 1], F32)
    nthr4 = C.sb("pg_nthr4", [4, 1], F32)
    thr_sb = C.sb("pg_thr", [128, 8], F32)
    self4 = C.sb("pg_self4", [4, 384], F32)
    bk = bankb(6)[:, 0:512]
    pr = bank(7)[:, 0:128]
    pth = bank(7)[:, 128:129]
    C.dma("gpsimd", self4[:, :], ein["self4"][:, :], S["sLD"])
    S["vLD"] = S["sLD"].n

    tiles = [("fw1", ein["fw1"][ts(t), :], D) for t in range(6)] + \
            [("fw2", ein["fw2"][ts(t), :], D) for t in range(6)] + \
            [("fw3", ein["fw3"][ts(t), :], 768) for t in range(16)]
    slot = 0
    for nm, ap, fd in tiles:
        vd = C.dma("sync", wf[:, 0:fd], ap, sdq)
        C.wait("scalar", sdq, vd)
        C.op("scalar", lambda e, fd=fd, slot=slot:
             e.activation(mp[:, 0:fd], wf[:, 0:fd], AF.Abs,
                          accum_out=acc[:, slot:slot + 1]), sem=ssc, drain=True)
        C.wait("sync", ssc, ssc.n)
        slot += 1
    C.wait("vector", ssc, ssc.n)
    C.op("vector", lambda e: e.tensor_reduce(red[:, 0:1], acc[:, 0:6], AX.X, ALU.add))
    C.op("vector", lambda e: e.tensor_reduce(red[:, 1:2], acc[:, 6:12], AX.X, ALU.add))
    C.op("vector", lambda e: e.tensor_reduce(red[:, 2:3], acc[:, 12:28], AX.X, ALU.add))
    C.op("vector", lambda e: e.memset(red[:, 3:4], 0.0), sem=svq, drain=True)
    C.wait("tensor", svq, svq.n)
    C.wait("tensor", S["sLD"], S["vLD"])
    C.op("tensor", lambda e: e.transpose(pr[0:4, :], red[:, :], identf[:, :]), sem=smq)
    C.wait("vector", smq, smq.n)
    C.op("vector", lambda e: e.tensor_reduce(red[0:4, 0:1], pr[0:4, :], AX.X, ALU.add),
         sem=svq, drain=True)
    C.wait("sync", svq, svq.n)
    vst = C.dma("sync", scr["gpart_in"][0:1, 0:4], red[0:4, 0:1], sst)
    C.wait("gpsimd", sst, vst)
    vgg = _ag(C, scr["gpart_in"].ap(), scr["gpart_out"].ap(), G8, sgg)
    C.wait("gpsimd", sgg, vgg)
    sgl = C.sem("pgl")
    vgl = C.dma("gpsimd", gp_sb[:, :], scr["gpart_out"][:, :], sgl)
    C.wait("tensor", sgl, vgl)
    C.wait("tensor", svq, svq.n)
    C.op("tensor", lambda e: e.transpose(pr[0:4, 0:8], gp_sb[:, :], identf[0:8, 0:8]),
         sem=smq)
    C.wait("vector", smq, smq.n)
    C.op("vector", lambda e: e.tensor_reduce(tot[:, 0:1], pr[0:4, 0:8], AX.X, ALU.add))
    C.op("vector", lambda e: e.tensor_scalar(thr4[:, :], tot[:, :],
                                             1.0 / (2.0 * HID * D), 0.5 * Q_EPS,
                                             ALU.mult, ALU.add))
    C.op("vector", lambda e: e.tensor_scalar(nthr4[:, :], thr4[:, :], -1.0, None,
                                             ALU.mult), sem=svq, drain=True)
    C.wait("tensor", svq, svq.n)
    for wi in range(3):
        for neg in range(2):
            src = nthr4 if neg else thr4
            C.op("tensor", lambda e, wi=wi, src=src:
                 e.matmul(pth, self4[:, wi * 128:(wi + 1) * 128], src[:, :],
                          start=True, stop=True), sem=smq)
            C.wait("scalar", smq, smq.n)
            col = 3 * neg + wi
            C.op("scalar", lambda e, col=col:
                 e.copy(thr_sb[:, col:col + 1], pth), sem=ssc, drain=True)
            C.wait("tensor", ssc, ssc.n)
    C.wait("vector", ssc, ssc.n)

    def quant_tile(src_ap, fd, wi):
        C.wait("sync", svq, svq.n)
        vd = C.dma("sync", wf[:, 0:fd], src_ap, sdq)
        C.wait("vector", sdq, vd)
        C.wait("vector", sgp, sgp.n)
        C.op("vector", lambda e, fd=fd, wi=wi:
             e.tensor_scalar(mp[:, 0:fd], wf[:, 0:fd], thr_sb[:, wi:wi + 1], None,
                             ALU.is_ge))
        C.op("vector", lambda e, fd=fd, wi=wi:
             e.tensor_scalar(mn[:, 0:fd], wf[:, 0:fd], thr_sb[:, 3 + wi:4 + wi], None,
                             ALU.is_le), sem=svq, drain=True)
        C.wait("gpsimd", svq, svq.n)
        C.wait("gpsimd", smq, smq.n)
        C.op("gpsimd", lambda e, fd=fd:
             e.tensor_tensor(qt[:, 0:fd], mp[:, 0:fd], mn[:, 0:fd], ALU.subtract),
             sem=sgp, drain=True)
        return sgp.n

    for wi, nm, inn in [(0, "fw1", "qw1_in"), (1, "fw2", "qw2_in")]:
        for t in range(6):
            vq = quant_tile(ein[nm][ts(t), :], D, wi)
            C.wait("tensor", sgp, vq)
            for g in range(4):
                C.wait("tensor", svq, svq.n)
                C.wait("tensor", sst, sst.n)
                for f4 in range(4):
                    fc = g * 4 + f4
                    C.op("tensor", lambda e, fc=fc, f4=f4:
                         e.transpose(bk[:, ts(f4)], qt[:, ts(fc)], identb[:, :]),
                         sem=(smq if f4 == 3 else None))
                C.wait("vector", smq, smq.n)
                C.op("vector", lambda e, g=g:
                     e.tensor_copy(tb[:, g * 512:(g + 1) * 512], bk),
                     sem=svq, drain=True)
            C.wait("sync", svq, svq.n)
            C.dma("sync", scr[inn][ts(t), :], tb, sst)
        C.wait("gpsimd", sst, sst.n)
        _ag(C, scr[inn].ap(), scr["qw" + str(wi + 1)].ap(), G8, sgg)
    for oc in range(16):
        vq = quant_tile(ein["fw3"][ts(oc), :], 768, 2)
        C.wait("tensor", sgp, vq)
        for g in range(2):
            nm_ = 4 if g == 0 else 2
            C.wait("tensor", svq, svq.n)
            C.wait("tensor", sst, sst.n)
            for m in range(nm_):
                hcl = g * 4 + m
                C.op("tensor", lambda e, hcl=hcl, m=m:
                     e.transpose(bk[:, ts(m)], qt[:, ts(hcl)], identb[:, :]),
                     sem=(smq if m == nm_ - 1 else None))
            C.wait("vector", smq, smq.n)
            C.op("vector", lambda e, g=g, nm_=nm_:
                 e.tensor_copy(tb[:, g * 512:g * 512 + nm_ * 128],
                               bk[:, 0:nm_ * 128]), sem=svq, drain=True)
        C.wait("sync", svq, svq.n)
        for hcl in range(6):
            g, m = divmod(hcl, 4)
            C.dma("sync", scr["qw3_in"][hcl * 128:(hcl + 1) * 128, ts(oc)],
                  tb[:, g * 512 + m * 128:g * 512 + (m + 1) * 128], sst)
    C.wait("gpsimd", sst, sst.n)
    vlast = _ag(C, scr["qw3_in"].ap(), scr["qw3"].ap(), G8, sgg)
    return [(sgg, vlast)]


def emit_ln(S, sems, tc, hname):
    C, bank, x_fm = S["C"], S["bank"], S["x_fm"]
    onesb = S["cb"]["onesb"]
    sm, sv, ss, sg = sems["m"], sems["v"], sems["s"], sems["g"]
    so2 = sems["o2"]
    xb2 = S["arB"][:, 0:1024]
    sq2 = S["arB"][:, 1024:2048]
    hb2 = S["arB"][:, 2048:3072]
    m_sb = S["arBf"][:, 1536:2048]
    rstd = S["arCf"][:, 3072:3584]
    ms_sb = S["arCf"][:, 3584:4096]
    sx, sq = bank(0), bank(1)
    C.wait("tensor", sv, sv.n)
    for fc in range(NFC):
        xs = x_fm[:, fc * TOK + tc * CH: fc * TOK + (tc + 1) * CH]
        sl = fc % 2
        C.wait("scalar", sm, sm.n)
        C.op("scalar", lambda e, xs=xs, sl=sl:
             e.copy(xb2[:, sl * 512:(sl + 1) * 512], xs), sem=ss, drain=True)
        C.wait("gpsimd", sm, sm.n)
        C.op("gpsimd", lambda e, xs=xs, sl=sl:
             e.tensor_tensor(sq2[:, sl * 512:(sl + 1) * 512], xs, xs, ALU.mult),
             sem=sg, drain=True)
        C.wait("tensor", ss, ss.n)
        C.op("tensor", lambda e, sl=sl, fc=fc:
             e.matmul(sx, onesb[:, :], xb2[:, sl * 512:(sl + 1) * 512],
                      start=(fc == 0), stop=(fc == NFC - 1)), sem=sm)
        C.wait("tensor", sg, sg.n)
        C.op("tensor", lambda e, sl=sl, fc=fc:
             e.matmul(sq, onesb[:, :], sq2[:, sl * 512:(sl + 1) * 512],
                      start=(fc == 0), stop=(fc == NFC - 1)), sem=sm)
    C.wait("vector", sm, sm.n)
    C.op("vector", lambda e: e.tensor_scalar(m_sb, sx, 1.0 / D, None, ALU.mult))
    C.op("vector", lambda e: e.tensor_tensor(ms_sb, m_sb, m_sb, ALU.mult))
    C.op("vector", lambda e: e.scalar_tensor_tensor(rstd, sq, 1.0 / D, ms_sb,
                                                    ALU.mult, ALU.subtract),
         sem=sv, drain=True)
    C.wait("scalar", sv, sv.n)
    C.op("scalar", lambda e: e.activation(rstd, rstd, AF.Ln, bias=LN_EPS, scale=1.0))
    C.op("scalar", lambda e: e.activation(rstd, rstd, AF.Exp, scale=-0.5),
         sem=ss, drain=True)
    C.wait("vector", ss, ss.n)
    for fc in range(NFC):
        xs = x_fm[:, fc * TOK + tc * CH: fc * TOK + (tc + 1) * CH]
        sl = fc % 2
        C.wait("vector", so2[sl], so2[sl].n)
        C.op("vector", lambda e, xs=xs, sl=sl:
             e.tensor_tensor(hb2[:, sl * 512:(sl + 1) * 512], xs, m_sb, ALU.subtract))
        C.op("vector", lambda e, sl=sl:
             e.tensor_tensor(hb2[:, sl * 512:(sl + 1) * 512],
                             hb2[:, sl * 512:(sl + 1) * 512], rstd, ALU.mult),
             sem=sv, drain=True)
        C.wait("sync", sv, sv.n)
        C.dma("sync", S["scr"][hname][fc, :, tc * CH:(tc + 1) * CH],
              hb2[:, sl * 512:(sl + 1) * 512], so2[sl])


def phase_P1(S, fence):
    C, bank, bankb = S["C"], S["bank"], S["bankb"]
    scr, ein, cb = S["scr"], S["ein"], S["cb"]
    x_fm, arXb = S["x_fm"], S["arXb"]
    onesb, rotT, identb = cb["onesb"], cb["rotT"], cb["identb"]
    sm = C.sem("p1m")
    sv = C.sem("p1v")
    ss = C.sem("p1s")
    sg = C.sem("p1g")
    so2 = [C.sem("p1o0"), C.sem("p1o1")]
    sp = C.sem("p1p")
    sld = [C.sem("p1l0"), C.sem("p1l1")]
    sq_ = C.sem("p1q")
    scc = C.sem("p1cc")
    sko2 = [C.sem("p1ko0"), C.sem("p1ko1")]
    skv = [C.sem("p1kv0"), C.sem("p1kv1")]
    sxs = C.sem("p1xs")
    shl = C.sem("p1hl")
    gd = Guard(C)

    C.barrier(fence, engines=("tensor", "vector", "scalar", "gpsimd", "sync"))
    lnsems = dict(m=sm, v=sv, s=ss, g=sg, o2=so2)
    for tc in range(NTC):
        emit_ln(S, lnsems, tc, "h_scr")
    C.wait("scalar", sv, sv.n)
    C.wait("scalar", sg, sg.n)
    for fc in range(NFC):
        C.dma("scalar", scr["xspill"][fc, :, :], x_fm[:, fc * TOK:(fc + 1) * TOK], sp)
    h_all = arXb[:, 0:NFC * TOK]
    cosb = S["arX"][:, 16384:18432]
    sinb = S["arX"][:, 18432:20480]
    t1 = S["arX"][:, 20480:20992]
    t2 = S["arX"][:, 20992:21504]
    outb = arXb[:, 45056:47104]
    wbuf = [arXb[:, 47104 + i * 2048: 47104 + (i + 1) * 2048] for i in range(2)]
    vtb = arXb[:, 51200:53248]
    C.wait("sync", sp, sp.n)
    C.wait("sync", so2[0], so2[0].n)
    C.wait("sync", so2[1], so2[1].n)
    for fc in range(NFC):
        C.dma("sync", h_all[:, fc * TOK:(fc + 1) * TOK], scr["h_scr"][fc, :, :], shl)
    C.dma("sync", cosb, ein["ropecos"][:, :], shl)
    C.dma("sync", sinb, ein["ropesin"][:, :], shl)
    C.barrier([(shl, shl.n)], engines=("tensor", "vector", "scalar"))

    ldtgt = {}

    def load_w(src_rows, li):
        sl = li % 2
        gd.acquire("sync", ("wbuf", sl))
        ldtgt[sl] = C.dma("sync", wbuf[sl], src_rows, sld[sl])

    def proj_psum(li, tc, pb_i):
        w = wbuf[li % 2]
        C.wait("tensor", sld[li % 2], ldtgt[li % 2])
        gd.acquire("tensor", ("bank", pb_i))
        for fc in range(NFC):
            C.op("tensor", lambda e, w=w, fc=fc, tc=tc, pb_i=pb_i:
                 e.matmul(bank(pb_i), w[:, ts(fc)],
                          h_all[:, fc * TOK + tc * CH: fc * TOK + (tc + 1) * CH],
                          start=(fc == 0), stop=(fc == NFC - 1)),
                 sem=(sm if fc == NFC - 1 else None))
        gd.use(("wbuf", li % 2), sm)

    def rope_unit(pb_i, rb_i, tc, sl):
        dst_bf = outb[:, sl * 512:(sl + 1) * 512]
        C.wait("scalar", sm, sm.n)
        gd.acquire("scalar", ("outb", sl))
        C.op("scalar", lambda e, pb_i=pb_i, dst_bf=dst_bf:
             e.copy(dst_bf, bank(pb_i)), sem=ss)
        C.wait("tensor", ss, ss.n)
        gd.acquire("tensor", ("bank", rb_i))
        C.op("tensor", lambda e, rb_i=rb_i, dst_bf=dst_bf:
             e.matmul(bank(rb_i), rotT[:, :], dst_bf, start=True, stop=True), sem=sm)
        C.wait("vector", sm, sm.n)
        cs = cosb[:, tc * CH:(tc + 1) * CH]
        sn = sinb[:, tc * CH:(tc + 1) * CH]
        C.op("vector", lambda e, pb_i=pb_i, cs=cs:
             e.tensor_tensor(t1, bank(pb_i), cs, ALU.mult))
        C.op("vector", lambda e, rb_i=rb_i, sn=sn:
             e.tensor_tensor(t2, bank(rb_i), sn, ALU.mult))
        C.op("vector", lambda e, dst_bf=dst_bf:
             e.tensor_tensor(dst_bf, t1, t2, ALU.add), sem=sv)
        gd.use(("bank", pb_i), ss)
        gd.use(("bank", pb_i), sv)
        gd.use(("bank", rb_i), sv)
        return dst_bf

    def store_out(dst_ap, dst_bf, sl):
        C.wait("scalar", sv, sv.n)
        C.dma("scalar", dst_ap, dst_bf, sko2[sl])
        gd.use(("outb", sl), sko2[sl])

    li = 0
    load_w(scr["aWkT"][ts(0), :], 0)
    for dc in range(4):
        if dc + 1 < 4:
            load_w(scr["aWkT"][ts(dc + 1), :], li + 1)
        for tc in range(NTC):
            proj_psum(li, tc, 2 + (tc % 2))
            dst = rope_unit(2 + (tc % 2), 4 + (tc % 2), tc, tc % 2)
            store_out(scr["kina"][ts(dc), tc * CH:(tc + 1) * CH], dst, tc % 2)
        li += 1
    for tc in range(NTC):
        load_w(scr["aWvT"][ts(0), :], li)
        for dc in range(4):
            if dc + 1 < 4:
                load_w(scr["aWvT"][ts(dc + 1), :], li + 1)
            pb_i = 2 + (dc % 2)
            proj_psum(li, tc, pb_i)
            C.wait("scalar", sm, sm.n)
            gd.acquire("scalar", ("vtb", dc))
            C.op("scalar", lambda e, pb_i=pb_i, dc=dc:
                 e.copy(vtb[:, dc * 512:(dc + 1) * 512], bank(pb_i)), sem=ss)
            gd.use(("bank", pb_i), ss)
            li += 1
        C.wait("tensor", ss, ss.n)
        for tt in range(4):
            tb_i = 4 + (tt % 2)
            gd.acquire("tensor", ("bank", tb_i))
            for dc in range(4):
                C.op("tensor", lambda e, tb_i=tb_i, dc=dc, tt=tt:
                     e.transpose(bankb(tb_i)[:, 0:512][:, ts(dc)],
                                 vtb[:, dc * 512 + tt * 128: dc * 512 + (tt + 1) * 128],
                                 identb[:, :]),
                     sem=(sm if dc == 3 else None))
                gd.use(("vtb", dc), sm)
            C.wait("vector", sm, sm.n)
            sl = tt % 2
            vout = outb[:, sl * 512:(sl + 1) * 512]
            gd.acquire("vector", ("outb", sl))
            C.op("vector", lambda e, tb_i=tb_i, vout=vout:
                 e.tensor_copy(vout, bankb(tb_i)[:, 0:512]), sem=sv)
            gd.use(("bank", tb_i), sv)
            store_out(scr["vina"][tc * CH + tt * 128: tc * CH + (tt + 1) * 128, :],
                      vout, sl)
    load_w(scr["aWqT"][ts(0), :], li)
    for oc in range(NFC):
        if oc + 1 < NFC:
            load_w(scr["aWqT"][ts(oc + 1), :], li + 1)
        for tc in range(NTC):
            proj_psum(li, tc, 2 + (tc % 2))
            dst = rope_unit(2 + (tc % 2), 4 + (tc % 2), tc, tc % 2)
            store_out(scr["qa_scr"][ts(oc), tc * CH:(tc + 1) * CH], dst, tc % 2)
        li += 1
    C.wait("gpsimd", sko2[0], sko2[0].n)
    C.wait("gpsimd", sko2[1], sko2[1].n)
    _ag(C, scr["kina"].ap(), scr["kga"].ap(), G2, scc)
    _ag(C, scr["vina"].ap(), scr["vga"].ap(), G2, scc)
    C.barrier([(scc, 2), (sm, sm.n), (sv, sv.n), (ss, ss.n)],
              engines=("tensor", "scalar", "vector", "sync"))

    kkb = [arXb[:, i * 4096: (i + 1) * 4096] for i in range(2)]
    vvb = [arXb[:, 8192 + i * 4096: 8192 + (i + 1) * 4096] for i in range(2)]
    qsb = arXb[:, 16384:24576]
    o_all = arXb[:, 24576:32768]
    p_sb = [arXb[:, 32768 + i * 1024: 32768 + (i + 1) * 1024] for i in range(4)]
    rec_sb = S["arX"][:, 18432:18944]
    lnt = S["arX"][:, 18944:19456]
    xcur = S["arX"][:, 19456:27648]
    wov = [arXb[:, 55296 + i * 2048: 55296 + (i + 1) * 2048] for i in range(2)]

    def load_kv(kvh, slot):
        sem = skv[slot]
        gd.acquire("sync", ("kv", slot))
        for r in range(2):
            C.dma("sync", kkb[slot][:, r * 2048:(r + 1) * 2048],
                  scr["kga"][r * 512 + kvh * 128: r * 512 + (kvh + 1) * 128, :], sem)
        for r in range(2):
            for hh in range(2):
                C.dma("sync",
                      vvb[slot][:, (r * 16 + hh * 8) * 128:
                                (r * 16 + (hh + 1) * 8) * 128]
                      .rearrange("p (a d) -> p a d", a=8),
                      scr["vga"][r * TOK + hh * 1024: r * TOK + (hh + 1) * 1024,
                                 kvh * 128:(kvh + 1) * 128]
                      .rearrange("(a p) d -> p a d", p=128), sem)
        return sem.n

    for qc in range(NTC):
        gd.acquire("sync", ("qsb",))
        vq = C.dma("sync", qsb.rearrange("p (oc t) -> p oc t", oc=16),
                   scr["qa_scr"].ap().rearrange("(oc p) t -> p oc t", p=128)
                   [:, :, qc * CH:(qc + 1) * CH], sq_)
        C.barrier([(sq_, vq)], engines=("tensor",))
        kvt = [None, None]
        kvt[0] = load_kv(0, 0)
        for kvh in range(4):
            slot = kvh % 2
            C.wait("tensor", skv[slot], kvt[slot])
            if kvh < 3:
                kvt[(kvh + 1) % 2] = load_kv(kvh + 1, (kvh + 1) % 2)
            for h in range(kvh * 4, kvh * 4 + 4):
                qh = qsb[:, h * 512:(h + 1) * 512]

                def emit_qk(kp, qh=qh, slot=slot):
                    sti = 2 * (kp % 2)
                    st2 = bank(sti, 2)
                    gd.acquire("tensor", ("st", sti))
                    for j in range(2):
                        kt = 2 * kp + j
                        C.op("tensor", lambda e, st2=st2, j=j, kt=kt, slot=slot,
                             qh=qh:
                             e.matmul(st2[:, j * 512:(j + 1) * 512],
                                      kkb[slot][:, kt * 128:(kt + 1) * 128],
                                      qh, start=True, stop=True),
                             sem=(sm if j == 1 else None))
                    C.wait("scalar", sm, sm.n)
                    gd.acquire("scalar", ("p", kp % 4))
                    C.op("scalar", lambda e, st2=st2, kp=kp:
                         e.activation(p_sb[kp % 4], st2, AF.Exp, bias=ACONST,
                                      scale=ISQD), sem=ss)
                    gd.use(("st", sti), ss)

                def emit_pv(kp, slot=slot):
                    pb2 = p_sb[kp % 4]
                    C.wait("tensor", ss, ss.n)
                    for j in range(2):
                        kt = 2 * kp + j
                        C.op("tensor", lambda e, pb2=pb2, j=j, kt=kt, slot=slot:
                             e.matmul(bank(4), vvb[slot][:, kt * 128:(kt + 1) * 128],
                                      pb2[:, j * 512:(j + 1) * 512],
                                      start=(kt == 0), stop=(kt == 31)))
                        C.op("tensor", lambda e, pb2=pb2, j=j, kt=kt:
                             e.matmul(bank(5), onesb[:, :],
                                      pb2[:, j * 512:(j + 1) * 512],
                                      start=(kt == 0), stop=(kt == 31)),
                             sem=(sm if j == 1 else None))
                    gd.use(("p", kp % 4), sm)

                gd.acquire("tensor", ("bank", 4))
                gd.acquire("tensor", ("bank", 5))
                for kp in range(16):
                    emit_qk(kp)
                    if kp > 0:
                        emit_pv(kp - 1)
                emit_pv(15)
                C.wait("scalar", sm, sm.n)
                gd.acquire("scalar", ("rec",))
                C.op("scalar", lambda e: e.activation(lnt, bank(5), AF.Ln))
                C.op("scalar", lambda e: e.activation(rec_sb, lnt, AF.Exp,
                                                      scale=-1.0), sem=ss)
                gd.use(("bank", 5), ss)
                C.wait("vector", ss, ss.n)
                oh = o_all[:, h * 512:(h + 1) * 512]
                gd.acquire("vector", ("oall", h))
                C.op("vector", lambda e, oh=oh:
                     e.tensor_tensor(oh, bank(4), rec_sb, ALU.mult), sem=sv)
                gd.use(("bank", 4), sv)
                gd.use(("rec",), sv)
            gd.use(("kv", slot), sm)
        gd.use(("qsb",), sm)
        C.wait("tensor", sv, sv.n)
        C.wait("sync", sxs, sxs.n)
        gd.acquire("sync", ("xcur",))
        vx = C.dma("sync", xcur.rearrange("p (oc t) -> p oc t", oc=16),
                   scr["xspill"].ap().rearrange("oc p t -> p oc t")
                   [:, :, qc * CH:(qc + 1) * CH], sq_)
        C.barrier([(sq_, vx)], engines=("vector",))
        def wov_load(oc):
            sl = oc % 2
            gd.acquire("sync", ("wov", sl))
            C.dma("sync", wov[sl], scr["aWoT"][ts(oc), :], sld[sl])
            return sld[sl].n

        wovt = [None, None]
        wovt[0] = wov_load(0)
        for oc in range(NFC):
            sl = oc % 2
            if oc + 1 < NFC:
                wovt[(oc + 1) % 2] = wov_load(oc + 1)
            C.wait("tensor", sld[sl], wovt[sl])
            pb_i = 2 + sl
            gd.acquire("tensor", ("bank", pb_i))
            for dc in range(NFC):
                C.op("tensor", lambda e, pb_i=pb_i, sl=sl, dc=dc:
                     e.matmul(bank(pb_i), wov[sl][:, ts(dc)],
                              o_all[:, dc * 512:(dc + 1) * 512],
                              start=(dc == 0), stop=(dc == NFC - 1)),
                     sem=(sm if dc == NFC - 1 else None))
            gd.use(("wov", sl), sm)
            if oc == NFC - 1:
                for hh in range(NH):
                    gd.use(("oall", hh), sm)
            C.wait("vector", sm, sm.n)
            xslice = xcur[:, oc * 512:(oc + 1) * 512]
            C.op("vector", lambda e, pb_i=pb_i, xslice=xslice:
                 e.tensor_tensor(xslice, bank(pb_i), xslice, ALU.add), sem=sv)
            gd.use(("bank", pb_i), sv)
        C.wait("sync", sv, sv.n)
        for oc in range(NFC):
            C.dma("sync", scr["xspill"][oc, :, qc * CH:(qc + 1) * CH],
                  xcur[:, oc * 512:(oc + 1) * 512], sxs)
        gd.use(("xcur",), sxs)
    sxr = C.sem("p1xr")
    C.wait("sync", sxs, sxs.n)
    for fc in range(NFC):
        C.dma("sync", S["x_fm"][:, fc * TOK:(fc + 1) * TOK], scr["xspill"][fc, :, :],
              sxr)
    return [(sxr, sxr.n), (sm, sm.n), (sv, sv.n), (ss, ss.n)]


def phase_P2(S, fence, tag):
    C, bank, arA, arB, arC = S["C"], S["bank"], S["arA"], S["arB"], S["arC"]
    scr = S["scr"]
    sm = C.sem(tag + "m")
    sv = C.sem(tag + "v")
    ss = C.sem(tag + "s")
    sg = C.sem(tag + "g")
    so2 = [C.sem(tag + "o0"), C.sem(tag + "o1")]
    swq = C.sem(tag + "wq")
    sw3 = [C.sem(tag + "w30"), C.sem(tag + "w31")]
    sh2 = [C.sem(tag + "h0"), C.sem(tag + "h1")]
    gd = Guard(C)
    C.barrier(fence, engines=("tensor", "vector", "scalar", "gpsimd", "sync"))
    lnsems = dict(m=sm, v=sv, s=ss, g=sg, o2=so2)
    for tc in range(NTC):
        emit_ln(S, lnsems, tc, "h_scr")
    C.wait("sync", so2[0], so2[0].n)
    C.wait("sync", so2[1], so2[1].n)
    qw1b = arC[:, 0:8192]
    qw2b = arC[:, 8192:16384]
    hid = arC[:, 16384:24576]
    h2b = [arA[:, 0:4096], arA[:, 4096:8192]]
    s1b = [arB[:, i * 512:(i + 1) * 512] for i in range(2)]
    qw3b = [arB[:, 1024 + i * 512: 1024 + (i + 1) * 512] for i in range(2)]
    h_scr = scr["h_scr"]
    for g in range(11):
        hcs = [4 * g + i for i in range(4)]
        gd.acquire("sync", ("qw12",))
        for hcl, hc in enumerate(hcs):
            C.dma("sync", qw1b[:, hcl * 2048:(hcl + 1) * 2048],
                  scr["qw1"][ts(hc), :], swq)
            C.dma("sync", qw2b[:, hcl * 2048:(hcl + 1) * 2048],
                  scr["qw2"][ts(hc), :], swq)
        C.wait("tensor", swq, swq.n)
        for tc in range(NTC):
            for half in range(2):
                qeng = "sync" if half == 0 else "scalar"
                gd.acquire(qeng, ("h2", half))
                C.dma(qeng,
                      h2b[half].rearrange("p (f j) -> p f j", f=8),
                      h_scr.ap()[half * 8:(half + 1) * 8, :,
                                 tc * CH:(tc + 1) * CH]
                      .rearrange("f p j -> p f j"), sh2[half])
            C.wait("tensor", sh2[0], sh2[0].n)
            C.wait("tensor", sh2[1], sh2[1].n)
            for hcl in range(4):
                b1 = (hcl % 2) * 2
                b2 = (hcl % 2) * 2 + 1
                gd.acquire("tensor", ("bank", b1))
                for fc in range(NFC):
                    C.op("tensor", lambda e, hcl=hcl, fc=fc, b1=b1:
                         e.matmul(bank(b1), qw1b[:, hcl * 2048 + fc * 128:
                                               hcl * 2048 + (fc + 1) * 128],
                                  h2b[fc // 8][:, (fc % 8) * 512:(fc % 8 + 1) * 512],
                                  start=(fc == 0), stop=(fc == NFC - 1)),
                         sem=(sm if fc == NFC - 1 else None))
                C.wait("scalar", sm, sm.n)
                gd.acquire("scalar", ("s1", hcl % 2))
                C.op("scalar", lambda e, hcl=hcl, b1=b1:
                     e.activation(s1b[hcl % 2], bank(b1), AF.Sigmoid), sem=ss)
                gd.use(("bank", b1), ss)
                gd.acquire("tensor", ("bank", b2))
                for fc in range(NFC):
                    C.op("tensor", lambda e, hcl=hcl, fc=fc, b2=b2:
                         e.matmul(bank(b2), qw2b[:, hcl * 2048 + fc * 128:
                                               hcl * 2048 + (fc + 1) * 128],
                                  h2b[fc // 8][:, (fc % 8) * 512:(fc % 8 + 1) * 512],
                                  start=(fc == 0), stop=(fc == NFC - 1)),
                         sem=(sm if fc == NFC - 1 else None))
                C.wait("vector", sm, sm.n)
                C.wait("vector", ss, ss.n)
                gd.acquire("vector", ("hid",))
                C.op("vector", lambda e, hcl=hcl, tc=tc, b2=b2:
                     e.tensor_tensor(hid[:, (hcl * 4 + tc) * 512:
                                         (hcl * 4 + tc + 1) * 512],
                                     s1b[hcl % 2], bank(b2), ALU.mult), sem=sv)
                gd.use(("bank", b2), sv)
                gd.use(("s1", hcl % 2), sv)
            gd.use(("h2", 0), sm)
            gd.use(("h2", 1), sm)
        gd.use(("qw12",), sm)
        C.wait("tensor", sv, sv.n)
        def qw3_load(oc):
            sl = oc % 2
            gd.acquire("sync", ("qw3", sl))
            C.dma("sync",
                  qw3b[sl].rearrange("p (h i) -> p h i", h=4),
                  scr["qw3"].ap().rearrange("(h p) o -> h p o", p=128)
                  [4 * g:4 * g + 4, :, oc * 128:(oc + 1) * 128]
                  .rearrange("h p o -> p h o"), sw3[sl])
            return sw3[sl].n

        qw3t = [None, None]
        qw3t[0] = qw3_load(0)
        for oc in range(NFC):
            sl = oc % 2
            if oc + 1 < NFC:
                qw3t[(oc + 1) % 2] = qw3_load(oc + 1)
            C.wait("tensor", sw3[sl], qw3t[sl])
            for tc in range(NTC):
                ob = (tc % 2)
                gd.acquire("tensor", ("bank", ob))
                for hcl in range(4):
                    C.op("tensor", lambda e, sl=sl, hcl=hcl, tc=tc, ob=ob:
                         e.matmul(bank(ob), qw3b[sl][:, hcl * 128:(hcl + 1) * 128],
                                  hid[:, (hcl * 4 + tc) * 512:
                                      (hcl * 4 + tc + 1) * 512],
                                  start=(hcl == 0), stop=(hcl == 3)),
                         sem=(sm if hcl == 3 else None))
                C.wait("vector", sm, sm.n)
                xsl = S["x_fm"][:, oc * TOK + tc * CH: oc * TOK + (tc + 1) * CH]
                C.op("vector", lambda e, xsl=xsl, ob=ob:
                     e.tensor_tensor(xsl, bank(ob), xsl, ALU.add), sem=sv)
                gd.use(("bank", ob), sv)
            gd.use(("qw3", sl), sm)
        gd.use(("hid",), sm)
    return [(sm, sm.n), (sv, sv.n), (ss, ss.n)]


def phase_P3(S, fence):
    C, bank, bankb = S["C"], S["bank"], S["bankb"]
    scr, ein, cb = S["scr"], S["ein"], S["cb"]
    x_fm, arXb = S["x_fm"], S["arXb"]
    onesb, identb = cb["onesb"], cb["identb"]
    ehmat = arXb[:, 57344:59392]
    crow_sb, bias_sb = S["crow_sb"], S["bias_sb"]
    sm = C.sem("p3m")
    sv = C.sem("p3v")
    ss = C.sem("p3s")
    sg = C.sem("p3g")
    so2 = [C.sem("p3o0"), C.sem("p3o1")]
    sp = C.sem("p3p")
    sld = [C.sem("p3l0"), C.sem("p3l1")]
    sq_ = C.sem("p3q")
    scc = C.sem("p3cc")
    sko2 = [C.sem("p3ko0"), C.sem("p3ko1")]
    skv = [C.sem("p3kv0"), C.sem("p3kv1")]
    sxs = C.sem("p3xs")
    shl = C.sem("p3hl")
    sms = C.sem("p3ms")
    gd = Guard(C)

    C.barrier(fence, engines=("tensor", "vector", "scalar", "gpsimd", "sync"))
    lnsems = dict(m=sm, v=sv, s=ss, g=sg, o2=so2)
    for tc in range(NTC):
        emit_ln(S, lnsems, tc, "h_scr")
    C.wait("scalar", sv, sv.n)
    C.wait("scalar", sg, sg.n)
    for fc in range(NFC):
        C.dma("scalar", scr["xspill"][fc, :, :], x_fm[:, fc * TOK:(fc + 1) * TOK], sp)
    h_all = arXb[:, 0:NFC * TOK]
    vchunk = arXb[:, 32768:40960]
    vout = arXb[:, 40960:45056]
    outb = arXb[:, 45056:47104]
    wbuf = [arXb[:, 47104 + i * 2048: 47104 + (i + 1) * 2048] for i in range(2)]
    C.wait("sync", sp, sp.n)
    C.wait("sync", so2[0], so2[0].n)
    C.wait("sync", so2[1], so2[1].n)
    for fc in range(NFC):
        C.dma("sync", h_all[:, fc * TOK:(fc + 1) * TOK], scr["h_scr"][fc, :, :], shl)
    C.dma("sync", ehmat, ein["ehmat"][:, :], shl)
    C.barrier([(shl, shl.n)], engines=("tensor", "vector", "scalar"))

    ldtgt3 = {}

    def load_wb(wname, oc, li):
        sl = li % 2
        gd.acquire("sync", ("wbuf", sl))
        for hf in range(2):
            C.dma("sync",
                  wbuf[sl][:, hf * 1024:(hf + 1) * 1024]
                  .rearrange("p (c i) -> p c i", c=8),
                  scr[wname].ap().rearrange("(c p) o -> c p o", p=128)
                  [hf * 8:(hf + 1) * 8, :, oc * 128:(oc + 1) * 128]
                  .rearrange("c p o -> p c o"), sld[sl])
        ldtgt3[sl] = sld[sl].n

    def proj3(li, tc, pb_i):
        w = wbuf[li % 2]
        C.wait("tensor", sld[li % 2], ldtgt3[li % 2])
        gd.acquire("tensor", ("bank", pb_i))
        for cc in range(NFC):
            C.op("tensor", lambda e, w=w, cc=cc, tc=tc, pb_i=pb_i:
                 e.matmul(bank(pb_i), w[:, ts(cc)],
                          h_all[:, cc * TOK + tc * CH: cc * TOK + (tc + 1) * CH],
                          start=(cc == 0), stop=(cc == NFC - 1)),
                 sem=(sm if cc == NFC - 1 else None))
        gd.use(("wbuf", li % 2), sm)

    li = 0
    for wi, wname, dst in ((0, "bWqb", "qb_scr"), (1, "bWkb", "kinb")):
        load_wb(wname, 0, li)
        for oc in range(NFC):
            if oc + 1 < NFC:
                load_wb(wname, oc + 1, li + 1)
            for tc in range(NTC):
                pb_i = 2 + (tc % 2)
                proj3(li, tc, pb_i)
                sl = tc % 2
                C.wait("vector", sm, sm.n)
                gd.acquire("vector", ("outb", sl))
                dst_bf = outb[:, sl * 512:(sl + 1) * 512]
                C.op("vector", lambda e, pb_i=pb_i, dst_bf=dst_bf, wi=wi, oc=oc:
                     e.tensor_scalar(dst_bf, bank(pb_i),
                                     bias_sb[:, wi * 16 + oc: wi * 16 + oc + 1],
                                     None, ALU.add), sem=sv)
                gd.use(("bank", pb_i), sv)
                C.wait("scalar", sv, sv.n)
                C.dma("scalar", scr[dst][ts(oc), tc * CH:(tc + 1) * CH], dst_bf,
                      sko2[sl])
                gd.use(("outb", sl), sko2[sl])
            li += 1
    for tc in range(NTC):
        load_wb("bWvb", 0, li)
        for oc in range(NFC):
            if oc + 1 < NFC:
                load_wb("bWvb", oc + 1, li + 1)
            pb_i = 2 + (oc % 2)
            proj3(li, tc, pb_i)
            C.wait("vector", sm, sm.n)
            gd.acquire("vector", ("vchunk", oc))
            C.op("vector", lambda e, pb_i=pb_i, oc=oc:
                 e.tensor_scalar(vchunk[:, oc * 512:(oc + 1) * 512], bank(pb_i),
                                 bias_sb[:, 32 + oc: 33 + oc], None, ALU.add),
                 sem=sv)
            gd.use(("bank", pb_i), sv)
            li += 1
        C.wait("tensor", sv, sv.n)
        for tt in range(4):
            for g4 in range(4):
                tb_i = 4 + (g4 % 2)
                gd.acquire("tensor", ("bank", tb_i))
                for m in range(4):
                    oc = g4 * 4 + m
                    C.op("tensor", lambda e, tb_i=tb_i, oc=oc, tt=tt, m=m:
                         e.transpose(bankb(tb_i)[:, 0:512][:, ts(m)],
                                     vchunk[:, oc * 512 + tt * 128:
                                            oc * 512 + (tt + 1) * 128],
                                     identb[:, :]),
                         sem=(sm if m == 3 else None))
                    gd.use(("vchunk", oc), sm)
                C.wait("vector", sm, sm.n)
                if g4 == 0:
                    gd.acquire("vector", ("vout",))
                C.op("vector", lambda e, tb_i=tb_i, g4=g4:
                     e.tensor_copy(vout[:, g4 * 512:(g4 + 1) * 512],
                                   bankb(tb_i)[:, 0:512]), sem=sv)
                gd.use(("bank", tb_i), sv)
            C.wait("scalar", sv, sv.n)
            C.dma("scalar", scr["vinb"][tc * CH + tt * 128: tc * CH + (tt + 1) * 128, :],
                  vout[:, 0:2048], sko2[0])
            gd.use(("vout",), sko2[0])
    C.wait("gpsimd", sko2[0], sko2[0].n)
    C.wait("gpsimd", sko2[1], sko2[1].n)
    _ag(C, scr["kinb"].ap(), scr["kgb"].ap(), G2, scc)
    _ag(C, scr["vinb"].ap(), scr["vgb"].ap(), G2, scc)
    C.barrier([(scc, 2), (sm, sm.n), (sv, sv.n), (ss, ss.n)],
              engines=("tensor", "scalar", "vector", "sync"))

    qsb = arXb[:, 0:8192]
    o_all = arXb[:, 8192:16384]
    kvr = [arXb[:, 16384 + i * 4096: 16384 + (i + 1) * 4096] for i in range(2)]
    p_sb = [arXb[:, 24576 + i * 512: 24576 + (i + 1) * 512] for i in range(4)]
    s_sb = [S["arX"][:, 13312 + i * 512: 13312 + (i + 1) * 512] for i in range(2)]
    masks8 = arXb[:, 28672:32768]
    rec_sb = S["arX"][:, 16384:16896]
    lnt = S["arX"][:, 16896:17408]
    kb_sb = S["arX"][:, 17664:18240]
    xcur = S["arX"][:, 18432:26624]
    wov = [arXb[:, 53248 + i * 2048: 53248 + (i + 1) * 2048] for i in range(2)]

    def g2ri(gkt):
        return g2rank_idx(128 * gkt)

    for slot in range(NTC):
        ktn = KTN[slot]
        pieces = [(p0, min(p0 + 8, ktn)) for p0 in range(0, ktn, 8)]
        gd.acquire("sync", ("qsb",))
        vq = C.dma("sync", qsb.rearrange("p (oc t) -> p oc t", oc=16),
                   scr["qb_scr"].ap().rearrange("(oc p) t -> p oc t", p=128)
                   [:, :, slot * CH:(slot + 1) * CH], sq_)
        gd.acquire("sync", ("masks8",))
        vmq = C.dma("sync", masks8.rearrange("p (j x) -> p j x", j=8),
                    ein["mdyn"].ap()[slot].rearrange("j p x -> p j x"), sms)
        gd.acquire("sync", ("kb",))
        vkb = C.dma("sync", kb_sb[:, 0:ktn * 16].rearrange("p (i h) -> p i h", i=ktn),
                    ein["kbias"].ap()[slot, 0:ktn].rearrange("i p h -> p i h"), sms)
        C.barrier([(sq_, vq)], engines=("tensor",))
        C.barrier([(sms, sms.n)], engines=("tensor", "vector", "scalar"))

        def load_piece(pi, slot_buf):
            p0, p1 = pieces[pi]
            L = p1 - p0
            sem = skv[slot_buf]
            gd.acquire("sync", ("kv", slot_buf))
            runs = []
            r0 = p0
            while r0 < p1:
                rk, idx = g2ri(r0)
                r1 = r0
                while r1 + 1 < p1 and g2ri(r1 + 1)[0] == rk:
                    r1 += 1
                runs.append((r0, r1 + 1, rk, idx))
                r0 = r1 + 1
            Lp = p1 - p0
            for (a, b, rk, idx) in runs:
                Lr = b - a
                for hh in range(2):
                    gh = hg * 2 + hh
                    C.dma("sync",
                          kvr[slot_buf][:, hh * Lp * 128 + (a - p0) * 128:
                                        hh * Lp * 128 + (b - p0) * 128],
                          scr["kgb"].ap()
                          .rearrange("(r c p) t -> r c p t", r=2, p=128)
                          [rk, gh, :, idx:idx + Lr * 128], sem)
                C.dma("sync",
                      kvr[slot_buf][:, 2048 + (a - p0) * 256:
                                    2048 + (a - p0) * 256 + Lr * 256]
                      .rearrange("p (l x) -> p l x", l=Lr),
                      scr["vgb"].ap()
                      .rearrange("(r l p) d -> r l p d", r=2, p=128)
                      [rk, idx // 128: idx // 128 + Lr, :,
                       (hg * 2) * 128:(hg * 2 + 2) * 128]
                      .rearrange("l p d -> p l d"), sem)
            return sem.n

        for hg in range(8):
            kvt = [None, None]
            kvt[0] = load_piece(0, 0)
            units = []
            for pi, (p0, p1) in enumerate(pieces):
                for itl in range(p1 - p0):
                    for hh in range(2):
                        units.append((p0 + itl, hh, pi, pi % 2, itl, p1 - p0))

            def emit_qk3(u):
                it, hh, pi, sbuf_i, itl, Lp_ = u
                gh = hg * 2 + hh
                sti = (it * 2 + hh) % 4
                gd.acquire("tensor", ("st", sti))
                qh = qsb[:, gh * 512:(gh + 1) * 512]
                C.op("tensor", lambda e, sti=sti, sbuf_i=sbuf_i, itl=itl,
                     hh=hh, qh=qh, Lp_=Lp_:
                     e.matmul(bank(sti),
                              kvr[sbuf_i][:, hh * Lp_ * 128 + itl * 128:
                                          hh * Lp_ * 128 + (itl + 1) * 128],
                              qh, start=True, stop=False))
                C.op("tensor", lambda e, sti=sti, gh=gh:
                     e.matmul(bank(sti), ehmat[:, ts(gh)], crow_sb[:, :],
                              start=False, stop=True), sem=sm)
                kbv = kb_sb[:, it * 16 + gh: it * 16 + gh + 1]
                if it in MASKPOS[slot]:
                    mj = MASKPOS[slot].index(it)
                    C.wait("vector", sm, sm.n)
                    gd.acquire("vector", ("ssb", it % 2))
                    C.op("vector", lambda e, sti=sti, mj=mj, it=it:
                         e.tensor_tensor(s_sb[it % 2], bank(sti),
                                         masks8[:, mj * 512:(mj + 1) * 512],
                                         ALU.add), sem=sv)
                    gd.use(("st", sti), sv)
                    C.wait("scalar", sv, sv.n)
                    gd.acquire("scalar", ("p", (it * 2 + hh) % 4))
                    C.op("scalar", lambda e, it=it, hh=hh, kbv=kbv:
                         e.activation(p_sb[(it * 2 + hh) % 4], s_sb[it % 2],
                                      AF.Exp, bias=kbv, scale=ISQD), sem=ss)
                    gd.use(("ssb", it % 2), ss)
                else:
                    C.wait("scalar", sm, sm.n)
                    gd.acquire("scalar", ("p", (it * 2 + hh) % 4))
                    C.op("scalar", lambda e, sti=sti, it=it, hh=hh, kbv=kbv:
                         e.activation(p_sb[(it * 2 + hh) % 4], bank(sti),
                                      AF.Exp, bias=kbv, scale=ISQD), sem=ss)
                    gd.use(("st", sti), ss)

            def emit_pv3(u, last_of_piece):
                it, hh, pi, sbuf_i, itl, Lp_ = u
                pb = p_sb[(it * 2 + hh) % 4]
                if it == 0:
                    gd.acquire("tensor", ("bank", 4 + hh))
                    gd.acquire("tensor", ("bank", 6 + hh))
                C.wait("tensor", ss, ss.n)
                C.op("tensor", lambda e, hh=hh, sbuf_i=sbuf_i, itl=itl, pb=pb,
                     it=it, Lp_=Lp_, ktn=ktn:
                     e.matmul(bank(4 + hh),
                              kvr[sbuf_i][:, 2048 + itl * 256 + hh * 128:
                                          2048 + itl * 256 + (hh + 1) * 128],
                              pb, start=(it == 0), stop=(it == ktn - 1)))
                C.op("tensor", lambda e, hh=hh, pb=pb, it=it, ktn=ktn:
                     e.matmul(bank(6 + hh), onesb[:, :], pb,
                              start=(it == 0), stop=(it == ktn - 1)), sem=sm)
                gd.use(("p", (it * 2 + hh) % 4), sm)
                if last_of_piece:
                    gd.use(("kv", sbuf_i), sm)

            for ui, u in enumerate(units):
                it, hh, pi, sbuf_i, itl, Lp_ = u
                if hh == 0 and itl == 0:
                    C.wait("tensor", skv[sbuf_i], kvt[sbuf_i])
                emit_qk3(u)
                if ui > 0:
                    up = units[ui - 1]
                    emit_pv3(up, up[2] != pi)
                if hh == 0 and itl == 0 and pi + 1 < len(pieces):
                    kvt[(pi + 1) % 2] = load_piece(pi + 1, (pi + 1) % 2)
            emit_pv3(units[-1], True)
            for hh in range(2):
                gh = hg * 2 + hh
                C.wait("scalar", sm, sm.n)
                gd.acquire("scalar", ("rec",))
                C.op("scalar", lambda e, hh=hh: e.activation(lnt, bank(6 + hh), AF.Ln))
                C.op("scalar", lambda e: e.activation(rec_sb, lnt, AF.Exp, scale=-1.0),
                     sem=ss)
                gd.use(("bank", 6 + hh), ss)
                C.wait("vector", ss, ss.n)
                gd.acquire("vector", ("oall", gh))
                C.op("vector", lambda e, gh=gh, hh=hh:
                     e.tensor_tensor(o_all[:, gh * 512:(gh + 1) * 512], bank(4 + hh),
                                     rec_sb, ALU.mult), sem=sv)
                gd.use(("bank", 4 + hh), sv)
                gd.use(("rec",), sv)
        gd.use(("qsb",), sm)
        gd.use(("masks8",), sv)
        gd.use(("kb",), ss)
        C.wait("tensor", sv, sv.n)
        C.wait("sync", sxs, sxs.n)
        gd.acquire("sync", ("xcur",))
        vx = C.dma("sync", xcur.rearrange("p (oc t) -> p oc t", oc=16),
                   scr["xspill"].ap().rearrange("oc p t -> p oc t")
                   [:, :, slot * CH:(slot + 1) * CH], sq_)
        C.barrier([(sq_, vx)], engines=("vector",))
        def wov3_load(oc):
            sl = oc % 2
            gd.acquire("sync", ("wov", sl))
            for hf in range(2):
                C.dma("sync",
                      wov[sl][:, hf * 1024:(hf + 1) * 1024]
                      .rearrange("p (c i) -> p c i", c=8),
                      scr["bWob"].ap().rearrange("(c p) o -> c p o", p=128)
                      [hf * 8:(hf + 1) * 8, :, oc * 128:(oc + 1) * 128]
                      .rearrange("c p o -> p c o"), sld[sl])
            return sld[sl].n

        wovt3 = [None, None]
        wovt3[0] = wov3_load(0)
        for oc in range(NFC):
            sl = oc % 2
            if oc + 1 < NFC:
                wovt3[(oc + 1) % 2] = wov3_load(oc + 1)
            C.wait("tensor", sld[sl], wovt3[sl])
            pb_i = 2 + sl
            gd.acquire("tensor", ("bank", pb_i))
            for dc in range(NFC):
                C.op("tensor", lambda e, pb_i=pb_i, sl=sl, dc=dc:
                     e.matmul(bank(pb_i), wov[sl][:, ts(dc)],
                              o_all[:, dc * 512:(dc + 1) * 512],
                              start=(dc == 0), stop=(dc == NFC - 1)),
                     sem=(sm if dc == NFC - 1 else None))
            gd.use(("wov", sl), sm)
            if oc == NFC - 1:
                for hh2 in range(NH):
                    gd.use(("oall", hh2), sm)
            C.wait("vector", sm, sm.n)
            xslice = xcur[:, oc * 512:(oc + 1) * 512]
            C.op("vector", lambda e, pb_i=pb_i, xslice=xslice, oc=oc:
                 e.scalar_tensor_tensor(xslice, bank(pb_i),
                                        bias_sb[:, 48 + oc: 49 + oc], xslice,
                                        ALU.add, ALU.add), sem=sv)
            gd.use(("bank", pb_i), sv)
        C.wait("sync", sv, sv.n)
        for oc in range(NFC):
            C.dma("sync", scr["xspill"][oc, :, slot * CH:(slot + 1) * CH],
                  xcur[:, oc * 512:(oc + 1) * 512], sxs)
        gd.use(("xcur",), sxs)
    sxr = C.sem("p3xr")
    C.wait("sync", sxs, sxs.n)
    for fc in range(NFC):
        C.dma("sync", S["x_fm"][:, fc * TOK:(fc + 1) * TOK], scr["xspill"][fc, :, :],
              sxr)
    return [(sxr, sxr.n), (sm, sm.n), (sv, sv.n), (ss, ss.n)]


def phase_P9(S, fence):
    C, bank = S["C"], S["bank"]
    x_fm, xout = S["x_fm"], S["xout"]
    identf = S["cb"]["identf"]
    sd = [C.sem(f"p9d{i}") for i in range(2)]
    sm = C.sem("p9m")
    sv = C.sem("p9v")
    ot = [S["arAf"][:, 0:2048], S["arAf"][:, 2048:4096]]
    C.barrier(fence, engines=("tensor", "vector", "scalar", "sync", "gpsimd"))
    gi = 0
    for tt in range(16):
        bi = tt % 2
        buf = ot[bi]
        for g in range(4):
            bk = bank(gi % 2)
            if gi >= 2:
                C.wait("tensor", sv, gi - 1)
            for f4 in range(4):
                fc = g * 4 + f4
                C.op("tensor", lambda e, bk=bk, fc=fc, f4=f4, tt=tt:
                     e.transpose(bk[:, ts(f4)],
                                 x_fm[:, fc * TOK + tt * 128: fc * TOK + (tt + 1) * 128],
                                 identf[:, :]),
                     sem=(sm if f4 == 3 else None))
            C.wait("vector", sm, sm.n)
            if tt >= 2 and g == 0:
                C.wait("vector", sd[bi], 16 * (tt // 2))
            C.op("vector", lambda e, bk=bk, buf=buf, g=g:
                 e.tensor_copy(buf[:, g * 512:(g + 1) * 512], bk), sem=sv, drain=True)
            gi += 1
        C.wait("sync", sv, sv.n)
        C.dma("sync", xout[ts(tt), :], buf, sd[bi])
    C.wait("sync", sd[0], sd[0].n)
    C.wait("sync", sd[1], sd[1].n)


def kernel(**inputs):
    xfull = np.ascontiguousarray(np.asarray(inputs["L0_attn_a_x"], dtype=np.float32))
    W = {k: np.asarray(inputs[k], dtype=np.float32) for k in
         ("L0_attn_a_Wq", "L0_attn_a_Wk", "L0_attn_a_Wv", "L0_attn_a_Wo",
          "L0_attn_b_Wq", "L0_attn_b_Wk", "L0_attn_b_Wv", "L0_attn_b_Wo",
          "L0_ffn_w1", "L0_ffn_w2", "L0_ffn_w3",
          "L0_attn_b_bq", "L0_attn_b_bk", "L0_attn_b_bv", "L0_attn_b_bo")}
    fw1p = np.zeros((6144, D), dtype=np.float32)
    fw1p[:HID] = W["L0_ffn_w1"]
    fw2p = np.zeros((6144, D), dtype=np.float32)
    fw2p[:HID] = W["L0_ffn_w2"]
    fw3p = np.zeros((D, 6144), dtype=np.float32)
    fw3p[:, :HID] = W["L0_ffn_w3"]
    nc = build_nc(_PHASES)
    in_maps = []
    for c in range(8):
        b = c // 2
        pos = np.concatenate([np.arange(t0, t0 + CH) for t0 in chunk_starts(c)])
        m = {"xin": np.ascontiguousarray(xfull[b, pos, :])}
        m["aWq"] = np.ascontiguousarray(W["L0_attn_a_Wq"][256 * c:256 * (c + 1), :])
        m["aWk"] = np.ascontiguousarray(W["L0_attn_a_Wk"][128 * (c % 4):128 * (c % 4 + 1), :])
        m["aWv"] = np.ascontiguousarray(W["L0_attn_a_Wv"][128 * (c % 4):128 * (c % 4 + 1), :])
        m["aWo"] = np.ascontiguousarray(W["L0_attn_a_Wo"][256 * c:256 * (c + 1), :])
        for k, nm in (("bWq", "L0_attn_b_Wq"), ("bWk", "L0_attn_b_Wk"),
                      ("bWv", "L0_attn_b_Wv"), ("bWo", "L0_attn_b_Wo")):
            m[k] = np.ascontiguousarray(W[nm][256 * c:256 * (c + 1), :])
        m["fw1"] = np.ascontiguousarray(fw1p[768 * c:768 * (c + 1), :])
        m["fw2"] = np.ascontiguousarray(fw2p[768 * c:768 * (c + 1), :])
        m["fw3"] = np.ascontiguousarray(fw3p[:, 768 * c:768 * (c + 1)])
        for k, nm in (("bbq", "L0_attn_b_bq"), ("bbk", "L0_attn_b_bk"),
                      ("bbv", "L0_attn_b_bv"), ("bbo", "L0_attn_b_bo")):
            m[k] = np.ascontiguousarray(W[nm].reshape(16, 128))
        hc = host_consts(c)
        for k in ("ropecos", "ropesin", "rotT", "identf", "identb", "onesb", "crow",
                  "ehmat", "masks", "kbias", "kvoff", "self4", "mdyn"):
            m[k] = hc[k]
        in_maps.append(m)
    res = run_bass_kernel_spmd(nc, in_maps, core_ids=list(range(8)))
    out = np.empty((B, S, D), dtype=np.float32)
    for c in range(8):
        b = c // 2
        pos = np.concatenate([np.arange(t0, t0 + CH) for t0 in chunk_starts(c)])
        out[b, pos, :] = res.results[c]["xout"]
    return out


# version 07
# speedup vs baseline: 1.3203x; 1.3203x over previous
import math
from contextlib import ExitStack

import numpy as np
import ml_dtypes

import concourse.bass as bass
import concourse.mybir as mybir
from concourse.bass_utils import run_bass_kernel_spmd

F32 = mybir.dt.float32
BF16 = mybir.dt.bfloat16
I32 = mybir.dt.int32
AF = mybir.ActivationFunctionType
ALU = mybir.AluOpType
AX = mybir.AxisListType
bfdt = ml_dtypes.bfloat16

B, S, D = 4, 4096, 2048
HID = 5632
NH, NKV, HD = 16, 4, 128
TOK = 2048
CH = 512
NTC = TOK // CH
NFC = D // 128
NHC = HID // 128
SQD = math.sqrt(128.0)
ISQD = 1.0 / SQD
ACONST = -16.0
KTN = [12, 16, 28, 32]
NEG = -1.0e30
LN_EPS = 1e-5
Q_EPS = 1e-8
NGRP = 8
FFN_GRPS = [list(range(g, min(g + 6, NHC))) for g in range(0, NHC, 6)]

PHASES_ALL = ("P0", "PW", "PG", "P1", "P2", "P3", "P4", "P9")
_PHASES = PHASES_ALL


def chunk_starts(core):
    return [0, 512, 3072, 3584] if core % 2 == 0 else [1024, 1536, 2048, 2560]


def _maskpos():
    re_ = [(t0 + CH) // 128 for t0 in chunk_starts(0)]
    ro_ = [(t0 + CH) // 128 for t0 in chunk_starts(1)]
    out = []
    for s_i in range(NTC):
        out.append(sorted(set(range(re_[s_i] - 4, re_[s_i])) |
                          set(range(ro_[s_i] - 4, ro_[s_i]))))
    return out


MASKPOS = _maskpos()

def g2rank_idx(g):
    if g < 1024:
        return 0, g
    if g < 3072:
        return 1, g - 1024
    return 0, g - 2048


def _slopes16():
    start = 2.0 ** (-(2.0 ** -(math.log2(16) - 3)))
    return np.array([start * start ** i for i in range(16)], dtype=np.float32)


def host_consts(core):
    c = {}
    pos = np.concatenate([np.arange(t0, t0 + CH) for t0 in chunk_starts(core)])
    ar = np.arange(0, HD, 2, dtype=np.float32) / np.float32(HD)
    inv = (np.float32(1.0) / (np.float32(10000.0) ** ar)).astype(np.float32)
    t = np.arange(S, dtype=np.float32)
    freqs = (t[:, None] * inv[None, :]).astype(np.float32)
    emb = np.concatenate([freqs, freqs], axis=1)
    c["ropecos"] = np.ascontiguousarray(np.cos(emb).astype(np.float32).T[:, pos])
    c["ropesin"] = np.ascontiguousarray(np.sin(emb).astype(np.float32).T[:, pos])
    R = np.zeros((HD, HD), dtype=np.float32)
    for d in range(64):
        R[d, d + 64] = -1.0
        R[64 + d, d] = 1.0
    c["rotT"] = np.ascontiguousarray(R.T).astype(bfdt)
    c["identf"] = np.eye(128, dtype=np.float32)
    c["identb"] = np.eye(128, dtype=bfdt)
    c["onesb"] = np.ones((128, 128), dtype=bfdt)
    slopes = _slopes16()
    j = np.arange(CH, dtype=np.float32)
    c["crow"] = (-(SQD) * slopes[:, None] * j[None, :]).astype(bfdt)
    eh = np.zeros((128, NH * 128), dtype=np.float32)
    for h in range(NH):
        eh[h, h * 128:(h + 1) * 128] = 1.0
    c["ehmat"] = eh.astype(bfdt)
    kk = np.arange(128)[:, None]
    jj = np.arange(CH)[None, :]
    masks = np.concatenate([(kk + d > jj).astype(np.float32) * NEG
                            for d in (0, 128, 256, 384)], axis=1)
    c["masks"] = np.ascontiguousarray(masks.astype(np.float32))
    starts = chunk_starts(core)
    kb = np.zeros((NTC, 36, 128, NH), dtype=np.float32)
    kkv = np.arange(128, dtype=np.float32)
    for s_i, t0 in enumerate(starts):
        R_real = (t0 + CH) // 128
        for it in range(KTN[s_i]):
            if it < R_real:
                gk = 128 * it + kkv
                kb[s_i, it, :, :] = slopes[None, :] * (gk[:, None] - t0)
            else:
                kb[s_i, it] = NEG
    c["kbias"] = kb
    c["maskpos"] = MASKPOS
    mdyn = np.zeros((NTC, 8, 128, CH), dtype=np.float32)
    kk2 = np.arange(128)[:, None]
    jj2 = np.arange(CH)[None, :]
    for s_i, t0 in enumerate(starts):
        R_real = (t0 + CH) // 128
        for j, it in enumerate(MASKPOS[s_i]):
            if R_real - 4 <= it < R_real:
                d = 128 * it - t0
                mdyn[s_i, j] = (kk2 + d > jj2).astype(np.float32) * NEG
    c["mdyn"] = mdyn.astype(bfdt)
    off = np.zeros((NTC, 36, 2), dtype=np.int64)
    c["kvoff"] = off.astype(np.int32)
    c["self4"] = np.concatenate(
        [np.tile((np.arange(4) == w).astype(np.float32)[:, None], (1, 128))
         for w in range(3)], axis=1)
    return c


ENGS = ("tensor", "vector", "scalar", "gpsimd", "sync")


class Sem:
    __slots__ = ("h", "n", "name")

    def __init__(self, h, name):
        self.h, self.n, self.name = h, 0, name

    def bump(self, k=1):
        self.n += k
        return self.n


class Ctx:
    def __init__(self, nc, st):
        self.nc = nc
        self.st = st
        self.q = {e: [] for e in ENGS}
        self._semn = 0
        self.chain = {e: None for e in ("vector", "scalar", "gpsimd")}
        self.last = {e: None for e in ("vector", "scalar", "gpsimd")}

    def sem(self, name=None):
        self._semn += 1
        h = self.st.enter_context(self.nc.semaphore(name or f"s{self._semn}"))
        return Sem(h, name)

    def sb(self, name, shape, dt):
        return self.st.enter_context(self.nc.sbuf_tensor(name, shape, dt))

    def emit(self, eng, fn):
        self.q[eng].append(fn)

    def wait(self, eng, sem, val):
        if val > 0:
            self.emit(eng, lambda e, s=sem.h, v=val: e.wait_ge(s, v))

    def op(self, eng, fn, sem=None, k=1, drain=False):
        tgt = sem.bump(k) if sem is not None else None
        if eng in self.chain:
            if self.chain[eng] is None:
                self.chain[eng] = self.sem("ch_" + eng)
            ch = self.chain[eng]
            prev = self.last[eng]
            if sem is not None:
                semh, tv, delta = sem.h, tgt, k
            else:
                tv = ch.bump(1)
                semh, delta = ch.h, 1
            if prev is not None:
                self.emit(eng, lambda e, s=prev[0], v=prev[1]: e.wait_ge(s, v))

            def run(e, fn=fn, semh=semh, delta=delta):
                ins = fn(e)
                (ins if ins is not None else e.nop()).then_inc(semh, delta)
            self.emit(eng, run)
            self.last[eng] = (semh, tv)
        else:
            def run(e, fn=fn, sem=sem, k=k, drain=drain):
                ins = fn(e)
                if sem is not None:
                    if drain:
                        e.drain().then_inc(sem.h, k)
                    elif ins is not None:
                        ins.then_inc(sem.h, k)
                    else:
                        e.nop().then_inc(sem.h, k)
            self.emit(eng, run)
        return tgt

    def dma(self, eng, out, in_, sem, **kw):
        tgt = sem.bump(16)
        self.emit(eng, lambda e, out=out, in_=in_, kw=kw, s=sem.h:
                  e.dma_start(out=out, in_=in_, **kw).then_inc(s, 16))
        return tgt

    def barrier(self, sems_vals, engines=ENGS):
        for e in engines:
            for sem, v in sems_vals:
                self.wait(e, sem, v)


def ts(i, n=128):
    return slice(i * n, (i + 1) * n)


class Guard:

    def __init__(self, C):
        self.C = C
        self.m = {}

    def use(self, key, sem):
        self.m.setdefault(key, {})[sem] = sem.n

    def acquire(self, eng, key):
        for sem, v in self.m.pop(key, {}).items():
            self.C.wait(eng, sem, v)


def _ag(C, in_ap, out_ap, groups, sem):
    tgt = sem.bump(1)
    C.emit("gpsimd", lambda e, i=in_ap, o=out_ap, g=groups, s=sem.h:
           e.collective_compute("AllGather", ALU.bypass, replica_groups=g,
                                ins=[i.opt()], outs=[o.opt()]).then_inc(s, 1))
    return tgt


G8 = [[0, 1, 2, 3, 4, 5, 6, 7]]
G4 = [[0, 1, 2, 3], [4, 5, 6, 7]]
G2 = [[0, 1], [2, 3], [4, 5], [6, 7]]


def build_nc(phases=None):
    if phases is None:
        phases = _PHASES
    nc = bass.Bass("TRN2", target_bir_lowering=False)
    st = ExitStack()
    C = Ctx(nc, st)

    ein = {}

    def EIN(name, shape, dt):
        ein[name] = nc.dram_tensor(name, shape, dt, kind="ExternalInput")
        return ein[name]

    xin = EIN("xin", [TOK, D], F32)
    for nm, shape in [("aWq", [256, D]), ("aWk", [128, D]), ("aWv", [128, D]),
                      ("aWo", [256, D]), ("bWq", [256, D]), ("bWk", [256, D]),
                      ("bWv", [256, D]), ("bWo", [256, D]),
                      ("fw1", [768, D]), ("fw2", [768, D]), ("fw3", [D, 768]),
                      ("bbq", [16, 128]), ("bbk", [16, 128]), ("bbv", [16, 128]),
                      ("bbo", [16, 128]),
                      ("ropecos", [128, TOK]), ("ropesin", [128, TOK]),
                      ("identf", [128, 128]), ("masks", [128, 4 * CH]),
                      ("kbias", [NTC, 36, 128, NH]), ("self4", [4, 384])]:
        EIN(nm, shape, F32)
    for nm, shape in [("rotT", [128, 128]), ("identb", [128, 128]),
                      ("onesb", [128, 128]), ("crow", [16, CH]),
                      ("ehmat", [128, NH * 128]), ("mdyn", [NTC, 8, 128, CH])]:
        EIN(nm, shape, BF16)
    EIN("kvoff", [NTC, 36, 2], I32)

    xout = nc.dram_tensor("xout", [TOK, D], F32, kind="ExternalOutput")

    DR = lambda name, shape, dt: nc.dram_tensor(name, shape, dt)
    scr = {}
    for nm, shape, dt in [
        ("aWqT", [D, D], BF16), ("aWkT", [512, D], BF16),
        ("aWvT", [512, D], BF16), ("aWoT", [D, D], BF16),
        ("bWqb", [D, D], BF16), ("bWkb", [D, D], BF16), ("bWvb", [D, D], BF16),
        ("bWob", [D, D], BF16),
        ("qw1", [6144, D], BF16), ("qw2", [6144, D], BF16), ("qw3", [6144, D], BF16),
        ("awq_in", [256, D], BF16), ("awk_in", [128, D], BF16),
        ("awv_in", [128, D], BF16), ("awo_in", [256, D], BF16),
        ("bwq_in", [256, D], BF16), ("bwk_in", [256, D], BF16),
        ("bwv_in", [256, D], BF16), ("bwo_in", [256, D], BF16),
        ("qw1_in", [768, D], BF16), ("qw2_in", [768, D], BF16),
        ("qw3_in", [768, D], BF16),
        ("gpart_in", [1, 4], F32), ("gpart_out", [8, 4], F32),
        ("h_scr", [NFC, 128, TOK], BF16),
        ("qa_scr", [D, TOK], BF16), ("qb_scr", [D, TOK], BF16),
        ("xspill", [NFC, 128, TOK], F32),
        ("kina", [512, TOK], BF16), ("vina", [TOK, 512], BF16),
        ("kga", [2 * 512, TOK], BF16), ("vga", [2 * TOK, 512], BF16),
        ("kinb", [D, TOK], BF16), ("vinb", [TOK, D], BF16),
        ("kgb", [2 * D, TOK], BF16), ("vgb", [2 * TOK, D], BF16),
    ]:
        scr[nm] = DR(nm, shape, dt)

    cb = {}
    for nm, shape, dt in [("rotT", [128, 128], BF16), ("identf", [128, 128], F32),
                          ("identb", [128, 128], BF16), ("onesb", [128, 128], BF16)]:
        cb[nm] = C.sb("c_" + nm, shape, dt)
    bias_sb = C.sb("bias_sb", [128, 4 * 16], F32)
    crow_sb = C.sb("crow_sb", [128, CH], BF16)

    arX = C.sb("arX", [128, NFC * TOK], F32)
    arXb = arX[:, :].bitcast(BF16)
    arA = C.sb("arA", [128, 8192], BF16)
    arAf = arA[:, :].bitcast(F32)
    arB = C.sb("arB", [128, 4096], BF16)
    arBf = arB[:, :].bitcast(F32)
    arC = C.sb("arC", [128, 24576], BF16)
    arCf = arC[:, :].bitcast(F32)
    psA = st.enter_context(nc.psum_tensor("psA", [128, 4096], F32))
    psAb = psA[:, :].bitcast(BF16)

    def bank(b, n=1):
        return psA[:, b * 512:(b + n) * 512]

    def bankb(b, n=1):
        return psAb[:, b * 1024:(b + n) * 1024]

    constv = C.sb("constv", [128, 4], F32)
    sCN = C.sem("cn")
    for i, val in enumerate((0.0, LN_EPS, ACONST)):
        C.op("vector", lambda e, i=i, val=val: e.memset(constv[:, i:i + 1], val),
             sem=(sCN if i == 2 else None), drain=(i == 2))
        nc.const_aps.register(F32, val, constv[:, i:i + 1])
    C.wait("scalar", sCN, 1)
    C.wait("tensor", sCN, 1)

    sLD = C.sem("ld")
    for nm in ("rotT", "identf", "identb", "onesb"):
        C.dma("gpsimd", cb[nm][:, :], ein[nm][:, :], sLD)
    sMS = C.sem("ms")
    vms = C.op("vector", lambda e: e.memset(crow_sb[:, :], 0.0), sem=sMS, drain=True)
    C.wait("gpsimd", sMS, vms)
    C.dma("gpsimd", crow_sb[0:16, :], ein["crow"][:, :], sLD)
    vLD = sLD.n

    S = dict(nc=nc, C=C, ein=ein, scr=scr, cb=cb, bias_sb=bias_sb,
             crow_sb=crow_sb, x_fm=arX, arX=arX, arXb=arXb, arA=arA, arAf=arAf,
             arB=arB, arBf=arBf, arC=arC, arCf=arCf, bank=bank, bankb=bankb,
             sLD=sLD, vLD=vLD, xin=xin, xout=xout)

    fence = phase_P0(S)
    fa, fb = phase_PW(S) if "PW" in phases else ([], [])
    fence_g = phase_PG(S) if "PG" in phases else []
    if "P1" in phases:
        fence = phase_P1(S, fence + fa)
    if "P2" in phases:
        fence = phase_P2(S, fence + fence_g, "f2")
    if "P3" in phases:
        fence = phase_P3(S, fence + fb)
    if "P4" in phases:
        fence = phase_P2(S, fence, "f4")
    phase_P9(S, fence)

    block = st.enter_context(nc.Block())

    def _mk(eng_name):
        def _run(e):
            for fn in C.q[eng_name]:
                fn(e)
        return _run

    block.tensor(_mk("tensor"))
    block.vector(_mk("vector"))
    block.scalar(_mk("scalar"))
    block.gpsimd(_mk("gpsimd"))
    block.sync(_mk("sync"))

    st.close()
    return nc


def phase_P0(S):
    C, bank = S["C"], S["bank"]
    x_fm, xin = S["x_fm"], S["xin"]
    identf = S["cb"]["identf"]
    sd = [C.sem(f"p0d{i}") for i in range(2)]
    sm = C.sem("p0m")
    sv = C.sem("p0v")
    xt = [S["arAf"][:, 0:2048], S["arAf"][:, 2048:4096]]
    C.wait("tensor", S["sLD"], S["vLD"])
    gi = 0
    for tt in range(16):
        bi = tt % 2
        buf = xt[bi]
        if tt >= 2:
            C.wait("sync", sm, 4 * (tt - 1))
        vland = C.dma("sync", buf, xin[ts(tt), :], sd[bi])
        C.wait("tensor", sd[bi], vland)
        for g in range(4):
            bk = bank(gi % 2)
            if gi >= 2:
                C.wait("tensor", sv, gi - 1)
            for f4 in range(4):
                fc = g * 4 + f4
                C.op("tensor", lambda e, bk=bk, buf=buf, fc=fc, f4=f4, identf=identf:
                     e.transpose(bk[:, ts(f4)], buf[:, ts(fc)], identf[:, :]),
                     sem=(sm if f4 == 3 else None))
            C.wait("vector", sm, sm.n)
            out_ap = x_fm[:, g * 4 * TOK:(g * 4 + 4) * TOK] \
                .rearrange("p (f l) -> p f l", f=4)[:, :, tt * 128:(tt + 1) * 128]
            in_ap = bk[:, :].rearrange("p (f l) -> p f l", f=4)
            C.op("vector", lambda e, o=out_ap, i=in_ap: e.tensor_copy(o, i),
                 sem=sv, drain=True)
            gi += 1
    return [(sv, sv.n)]


def phase_PW(S):
    C, ein, scr, cb, bankb = S["C"], S["ein"], S["scr"], S["cb"], S["bankb"]
    identb, identf = cb["identb"], cb["identf"]
    sdw = C.sem("pwd")
    smw = C.sem("pwm")
    svw = C.sem("pwv")
    sso = C.sem("pwo")
    sgw = C.sem("pwg")
    wb = S["arB"][:, 0:2048]
    tb = S["arB"][:, 2048:4096]
    bk = bankb(2)[:, 0:512]
    C.wait("tensor", S["sLD"], S["vLD"])

    def prep_tile(src_ap, dst_ap):
        vw = sso.n
        C.wait("gpsimd", smw, smw.n)
        vd = C.dma("gpsimd", wb, src_ap, sdw)
        C.wait("tensor", sdw, vd)
        for g in range(4):
            C.wait("tensor", svw, svw.n)
            for f4 in range(4):
                fc = g * 4 + f4
                C.op("tensor", lambda e, fc=fc, f4=f4:
                     e.transpose(bk[:, ts(f4)], wb[:, ts(fc)], identb[:, :]),
                     sem=(smw if f4 == 3 else None))
            C.wait("vector", smw, smw.n)
            if g == 0:
                C.wait("vector", sso, vw)
            C.op("vector", lambda e, g=g:
                 e.tensor_copy(tb[:, g * 512:(g + 1) * 512], bk), sem=svw, drain=True)
        C.wait("sync", svw, svw.n)
        return C.dma("sync", dst_ap, tb, sso)

    for nm, inn, agout, ntile, groups in [
        ("aWk", "awk_in", "aWkT", 1, G4), ("aWv", "awv_in", "aWvT", 1, G4),
        ("aWq", "awq_in", "aWqT", 2, G8), ("aWo", "awo_in", "aWoT", 2, G8),
    ]:
        for ot in range(ntile):
            vst = prep_tile(ein[nm][ts(ot), :], scr[inn][ts(ot), :])
        C.wait("gpsimd", sso, vst)
        _ag(C, scr[inn].ap(), scr[agout].ap(), groups, sgw)
    for nm, inn, agout in [("bWq", "bwq_in", "bWqb"), ("bWk", "bwk_in", "bWkb"),
                           ("bWv", "bwv_in", "bWvb"), ("bWo", "bwo_in", "bWob")]:
        vc = C.dma("gpsimd", scr[inn][:, :], ein[nm][:, :], sdw)
        C.wait("gpsimd", sdw, vc)
        _ag(C, scr[inn].ap(), scr[agout].ap(), G8, sgw)
    sbb = C.sem("pwb")
    ssb = C.sem("pwbs")
    bbs = S["arBf"][0:16, 0:128]
    pb = S["bank"](2)[:, 0:16]
    for wi, nm in enumerate(("bbq", "bbk", "bbv", "bbo")):
        C.wait("sync", smw, smw.n)
        vb = C.dma("sync", bbs, ein[nm][:, :], sbb)
        C.wait("tensor", sbb, vb)
        C.wait("tensor", ssb, ssb.n)
        C.op("tensor", lambda e: e.transpose(pb, bbs, identf[0:16, 0:16]), sem=smw)
        C.wait("scalar", smw, smw.n)
        C.op("scalar", lambda e, wi=wi:
             e.copy(S["bias_sb"][:, wi * 16:(wi + 1) * 16], pb), sem=ssb, drain=True)
    fa = [(sgw, sgw.n)]
    fb = [(sgw, sgw.n), (ssb, ssb.n)]
    S["pw_done"] = [(sgw, sgw.n), (ssb, ssb.n), (svw, svw.n), (smw, smw.n)]
    return fa, fb


def phase_PG(S):
    C, ein, scr, cb = S["C"], S["ein"], S["scr"], S["cb"]
    identb, identf = cb["identb"], cb["identf"]
    bank, bankb = S["bank"], S["bankb"]
    sdq = C.sem("pgd")
    ssc = C.sem("pgs")
    svq = C.sem("pgv")
    sgp = C.sem("pgp")
    smq = C.sem("pgm")
    sgg = C.sem("pgg")
    sst = C.sem("pgt")
    wf = S["arCf"][:, 0:2048]
    mp = S["arC"][:, 4096:6144]
    mn = S["arC"][:, 6144:8192]
    qt = S["arC"][:, 8192:10240]
    tb = S["arC"][:, 10240:12288]
    acc = C.sb("pg_acc", [128, 32], F32)
    red = C.sb("pg_red", [128, 4], F32)
    gp_sb = C.sb("pg_gp", [8, 4], F32)
    tot = C.sb("pg_tot", [4, 1], F32)
    thr4 = C.sb("pg_thr4", [4, 1], F32)
    nthr4 = C.sb("pg_nthr4", [4, 1], F32)
    thr_sb = C.sb("pg_thr", [128, 8], F32)
    self4 = C.sb("pg_self4", [4, 384], F32)
    bk = bankb(6)[:, 0:512]
    pr = bank(7)[:, 0:128]
    pth = bank(7)[:, 128:129]
    C.dma("gpsimd", self4[:, :], ein["self4"][:, :], S["sLD"])
    S["vLD"] = S["sLD"].n

    tiles = [("fw1", ein["fw1"][ts(t), :], D) for t in range(6)] + \
            [("fw2", ein["fw2"][ts(t), :], D) for t in range(6)] + \
            [("fw3", ein["fw3"][ts(t), :], 768) for t in range(16)]
    slot = 0
    for nm, ap, fd in tiles:
        vd = C.dma("sync", wf[:, 0:fd], ap, sdq)
        C.wait("scalar", sdq, vd)
        C.op("scalar", lambda e, fd=fd, slot=slot:
             e.activation(mp[:, 0:fd], wf[:, 0:fd], AF.Abs,
                          accum_out=acc[:, slot:slot + 1]), sem=ssc, drain=True)
        C.wait("sync", ssc, ssc.n)
        slot += 1
    C.wait("vector", ssc, ssc.n)
    C.op("vector", lambda e: e.tensor_reduce(red[:, 0:1], acc[:, 0:6], AX.X, ALU.add))
    C.op("vector", lambda e: e.tensor_reduce(red[:, 1:2], acc[:, 6:12], AX.X, ALU.add))
    C.op("vector", lambda e: e.tensor_reduce(red[:, 2:3], acc[:, 12:28], AX.X, ALU.add))
    C.op("vector", lambda e: e.memset(red[:, 3:4], 0.0), sem=svq, drain=True)
    C.wait("tensor", svq, svq.n)
    C.wait("tensor", S["sLD"], S["vLD"])
    C.op("tensor", lambda e: e.transpose(pr[0:4, :], red[:, :], identf[:, :]), sem=smq)
    C.wait("vector", smq, smq.n)
    C.op("vector", lambda e: e.tensor_reduce(red[0:4, 0:1], pr[0:4, :], AX.X, ALU.add),
         sem=svq, drain=True)
    C.wait("sync", svq, svq.n)
    vst = C.dma("sync", scr["gpart_in"][0:1, 0:4], red[0:4, 0:1], sst)
    C.wait("gpsimd", sst, vst)
    vgg = _ag(C, scr["gpart_in"].ap(), scr["gpart_out"].ap(), G8, sgg)
    C.wait("gpsimd", sgg, vgg)
    sgl = C.sem("pgl")
    vgl = C.dma("gpsimd", gp_sb[:, :], scr["gpart_out"][:, :], sgl)
    C.wait("tensor", sgl, vgl)
    C.wait("tensor", svq, svq.n)
    C.op("tensor", lambda e: e.transpose(pr[0:4, 0:8], gp_sb[:, :], identf[0:8, 0:8]),
         sem=smq)
    C.wait("vector", smq, smq.n)
    C.op("vector", lambda e: e.tensor_reduce(tot[:, 0:1], pr[0:4, 0:8], AX.X, ALU.add))
    C.op("vector", lambda e: e.tensor_scalar(thr4[:, :], tot[:, :],
                                             1.0 / (2.0 * HID * D), 0.5 * Q_EPS,
                                             ALU.mult, ALU.add))
    C.op("vector", lambda e: e.tensor_scalar(nthr4[:, :], thr4[:, :], -1.0, None,
                                             ALU.mult), sem=svq, drain=True)
    C.wait("tensor", svq, svq.n)
    for wi in range(3):
        for neg in range(2):
            src = nthr4 if neg else thr4
            C.op("tensor", lambda e, wi=wi, src=src:
                 e.matmul(pth, self4[:, wi * 128:(wi + 1) * 128], src[:, :],
                          start=True, stop=True), sem=smq)
            C.wait("scalar", smq, smq.n)
            col = 3 * neg + wi
            C.op("scalar", lambda e, col=col:
                 e.copy(thr_sb[:, col:col + 1], pth), sem=ssc, drain=True)
            C.wait("tensor", ssc, ssc.n)
    C.wait("vector", ssc, ssc.n)

    def quant_tile(src_ap, fd, wi):
        C.wait("sync", svq, svq.n)
        vd = C.dma("sync", wf[:, 0:fd], src_ap, sdq)
        C.wait("vector", sdq, vd)
        C.wait("vector", sgp, sgp.n)
        C.op("vector", lambda e, fd=fd, wi=wi:
             e.tensor_scalar(mp[:, 0:fd], wf[:, 0:fd], thr_sb[:, wi:wi + 1], None,
                             ALU.is_ge))
        C.op("vector", lambda e, fd=fd, wi=wi:
             e.tensor_scalar(mn[:, 0:fd], wf[:, 0:fd], thr_sb[:, 3 + wi:4 + wi], None,
                             ALU.is_le), sem=svq, drain=True)
        C.wait("gpsimd", svq, svq.n)
        C.wait("gpsimd", smq, smq.n)
        C.op("gpsimd", lambda e, fd=fd:
             e.tensor_tensor(qt[:, 0:fd], mp[:, 0:fd], mn[:, 0:fd], ALU.subtract),
             sem=sgp, drain=True)
        return sgp.n

    for wi, nm, inn in [(0, "fw1", "qw1_in"), (1, "fw2", "qw2_in")]:
        for t in range(6):
            vq = quant_tile(ein[nm][ts(t), :], D, wi)
            C.wait("tensor", sgp, vq)
            for g in range(4):
                C.wait("tensor", svq, svq.n)
                C.wait("tensor", sst, sst.n)
                for f4 in range(4):
                    fc = g * 4 + f4
                    C.op("tensor", lambda e, fc=fc, f4=f4:
                         e.transpose(bk[:, ts(f4)], qt[:, ts(fc)], identb[:, :]),
                         sem=(smq if f4 == 3 else None))
                C.wait("vector", smq, smq.n)
                C.op("vector", lambda e, g=g:
                     e.tensor_copy(tb[:, g * 512:(g + 1) * 512], bk),
                     sem=svq, drain=True)
            C.wait("sync", svq, svq.n)
            C.dma("sync", scr[inn][ts(t), :], tb, sst)
        C.wait("gpsimd", sst, sst.n)
        _ag(C, scr[inn].ap(), scr["qw" + str(wi + 1)].ap(), G8, sgg)
    for oc in range(16):
        vq = quant_tile(ein["fw3"][ts(oc), :], 768, 2)
        C.wait("tensor", sgp, vq)
        for g in range(2):
            nm_ = 4 if g == 0 else 2
            C.wait("tensor", svq, svq.n)
            C.wait("tensor", sst, sst.n)
            for m in range(nm_):
                hcl = g * 4 + m
                C.op("tensor", lambda e, hcl=hcl, m=m:
                     e.transpose(bk[:, ts(m)], qt[:, ts(hcl)], identb[:, :]),
                     sem=(smq if m == nm_ - 1 else None))
            C.wait("vector", smq, smq.n)
            C.op("vector", lambda e, g=g, nm_=nm_:
                 e.tensor_copy(tb[:, g * 512:g * 512 + nm_ * 128],
                               bk[:, 0:nm_ * 128]), sem=svq, drain=True)
        C.wait("sync", svq, svq.n)
        for hcl in range(6):
            g, m = divmod(hcl, 4)
            C.dma("sync", scr["qw3_in"][hcl * 128:(hcl + 1) * 128, ts(oc)],
                  tb[:, g * 512 + m * 128:g * 512 + (m + 1) * 128], sst)
    C.wait("gpsimd", sst, sst.n)
    vlast = _ag(C, scr["qw3_in"].ap(), scr["qw3"].ap(), G8, sgg)
    return [(sgg, vlast)]


def emit_ln(S, sems, tc, hname):
    C, bank, x_fm = S["C"], S["bank"], S["x_fm"]
    onesb = S["cb"]["onesb"]
    sm, sv, ss, sg = sems["m"], sems["v"], sems["s"], sems["g"]
    so2 = sems["o2"]
    xb2 = S["arB"][:, 0:1024]
    sq2 = S["arB"][:, 1024:2048]
    hb2 = S["arB"][:, 2048:3072]
    m_sb = S["arBf"][:, 1536:2048]
    rstd = S["arCf"][:, 3072:3584]
    ms_sb = S["arCf"][:, 3584:4096]
    sx, sq = bank(0), bank(1)
    C.wait("tensor", sv, sv.n)
    for fc in range(NFC):
        xs = x_fm[:, fc * TOK + tc * CH: fc * TOK + (tc + 1) * CH]
        sl = fc % 2
        C.wait("scalar", sm, sm.n)
        C.op("scalar", lambda e, xs=xs, sl=sl:
             e.copy(xb2[:, sl * 512:(sl + 1) * 512], xs), sem=ss, drain=True)
        C.wait("gpsimd", sm, sm.n)
        C.op("gpsimd", lambda e, xs=xs, sl=sl:
             e.tensor_tensor(sq2[:, sl * 512:(sl + 1) * 512], xs, xs, ALU.mult),
             sem=sg, drain=True)
        C.wait("tensor", ss, ss.n)
        C.op("tensor", lambda e, sl=sl, fc=fc:
             e.matmul(sx, onesb[:, :], xb2[:, sl * 512:(sl + 1) * 512],
                      start=(fc == 0), stop=(fc == NFC - 1)), sem=sm)
        C.wait("tensor", sg, sg.n)
        C.op("tensor", lambda e, sl=sl, fc=fc:
             e.matmul(sq, onesb[:, :], sq2[:, sl * 512:(sl + 1) * 512],
                      start=(fc == 0), stop=(fc == NFC - 1)), sem=sm)
    C.wait("vector", sm, sm.n)
    C.op("vector", lambda e: e.tensor_scalar(m_sb, sx, 1.0 / D, None, ALU.mult))
    C.op("vector", lambda e: e.tensor_tensor(ms_sb, m_sb, m_sb, ALU.mult))
    C.op("vector", lambda e: e.scalar_tensor_tensor(rstd, sq, 1.0 / D, ms_sb,
                                                    ALU.mult, ALU.subtract),
         sem=sv, drain=True)
    C.wait("scalar", sv, sv.n)
    C.op("scalar", lambda e: e.activation(rstd, rstd, AF.Ln, bias=LN_EPS, scale=1.0))
    C.op("scalar", lambda e: e.activation(rstd, rstd, AF.Exp, scale=-0.5),
         sem=ss, drain=True)
    C.wait("vector", ss, ss.n)
    for fc in range(NFC):
        xs = x_fm[:, fc * TOK + tc * CH: fc * TOK + (tc + 1) * CH]
        sl = fc % 2
        C.wait("vector", so2[sl], so2[sl].n)
        C.op("vector", lambda e, xs=xs, sl=sl:
             e.tensor_tensor(hb2[:, sl * 512:(sl + 1) * 512], xs, m_sb, ALU.subtract))
        C.op("vector", lambda e, sl=sl:
             e.tensor_tensor(hb2[:, sl * 512:(sl + 1) * 512],
                             hb2[:, sl * 512:(sl + 1) * 512], rstd, ALU.mult),
             sem=sv, drain=True)
        C.wait("sync", sv, sv.n)
        C.dma("sync", S["scr"][hname][fc, :, tc * CH:(tc + 1) * CH],
              hb2[:, sl * 512:(sl + 1) * 512], so2[sl])


def phase_P1(S, fence):
    C, bank, bankb = S["C"], S["bank"], S["bankb"]
    scr, ein, cb = S["scr"], S["ein"], S["cb"]
    x_fm, arXb = S["x_fm"], S["arXb"]
    onesb, rotT, identb = cb["onesb"], cb["rotT"], cb["identb"]
    sm = C.sem("p1m")
    sv = C.sem("p1v")
    ss = C.sem("p1s")
    sg = C.sem("p1g")
    so2 = [C.sem("p1o0"), C.sem("p1o1")]
    sp = C.sem("p1p")
    sld = [C.sem("p1l0"), C.sem("p1l1")]
    sq_ = C.sem("p1q")
    scc = C.sem("p1cc")
    sko2 = [C.sem("p1ko0"), C.sem("p1ko1")]
    skv = [C.sem("p1kv0"), C.sem("p1kv1")]
    sxs = C.sem("p1xs")
    shl = C.sem("p1hl")
    gd = Guard(C)

    C.barrier(fence, engines=("tensor", "vector", "scalar", "gpsimd", "sync"))
    lnsems = dict(m=sm, v=sv, s=ss, g=sg, o2=so2)
    for tc in range(NTC):
        emit_ln(S, lnsems, tc, "h_scr")
    C.wait("scalar", sv, sv.n)
    C.wait("scalar", sg, sg.n)
    for fc in range(NFC):
        C.dma("scalar", scr["xspill"][fc, :, :], x_fm[:, fc * TOK:(fc + 1) * TOK], sp)
    h_all = arXb[:, 0:NFC * TOK]
    cosb = S["arX"][:, 16384:18432]
    sinb = S["arX"][:, 18432:20480]
    t1 = S["arX"][:, 20480:20992]
    t2 = S["arX"][:, 20992:21504]
    outb = arXb[:, 45056:47104]
    wbuf = [arXb[:, 47104 + i * 2048: 47104 + (i + 1) * 2048] for i in range(2)]
    vtb = arXb[:, 51200:53248]
    C.wait("sync", sp, sp.n)
    C.wait("sync", so2[0], so2[0].n)
    C.wait("sync", so2[1], so2[1].n)
    for fc in range(NFC):
        C.dma("sync", h_all[:, fc * TOK:(fc + 1) * TOK], scr["h_scr"][fc, :, :], shl)
    C.dma("sync", cosb, ein["ropecos"][:, :], shl)
    C.dma("sync", sinb, ein["ropesin"][:, :], shl)
    C.barrier([(shl, shl.n)], engines=("tensor", "vector", "scalar"))

    ldtgt = {}

    def load_w(src_rows, li):
        sl = li % 2
        gd.acquire("sync", ("wbuf", sl))
        ldtgt[sl] = C.dma("sync", wbuf[sl], src_rows, sld[sl])

    def proj_psum(li, tc, pb_i):
        w = wbuf[li % 2]
        C.wait("tensor", sld[li % 2], ldtgt[li % 2])
        gd.acquire("tensor", ("bank", pb_i))
        for fc in range(NFC):
            C.op("tensor", lambda e, w=w, fc=fc, tc=tc, pb_i=pb_i:
                 e.matmul(bank(pb_i), w[:, ts(fc)],
                          h_all[:, fc * TOK + tc * CH: fc * TOK + (tc + 1) * CH],
                          start=(fc == 0), stop=(fc == NFC - 1)),
                 sem=(sm if fc == NFC - 1 else None))
        gd.use(("wbuf", li % 2), sm)

    def rope_unit(pb_i, rb_i, tc, sl):
        dst_bf = outb[:, sl * 512:(sl + 1) * 512]
        C.wait("scalar", sm, sm.n)
        gd.acquire("scalar", ("outb", sl))
        C.op("scalar", lambda e, pb_i=pb_i, dst_bf=dst_bf:
             e.copy(dst_bf, bank(pb_i)), sem=ss)
        C.wait("tensor", ss, ss.n)
        gd.acquire("tensor", ("bank", rb_i))
        C.op("tensor", lambda e, rb_i=rb_i, dst_bf=dst_bf:
             e.matmul(bank(rb_i), rotT[:, :], dst_bf, start=True, stop=True), sem=sm)
        C.wait("vector", sm, sm.n)
        cs = cosb[:, tc * CH:(tc + 1) * CH]
        sn = sinb[:, tc * CH:(tc + 1) * CH]
        C.op("vector", lambda e, pb_i=pb_i, cs=cs:
             e.tensor_tensor(t1, bank(pb_i), cs, ALU.mult))
        C.op("vector", lambda e, rb_i=rb_i, sn=sn:
             e.tensor_tensor(t2, bank(rb_i), sn, ALU.mult))
        C.op("vector", lambda e, dst_bf=dst_bf:
             e.tensor_tensor(dst_bf, t1, t2, ALU.add), sem=sv)
        gd.use(("bank", pb_i), ss)
        gd.use(("bank", pb_i), sv)
        gd.use(("bank", rb_i), sv)
        return dst_bf

    def store_out(dst_ap, dst_bf, sl):
        C.wait("scalar", sv, sv.n)
        C.dma("scalar", dst_ap, dst_bf, sko2[sl])
        gd.use(("outb", sl), sko2[sl])

    li = 0
    load_w(scr["aWkT"][ts(0), :], 0)
    for dc in range(4):
        if dc + 1 < 4:
            load_w(scr["aWkT"][ts(dc + 1), :], li + 1)
        for tc in range(NTC):
            proj_psum(li, tc, 2 + (tc % 2))
            dst = rope_unit(2 + (tc % 2), 4 + (tc % 2), tc, tc % 2)
            store_out(scr["kina"][ts(dc), tc * CH:(tc + 1) * CH], dst, tc % 2)
        li += 1
    for tc in range(NTC):
        load_w(scr["aWvT"][ts(0), :], li)
        for dc in range(4):
            if dc + 1 < 4:
                load_w(scr["aWvT"][ts(dc + 1), :], li + 1)
            pb_i = 2 + (dc % 2)
            proj_psum(li, tc, pb_i)
            C.wait("scalar", sm, sm.n)
            gd.acquire("scalar", ("vtb", dc))
            C.op("scalar", lambda e, pb_i=pb_i, dc=dc:
                 e.copy(vtb[:, dc * 512:(dc + 1) * 512], bank(pb_i)), sem=ss)
            gd.use(("bank", pb_i), ss)
            li += 1
        C.wait("tensor", ss, ss.n)
        for tt in range(4):
            tb_i = 4 + (tt % 2)
            gd.acquire("tensor", ("bank", tb_i))
            for dc in range(4):
                C.op("tensor", lambda e, tb_i=tb_i, dc=dc, tt=tt:
                     e.transpose(bankb(tb_i)[:, 0:512][:, ts(dc)],
                                 vtb[:, dc * 512 + tt * 128: dc * 512 + (tt + 1) * 128],
                                 identb[:, :]),
                     sem=(sm if dc == 3 else None))
                gd.use(("vtb", dc), sm)
            C.wait("vector", sm, sm.n)
            sl = tt % 2
            vout = outb[:, sl * 512:(sl + 1) * 512]
            gd.acquire("vector", ("outb", sl))
            C.op("vector", lambda e, tb_i=tb_i, vout=vout:
                 e.tensor_copy(vout, bankb(tb_i)[:, 0:512]), sem=sv)
            gd.use(("bank", tb_i), sv)
            store_out(scr["vina"][tc * CH + tt * 128: tc * CH + (tt + 1) * 128, :],
                      vout, sl)
    load_w(scr["aWqT"][ts(0), :], li)
    for oc in range(NFC):
        if oc + 1 < NFC:
            load_w(scr["aWqT"][ts(oc + 1), :], li + 1)
        for tc in range(NTC):
            proj_psum(li, tc, 2 + (tc % 2))
            dst = rope_unit(2 + (tc % 2), 4 + (tc % 2), tc, tc % 2)
            store_out(scr["qa_scr"][ts(oc), tc * CH:(tc + 1) * CH], dst, tc % 2)
        li += 1
    C.wait("gpsimd", sko2[0], sko2[0].n)
    C.wait("gpsimd", sko2[1], sko2[1].n)
    _ag(C, scr["kina"].ap(), scr["kga"].ap(), G2, scc)
    _ag(C, scr["vina"].ap(), scr["vga"].ap(), G2, scc)
    C.barrier([(scc, 2), (sm, sm.n), (sv, sv.n), (ss, ss.n)],
              engines=("tensor", "scalar", "vector", "sync"))

    kkb = [arXb[:, i * 4096: (i + 1) * 4096] for i in range(2)]
    vvb = [arXb[:, 8192 + i * 4096: 8192 + (i + 1) * 4096] for i in range(2)]
    qsb = arXb[:, 16384:24576]
    o_all = arXb[:, 24576:32768]
    p_sb = [arXb[:, 32768 + i * 1024: 32768 + (i + 1) * 1024] for i in range(4)]
    rec_sb = S["arX"][:, 18432:18944]
    lnt = S["arX"][:, 18944:19456]
    xcur = S["arX"][:, 19456:27648]
    wov = [arXb[:, 55296 + i * 2048: 55296 + (i + 1) * 2048] for i in range(2)]

    def load_kv(kvh, slot):
        sem = skv[slot]
        gd.acquire("sync", ("kv", slot))
        for r in range(2):
            C.dma("sync", kkb[slot][:, r * 2048:(r + 1) * 2048],
                  scr["kga"][r * 512 + kvh * 128: r * 512 + (kvh + 1) * 128, :], sem)
        for r in range(2):
            for hh in range(2):
                C.dma("sync",
                      vvb[slot][:, (r * 16 + hh * 8) * 128:
                                (r * 16 + (hh + 1) * 8) * 128]
                      .rearrange("p (a d) -> p a d", a=8),
                      scr["vga"][r * TOK + hh * 1024: r * TOK + (hh + 1) * 1024,
                                 kvh * 128:(kvh + 1) * 128]
                      .rearrange("(a p) d -> p a d", p=128), sem)
        return sem.n

    recp = [rec_sb, S["arX"][:, 29696:30208]]
    lntp = [lnt, S["arX"][:, 30208:30720]]
    sqp = C.sem("p1qp")
    kvt = [None, None]
    kvt[0] = load_kv(0, 0)
    vq_t = None
    for qc in range(NTC):
        if qc == 0:
            gd.acquire("sync", ("qsb",))
            vq_t = C.dma("sync", qsb.rearrange("p (oc t) -> p oc t", oc=16),
                         scr["qa_scr"].ap().rearrange("(oc p) t -> p oc t", p=128)
                         [:, :, qc * CH:(qc + 1) * CH], sqp)
        C.barrier([(sqp, vq_t)], engines=("tensor",))
        units = [(h, kp, h // 4) for h in range(16) for kp in range(16)]
        ss_t = {}

        def emit_qk(ui, h, kp, kvh, qc=qc):
            qh = qsb[:, h * 512:(h + 1) * 512]
            sti = 2 * (ui % 2)
            st2 = bank(sti, 2)
            gd.acquire("tensor", ("st", sti))
            for j in range(2):
                kt = 2 * kp + j
                C.op("tensor", lambda e, st2=st2, j=j, kt=kt, kvh=kvh, qh=qh:
                     e.matmul(st2[:, j * 512:(j + 1) * 512],
                              kkb[kvh % 2][:, kt * 128:(kt + 1) * 128],
                              qh, start=True, stop=True),
                     sem=(sm if j == 1 else None))
            C.wait("scalar", sm, sm.n)
            gd.acquire("scalar", ("p", ui % 4))
            ss_t[ui] = C.op("scalar", lambda e, st2=st2, ui=ui:
                            e.activation(p_sb[ui % 4], st2, AF.Exp, bias=ACONST,
                                         scale=ISQD), sem=ss)
            gd.use(("st", sti), ss)

        def emit_pv(ui, h, kp, kvh, qc=qc):
            pb2 = p_sb[ui % 4]
            ob_, dnb = 4 + 2 * (h % 2), 5 + 2 * (h % 2)
            if kp == 0:
                gd.acquire("tensor", ("bank", ob_))
                gd.acquire("tensor", ("bank", dnb))
            C.wait("tensor", ss, ss_t[ui])
            for j in range(2):
                kt = 2 * kp + j
                C.op("tensor", lambda e, pb2=pb2, j=j, kt=kt, kvh=kvh, ob_=ob_:
                     e.matmul(bank(ob_), vvb[kvh % 2][:, kt * 128:(kt + 1) * 128],
                              pb2[:, j * 512:(j + 1) * 512],
                              start=(kt == 0), stop=(kt == 31)))
                C.op("tensor", lambda e, pb2=pb2, j=j, kt=kt, dnb=dnb:
                     e.matmul(bank(dnb), onesb[:, :],
                              pb2[:, j * 512:(j + 1) * 512],
                              start=(kt == 0), stop=(kt == 31)),
                     sem=(sm if j == 1 else None))
            gd.use(("p", ui % 4), sm)
            if kp == 15:
                gd.use(("kv", kvh % 2), sm)
                rc, lt = recp[h % 2], lntp[h % 2]
                C.wait("scalar", sm, sm.n)
                gd.acquire("scalar", ("rec", h % 2))
                C.op("scalar", lambda e, dnb=dnb, lt=lt:
                     e.activation(lt, bank(dnb), AF.Ln))
                t_rec = C.op("scalar", lambda e, rc=rc, lt=lt:
                             e.activation(rc, lt, AF.Exp, scale=-1.0), sem=ss)
                gd.use(("bank", dnb), ss)
                C.wait("vector", ss, t_rec)
                oh = o_all[:, h * 512:(h + 1) * 512]
                gd.acquire("vector", ("oall", h))
                C.op("vector", lambda e, oh=oh, ob_=ob_, rc=rc:
                     e.tensor_tensor(oh, bank(ob_), rc, ALU.mult), sem=sv)
                gd.use(("bank", ob_), sv)
                gd.use(("rec", h % 2), sv)

        for ui, (h, kp, kvh) in enumerate(units):
            if kp == 0 and h % 4 == 0:
                C.wait("tensor", skv[kvh % 2], kvt[kvh % 2])
            elif kp == 2 and h % 4 == 0:
                if kvh + 1 < 4:
                    kvt[(kvh + 1) % 2] = load_kv(kvh + 1, (kvh + 1) % 2)
                elif qc + 1 < NTC:
                    kvt[0] = load_kv(0, 0)
            emit_qk(ui, h, kp, kvh)
            if ui >= 2:
                emit_pv(ui - 2, *units[ui - 2])
        gd.use(("qsb",), sm)
        for ui in range(len(units) - 2, len(units)):
            emit_pv(ui, *units[ui])
        if qc + 1 < NTC:
            gd.acquire("gpsimd", ("qsb",))
            vq_t = C.dma("gpsimd", qsb.rearrange("p (oc t) -> p oc t", oc=16),
                         scr["qa_scr"].ap().rearrange("(oc p) t -> p oc t", p=128)
                         [:, :, (qc + 1) * CH:(qc + 2) * CH], sqp)
        C.wait("tensor", sv, sv.n)
        C.wait("sync", sxs, sxs.n)
        gd.acquire("sync", ("xcur",))
        vx = C.dma("sync", xcur.rearrange("p (oc t) -> p oc t", oc=16),
                   scr["xspill"].ap().rearrange("oc p t -> p oc t")
                   [:, :, qc * CH:(qc + 1) * CH], sq_)
        C.barrier([(sq_, vx)], engines=("vector",))
        def wov_load(oc):
            sl = oc % 2
            gd.acquire("sync", ("wov", sl))
            C.dma("sync", wov[sl], scr["aWoT"][ts(oc), :], sld[sl])
            return sld[sl].n

        wovt = [None, None]
        wovt[0] = wov_load(0)
        for oc in range(NFC):
            sl = oc % 2
            if oc + 1 < NFC:
                wovt[(oc + 1) % 2] = wov_load(oc + 1)
            C.wait("tensor", sld[sl], wovt[sl])
            pb_i = 2 + sl
            gd.acquire("tensor", ("bank", pb_i))
            for dc in range(NFC):
                C.op("tensor", lambda e, pb_i=pb_i, sl=sl, dc=dc:
                     e.matmul(bank(pb_i), wov[sl][:, ts(dc)],
                              o_all[:, dc * 512:(dc + 1) * 512],
                              start=(dc == 0), stop=(dc == NFC - 1)),
                     sem=(sm if dc == NFC - 1 else None))
            gd.use(("wov", sl), sm)
            if oc == NFC - 1:
                for hh in range(NH):
                    gd.use(("oall", hh), sm)
            C.wait("vector", sm, sm.n)
            xslice = xcur[:, oc * 512:(oc + 1) * 512]
            C.op("vector", lambda e, pb_i=pb_i, xslice=xslice:
                 e.tensor_tensor(xslice, bank(pb_i), xslice, ALU.add), sem=sv)
            gd.use(("bank", pb_i), sv)
            gd.use(("st", 2), sv)
        C.wait("sync", sv, sv.n)
        for oc in range(NFC):
            C.dma("sync", scr["xspill"][oc, :, qc * CH:(qc + 1) * CH],
                  xcur[:, oc * 512:(oc + 1) * 512], sxs)
        gd.use(("xcur",), sxs)
    sxr = C.sem("p1xr")
    C.wait("sync", sxs, sxs.n)
    for fc in range(NFC):
        C.dma("sync", S["x_fm"][:, fc * TOK:(fc + 1) * TOK], scr["xspill"][fc, :, :],
              sxr)
    return [(sxr, sxr.n), (sm, sm.n), (sv, sv.n), (ss, ss.n)]


def phase_P2(S, fence, tag):
    C, bank, arA, arB, arC = S["C"], S["bank"], S["arA"], S["arB"], S["arC"]
    scr = S["scr"]
    sm = C.sem(tag + "m")
    sv = C.sem(tag + "v")
    ss = C.sem(tag + "s")
    sg = C.sem(tag + "g")
    so2 = [C.sem(tag + "o0"), C.sem(tag + "o1")]
    swq = C.sem(tag + "wq")
    sw3 = [C.sem(tag + "w30"), C.sem(tag + "w31")]
    sh2 = [C.sem(tag + "h0"), C.sem(tag + "h1")]
    gd = Guard(C)
    C.barrier(fence, engines=("tensor", "vector", "scalar", "gpsimd", "sync"))
    lnsems = dict(m=sm, v=sv, s=ss, g=sg, o2=so2)
    for tc in range(NTC):
        emit_ln(S, lnsems, tc, "h_scr")
    C.wait("sync", so2[0], so2[0].n)
    C.wait("sync", so2[1], so2[1].n)
    qw1b = arC[:, 0:8192]
    qw2b = arC[:, 8192:16384]
    hid = arC[:, 16384:24576]
    h2b = [arA[:, 0:4096], arA[:, 4096:8192]]
    s1b = [arB[:, i * 512:(i + 1) * 512] for i in range(2)]
    qw3b = [arB[:, 1024 + i * 512: 1024 + (i + 1) * 512] for i in range(2)]
    h_scr = scr["h_scr"]
    def load_qw12(g, eng):
        gd.acquire(eng, ("qw12",))
        for hcl, hc in enumerate([4 * g + i for i in range(4)]):
            C.dma(eng, qw1b[:, hcl * 2048:(hcl + 1) * 2048],
                  scr["qw1"][ts(hc), :], swq)
            C.dma(eng, qw2b[:, hcl * 2048:(hcl + 1) * 2048],
                  scr["qw2"][ts(hc), :], swq)
        return swq.n

    qwtgt = load_qw12(0, "sync")
    for g in range(11):
        C.wait("tensor", swq, qwtgt)
        for tc in range(NTC):
            qengs = ("sync", "scalar", "gpsimd", "sync")
            for q in range(4):
                gd.acquire(qengs[q], ("h2", q))
                C.dma(qengs[q],
                      h2b[q // 2].rearrange("p (f j) -> p f j", f=8)
                      [:, (q % 2) * 4:(q % 2) * 4 + 4, :],
                      h_scr.ap()[q * 4:(q + 1) * 4, :,
                                 tc * CH:(tc + 1) * CH]
                      .rearrange("f p j -> p f j"), sh2[q % 2])
            C.wait("tensor", sh2[0], sh2[0].n)
            C.wait("tensor", sh2[1], sh2[1].n)
            for hcl in range(4):
                b1 = (hcl % 2) * 2
                b2 = (hcl % 2) * 2 + 1
                gd.acquire("tensor", ("bank", b1))
                for fc in range(NFC):
                    C.op("tensor", lambda e, hcl=hcl, fc=fc, b1=b1:
                         e.matmul(bank(b1), qw1b[:, hcl * 2048 + fc * 128:
                                               hcl * 2048 + (fc + 1) * 128],
                                  h2b[fc // 8][:, (fc % 8) * 512:(fc % 8 + 1) * 512],
                                  start=(fc == 0), stop=(fc == NFC - 1)),
                         sem=(sm if fc == NFC - 1 else None))
                C.wait("scalar", sm, sm.n)
                gd.acquire("scalar", ("s1", hcl % 2))
                C.op("scalar", lambda e, hcl=hcl, b1=b1:
                     e.activation(s1b[hcl % 2], bank(b1), AF.Sigmoid), sem=ss)
                gd.use(("bank", b1), ss)
                gd.acquire("tensor", ("bank", b2))
                for fc in range(NFC):
                    C.op("tensor", lambda e, hcl=hcl, fc=fc, b2=b2:
                         e.matmul(bank(b2), qw2b[:, hcl * 2048 + fc * 128:
                                               hcl * 2048 + (fc + 1) * 128],
                                  h2b[fc // 8][:, (fc % 8) * 512:(fc % 8 + 1) * 512],
                                  start=(fc == 0), stop=(fc == NFC - 1)),
                         sem=(sm if fc == NFC - 1 else None))
                C.wait("vector", sm, sm.n)
                C.wait("vector", ss, ss.n)
                gd.acquire("vector", ("hid",))
                C.op("vector", lambda e, hcl=hcl, tc=tc, b2=b2:
                     e.tensor_tensor(hid[:, (hcl * 4 + tc) * 512:
                                         (hcl * 4 + tc + 1) * 512],
                                     s1b[hcl % 2], bank(b2), ALU.mult), sem=sv)
                gd.use(("bank", b2), sv)
                gd.use(("s1", hcl % 2), sv)
            for q in range(4):
                gd.use(("h2", q), sm)
        gd.use(("qw12",), sm)
        if g + 1 < 11:
            qwtgt = load_qw12(g + 1, "scalar")
        C.wait("tensor", sv, sv.n)
        def qw3_load(oc):
            sl = oc % 2
            gd.acquire("sync", ("qw3", sl))
            C.dma("sync",
                  qw3b[sl].rearrange("p (h i) -> p h i", h=4),
                  scr["qw3"].ap().rearrange("(h p) o -> h p o", p=128)
                  [4 * g:4 * g + 4, :, oc * 128:(oc + 1) * 128]
                  .rearrange("h p o -> p h o"), sw3[sl])
            return sw3[sl].n

        qw3t = [None, None]
        qw3t[0] = qw3_load(0)
        for oc in range(NFC):
            sl = oc % 2
            if oc + 1 < NFC:
                qw3t[(oc + 1) % 2] = qw3_load(oc + 1)
            C.wait("tensor", sw3[sl], qw3t[sl])
            for tc in range(NTC):
                ob = (tc % 2)
                gd.acquire("tensor", ("bank", ob))
                for hcl in range(4):
                    C.op("tensor", lambda e, sl=sl, hcl=hcl, tc=tc, ob=ob:
                         e.matmul(bank(ob), qw3b[sl][:, hcl * 128:(hcl + 1) * 128],
                                  hid[:, (hcl * 4 + tc) * 512:
                                      (hcl * 4 + tc + 1) * 512],
                                  start=(hcl == 0), stop=(hcl == 3)),
                         sem=(sm if hcl == 3 else None))
                C.wait("vector", sm, sm.n)
                xsl = S["x_fm"][:, oc * TOK + tc * CH: oc * TOK + (tc + 1) * CH]
                C.op("vector", lambda e, xsl=xsl, ob=ob:
                     e.tensor_tensor(xsl, bank(ob), xsl, ALU.add), sem=sv)
                gd.use(("bank", ob), sv)
            gd.use(("qw3", sl), sm)
        gd.use(("hid",), sm)
    return [(sm, sm.n), (sv, sv.n), (ss, ss.n)]


def phase_P3(S, fence):
    C, bank, bankb = S["C"], S["bank"], S["bankb"]
    scr, ein, cb = S["scr"], S["ein"], S["cb"]
    x_fm, arXb = S["x_fm"], S["arXb"]
    onesb, identb = cb["onesb"], cb["identb"]
    ehmat = arXb[:, 57344:59392]
    crow_sb, bias_sb = S["crow_sb"], S["bias_sb"]
    sm = C.sem("p3m")
    sv = C.sem("p3v")
    ss = C.sem("p3s")
    sg = C.sem("p3g")
    so2 = [C.sem("p3o0"), C.sem("p3o1")]
    sp = C.sem("p3p")
    sld = [C.sem("p3l0"), C.sem("p3l1")]
    sq_ = C.sem("p3q")
    scc = C.sem("p3cc")
    sko2 = [C.sem("p3ko0"), C.sem("p3ko1")]
    skv = [C.sem("p3kv0"), C.sem("p3kv1")]
    sxs = C.sem("p3xs")
    shl = C.sem("p3hl")
    sms = C.sem("p3ms")
    gd = Guard(C)

    C.barrier(fence, engines=("tensor", "vector", "scalar", "gpsimd", "sync"))
    lnsems = dict(m=sm, v=sv, s=ss, g=sg, o2=so2)
    for tc in range(NTC):
        emit_ln(S, lnsems, tc, "h_scr")
    C.wait("scalar", sv, sv.n)
    C.wait("scalar", sg, sg.n)
    for fc in range(NFC):
        C.dma("scalar", scr["xspill"][fc, :, :], x_fm[:, fc * TOK:(fc + 1) * TOK], sp)
    h_all = arXb[:, 0:NFC * TOK]
    vchunk = arXb[:, 32768:40960]
    vout = arXb[:, 40960:45056]
    outb = arXb[:, 45056:47104]
    wbuf = [arXb[:, 47104 + i * 2048: 47104 + (i + 1) * 2048] for i in range(2)]
    C.wait("sync", sp, sp.n)
    C.wait("sync", so2[0], so2[0].n)
    C.wait("sync", so2[1], so2[1].n)
    for fc in range(NFC):
        C.dma("sync", h_all[:, fc * TOK:(fc + 1) * TOK], scr["h_scr"][fc, :, :], shl)
    C.dma("sync", ehmat, ein["ehmat"][:, :], shl)
    C.barrier([(shl, shl.n)], engines=("tensor", "vector", "scalar"))

    ldtgt3 = {}

    def load_wb(wname, oc, li):
        sl = li % 2
        gd.acquire("sync", ("wbuf", sl))
        for hf in range(2):
            C.dma("sync",
                  wbuf[sl][:, hf * 1024:(hf + 1) * 1024]
                  .rearrange("p (c i) -> p c i", c=8),
                  scr[wname].ap().rearrange("(c p) o -> c p o", p=128)
                  [hf * 8:(hf + 1) * 8, :, oc * 128:(oc + 1) * 128]
                  .rearrange("c p o -> p c o"), sld[sl])
        ldtgt3[sl] = sld[sl].n

    def proj3(li, tc, pb_i):
        w = wbuf[li % 2]
        C.wait("tensor", sld[li % 2], ldtgt3[li % 2])
        gd.acquire("tensor", ("bank", pb_i))
        for cc in range(NFC):
            C.op("tensor", lambda e, w=w, cc=cc, tc=tc, pb_i=pb_i:
                 e.matmul(bank(pb_i), w[:, ts(cc)],
                          h_all[:, cc * TOK + tc * CH: cc * TOK + (tc + 1) * CH],
                          start=(cc == 0), stop=(cc == NFC - 1)),
                 sem=(sm if cc == NFC - 1 else None))
        gd.use(("wbuf", li % 2), sm)

    li = 0
    for wi, wname, dst in ((0, "bWqb", "qb_scr"), (1, "bWkb", "kinb")):
        load_wb(wname, 0, li)
        for oc in range(NFC):
            if oc + 1 < NFC:
                load_wb(wname, oc + 1, li + 1)
            for tc in range(NTC):
                pb_i = 2 + (tc % 2)
                proj3(li, tc, pb_i)
                sl = tc % 2
                C.wait("vector", sm, sm.n)
                gd.acquire("vector", ("outb", sl))
                dst_bf = outb[:, sl * 512:(sl + 1) * 512]
                C.op("vector", lambda e, pb_i=pb_i, dst_bf=dst_bf, wi=wi, oc=oc:
                     e.tensor_scalar(dst_bf, bank(pb_i),
                                     bias_sb[:, wi * 16 + oc: wi * 16 + oc + 1],
                                     None, ALU.add), sem=sv)
                gd.use(("bank", pb_i), sv)
                C.wait("scalar", sv, sv.n)
                C.dma("scalar", scr[dst][ts(oc), tc * CH:(tc + 1) * CH], dst_bf,
                      sko2[sl])
                gd.use(("outb", sl), sko2[sl])
            li += 1
    for tc in range(NTC):
        load_wb("bWvb", 0, li)
        for oc in range(NFC):
            if oc + 1 < NFC:
                load_wb("bWvb", oc + 1, li + 1)
            pb_i = 2 + (oc % 2)
            proj3(li, tc, pb_i)
            C.wait("vector", sm, sm.n)
            gd.acquire("vector", ("vchunk", oc))
            C.op("vector", lambda e, pb_i=pb_i, oc=oc:
                 e.tensor_scalar(vchunk[:, oc * 512:(oc + 1) * 512], bank(pb_i),
                                 bias_sb[:, 32 + oc: 33 + oc], None, ALU.add),
                 sem=sv)
            gd.use(("bank", pb_i), sv)
            li += 1
        C.wait("tensor", sv, sv.n)
        for tt in range(4):
            for g4 in range(4):
                tb_i = 4 + (g4 % 2)
                gd.acquire("tensor", ("bank", tb_i))
                for m in range(4):
                    oc = g4 * 4 + m
                    C.op("tensor", lambda e, tb_i=tb_i, oc=oc, tt=tt, m=m:
                         e.transpose(bankb(tb_i)[:, 0:512][:, ts(m)],
                                     vchunk[:, oc * 512 + tt * 128:
                                            oc * 512 + (tt + 1) * 128],
                                     identb[:, :]),
                         sem=(sm if m == 3 else None))
                    gd.use(("vchunk", oc), sm)
                C.wait("vector", sm, sm.n)
                if g4 == 0:
                    gd.acquire("vector", ("vout",))
                C.op("vector", lambda e, tb_i=tb_i, g4=g4:
                     e.tensor_copy(vout[:, g4 * 512:(g4 + 1) * 512],
                                   bankb(tb_i)[:, 0:512]), sem=sv)
                gd.use(("bank", tb_i), sv)
            C.wait("scalar", sv, sv.n)
            C.dma("scalar", scr["vinb"][tc * CH + tt * 128: tc * CH + (tt + 1) * 128, :],
                  vout[:, 0:2048], sko2[0])
            gd.use(("vout",), sko2[0])
    C.wait("gpsimd", sko2[0], sko2[0].n)
    C.wait("gpsimd", sko2[1], sko2[1].n)
    _ag(C, scr["kinb"].ap(), scr["kgb"].ap(), G2, scc)
    _ag(C, scr["vinb"].ap(), scr["vgb"].ap(), G2, scc)
    C.barrier([(scc, 2), (sm, sm.n), (sv, sv.n), (ss, ss.n)],
              engines=("tensor", "scalar", "vector", "sync"))

    qsb = arXb[:, 0:8192]
    o_all = arXb[:, 8192:16384]
    kvr = [arXb[:, 16384 + i * 4096: 16384 + (i + 1) * 4096] for i in range(2)]
    p_sb = [arXb[:, 24576 + i * 512: 24576 + (i + 1) * 512] for i in range(4)]
    s_sb = [S["arX"][:, 13312 + i * 512: 13312 + (i + 1) * 512] for i in range(2)]
    masks8 = arXb[:, 28672:32768]
    rec_sb = S["arX"][:, 16384:16896]
    lnt = S["arX"][:, 16896:17408]
    kb_sb = S["arX"][:, 17664:18240]
    xcur = S["arX"][:, 18432:26624]
    wov = [arXb[:, 53248 + i * 2048: 53248 + (i + 1) * 2048] for i in range(2)]

    def g2ri(gkt):
        return g2rank_idx(128 * gkt)

    sqp3 = C.sem("p3qp")
    vq_t = vms_t = None
    for slot in range(NTC):
        ktn = KTN[slot]
        pieces = [(p0, min(p0 + 8, ktn)) for p0 in range(0, ktn, 8)]
        if slot == 0:
            gd.acquire("sync", ("qsb",))
            vq_t = C.dma("sync", qsb.rearrange("p (oc t) -> p oc t", oc=16),
                         scr["qb_scr"].ap().rearrange("(oc p) t -> p oc t", p=128)
                         [:, :, slot * CH:(slot + 1) * CH], sqp3)
            gd.acquire("sync", ("masks8",))
            C.dma("sync", masks8.rearrange("p (j x) -> p j x", j=8),
                  ein["mdyn"].ap()[slot].rearrange("j p x -> p j x"), sms)
            gd.acquire("sync", ("kb",))
            C.dma("sync", kb_sb[:, 0:ktn * 16].rearrange("p (i h) -> p i h", i=ktn),
                  ein["kbias"].ap()[slot, 0:ktn].rearrange("i p h -> p i h"), sms)
            vms_t = sms.n
        C.barrier([(sqp3, vq_t)], engines=("tensor",))
        C.barrier([(sms, vms_t)], engines=("tensor", "vector", "scalar"))

        def load_piece(pi, slot_buf, hg):
            p0, p1 = pieces[pi]
            L = p1 - p0
            sem = skv[slot_buf]
            gd.acquire("sync", ("kv", slot_buf))
            runs = []
            r0 = p0
            while r0 < p1:
                rk, idx = g2ri(r0)
                r1 = r0
                while r1 + 1 < p1 and g2ri(r1 + 1)[0] == rk:
                    r1 += 1
                runs.append((r0, r1 + 1, rk, idx))
                r0 = r1 + 1
            Lp = p1 - p0
            for (a, b, rk, idx) in runs:
                Lr = b - a
                for hh in range(2):
                    gh = hg * 2 + hh
                    C.dma("sync",
                          kvr[slot_buf][:, hh * Lp * 128 + (a - p0) * 128:
                                        hh * Lp * 128 + (b - p0) * 128],
                          scr["kgb"].ap()
                          .rearrange("(r c p) t -> r c p t", r=2, p=128)
                          [rk, gh, :, idx:idx + Lr * 128], sem)
                C.dma("sync",
                      kvr[slot_buf][:, 2048 + (a - p0) * 256:
                                    2048 + (a - p0) * 256 + Lr * 256]
                      .rearrange("p (l x) -> p l x", l=Lr),
                      scr["vgb"].ap()
                      .rearrange("(r l p) d -> r l p d", r=2, p=128)
                      [rk, idx // 128: idx // 128 + Lr, :,
                       (hg * 2) * 128:(hg * 2 + 2) * 128]
                      .rearrange("l p d -> p l d"), sem)
            return sem.n

        kvt = [None, None]
        kvt[0] = load_piece(0, 0, 0)
        crowb2 = [arXb[:, 59392 + i * 512: 59392 + (i + 1) * 512] for i in range(2)]
        for hg in range(8):
            units = []
            for pi, (p0, p1) in enumerate(pieces):
                for itl in range(p1 - p0):
                    for hh in range(2):
                        units.append((p0 + itl, hh, pi, pi % 2, itl, p1 - p0))
            ss_t3 = {}
            t_cb = 0
            for hh in range(2):
                gh = hg * 2 + hh
                gd.acquire("tensor", ("st", hh))
                C.op("tensor", lambda e, hh=hh, gh=gh:
                     e.matmul(bank(hh), ehmat[:, ts(gh)], crow_sb[:, :],
                              start=True, stop=True), sem=sm)
                C.wait("scalar", sm, sm.n)
                gd.acquire("scalar", ("crowb", hh))
                t_cb = C.op("scalar", lambda e, hh=hh:
                            e.copy(crowb2[hh], bank(hh)), sem=ss)
                gd.use(("st", hh), ss)
            C.wait("vector", ss, t_cb)

            def emit_qk3(ui, u):
                it, hh, pi, sbuf_i, itl, Lp_ = u
                gh = hg * 2 + hh
                sti = (it * 2 + hh) % 4
                gd.acquire("tensor", ("st", sti))
                qh = qsb[:, gh * 512:(gh + 1) * 512]
                C.op("tensor", lambda e, sti=sti, sbuf_i=sbuf_i, itl=itl,
                     hh=hh, qh=qh, Lp_=Lp_:
                     e.matmul(bank(sti),
                              kvr[sbuf_i][:, hh * Lp_ * 128 + itl * 128:
                                          hh * Lp_ * 128 + (itl + 1) * 128],
                              qh, start=True, stop=True), sem=sm)
                kbv = kb_sb[:, it * 16 + gh: it * 16 + gh + 1]
                sb = s_sb[ui % 2]
                C.wait("vector", sm, sm.n)
                gd.acquire("vector", ("ssb", ui % 2))
                if it in MASKPOS[slot]:
                    mj = MASKPOS[slot].index(it)
                    C.op("vector", lambda e, sti=sti, mj=mj, sb=sb:
                         e.tensor_tensor(sb, bank(sti),
                                         masks8[:, mj * 512:(mj + 1) * 512],
                                         ALU.add))
                    C.op("vector", lambda e, sb=sb, hh=hh:
                         e.tensor_tensor(sb, sb, crowb2[hh], ALU.add), sem=sv)
                else:
                    C.op("vector", lambda e, sti=sti, sb=sb, hh=hh:
                         e.tensor_tensor(sb, bank(sti), crowb2[hh], ALU.add),
                         sem=sv)
                gd.use(("st", sti), sv)
                C.wait("scalar", sv, sv.n)
                gd.acquire("scalar", ("p", (it * 2 + hh) % 4))
                ss_t3[ui] = C.op("scalar", lambda e, it=it, hh=hh, kbv=kbv,
                                 sb=sb:
                                 e.activation(p_sb[(it * 2 + hh) % 4], sb,
                                              AF.Exp, bias=kbv, scale=ISQD),
                                 sem=ss)
                gd.use(("ssb", ui % 2), ss)

            def emit_pv3(ui, u, last_of_piece):
                it, hh, pi, sbuf_i, itl, Lp_ = u
                pb = p_sb[(it * 2 + hh) % 4]
                if it == 0:
                    gd.acquire("tensor", ("bank", 4 + hh))
                    gd.acquire("tensor", ("bank", 6 + hh))
                C.wait("tensor", ss, ss_t3[ui])
                C.op("tensor", lambda e, hh=hh, sbuf_i=sbuf_i, itl=itl, pb=pb,
                     it=it, Lp_=Lp_, ktn=ktn:
                     e.matmul(bank(4 + hh),
                              kvr[sbuf_i][:, 2048 + itl * 256 + hh * 128:
                                          2048 + itl * 256 + (hh + 1) * 128],
                              pb, start=(it == 0), stop=(it == ktn - 1)))
                C.op("tensor", lambda e, hh=hh, pb=pb, it=it, ktn=ktn:
                     e.matmul(bank(6 + hh), onesb[:, :], pb,
                              start=(it == 0), stop=(it == ktn - 1)), sem=sm)
                gd.use(("p", (it * 2 + hh) % 4), sm)
                if last_of_piece:
                    gd.use(("kv", sbuf_i), sm)

            for ui, u in enumerate(units):
                it, hh, pi, sbuf_i, itl, Lp_ = u
                if hh == 0 and itl == 0:
                    C.wait("tensor", skv[sbuf_i], kvt[sbuf_i])
                emit_qk3(ui, u)
                if ui >= 2:
                    up = units[ui - 2]
                    emit_pv3(ui - 2, up, up[2] != units[ui - 1][2])
                if hh == 1 and itl == 0:
                    if pi + 1 < len(pieces):
                        kvt[(pi + 1) % 2] = load_piece(pi + 1, (pi + 1) % 2, hg)
                    elif hg + 1 < 8:
                        kvt[0] = load_piece(0, 0, hg + 1)
            emit_pv3(len(units) - 2, units[-2], units[-2][2] != units[-1][2])
            emit_pv3(len(units) - 1, units[-1], True)
            for hh in range(2):
                gd.use(("crowb", hh), sv)
            for hh in range(2):
                gh = hg * 2 + hh
                C.wait("scalar", sm, sm.n)
                gd.acquire("scalar", ("rec",))
                C.op("scalar", lambda e, hh=hh: e.activation(lnt, bank(6 + hh), AF.Ln))
                t_rec = C.op("scalar", lambda e:
                             e.activation(rec_sb, lnt, AF.Exp, scale=-1.0), sem=ss)
                gd.use(("bank", 6 + hh), ss)
                C.wait("vector", ss, t_rec)
                gd.acquire("vector", ("oall", gh))
                C.op("vector", lambda e, gh=gh, hh=hh:
                     e.tensor_tensor(o_all[:, gh * 512:(gh + 1) * 512], bank(4 + hh),
                                     rec_sb, ALU.mult), sem=sv)
                gd.use(("bank", 4 + hh), sv)
                gd.use(("rec",), sv)
        gd.use(("qsb",), sm)
        gd.use(("masks8",), sv)
        gd.use(("kb",), ss)
        if slot + 1 < NTC:
            ktn2 = KTN[slot + 1]
            gd.acquire("gpsimd", ("qsb",))
            vq_t = C.dma("gpsimd", qsb.rearrange("p (oc t) -> p oc t", oc=16),
                         scr["qb_scr"].ap().rearrange("(oc p) t -> p oc t", p=128)
                         [:, :, (slot + 1) * CH:(slot + 2) * CH], sqp3)
            gd.acquire("gpsimd", ("masks8",))
            C.dma("gpsimd", masks8.rearrange("p (j x) -> p j x", j=8),
                  ein["mdyn"].ap()[slot + 1].rearrange("j p x -> p j x"), sms)
            gd.acquire("gpsimd", ("kb",))
            C.dma("gpsimd",
                  kb_sb[:, 0:ktn2 * 16].rearrange("p (i h) -> p i h", i=ktn2),
                  ein["kbias"].ap()[slot + 1, 0:ktn2].rearrange("i p h -> p i h"),
                  sms)
            vms_t = sms.n
        C.wait("tensor", sv, sv.n)
        C.wait("sync", sxs, sxs.n)
        gd.acquire("sync", ("xcur",))
        vx = C.dma("sync", xcur.rearrange("p (oc t) -> p oc t", oc=16),
                   scr["xspill"].ap().rearrange("oc p t -> p oc t")
                   [:, :, slot * CH:(slot + 1) * CH], sq_)
        C.barrier([(sq_, vx)], engines=("vector",))
        def wov3_load(oc):
            sl = oc % 2
            gd.acquire("sync", ("wov", sl))
            for hf in range(2):
                C.dma("sync",
                      wov[sl][:, hf * 1024:(hf + 1) * 1024]
                      .rearrange("p (c i) -> p c i", c=8),
                      scr["bWob"].ap().rearrange("(c p) o -> c p o", p=128)
                      [hf * 8:(hf + 1) * 8, :, oc * 128:(oc + 1) * 128]
                      .rearrange("c p o -> p c o"), sld[sl])
            return sld[sl].n

        wovt3 = [None, None]
        wovt3[0] = wov3_load(0)
        for oc in range(NFC):
            sl = oc % 2
            if oc + 1 < NFC:
                wovt3[(oc + 1) % 2] = wov3_load(oc + 1)
            C.wait("tensor", sld[sl], wovt3[sl])
            pb_i = 2 + sl
            gd.acquire("tensor", ("bank", pb_i))
            for dc in range(NFC):
                C.op("tensor", lambda e, pb_i=pb_i, sl=sl, dc=dc:
                     e.matmul(bank(pb_i), wov[sl][:, ts(dc)],
                              o_all[:, dc * 512:(dc + 1) * 512],
                              start=(dc == 0), stop=(dc == NFC - 1)),
                     sem=(sm if dc == NFC - 1 else None))
            gd.use(("wov", sl), sm)
            if oc == NFC - 1:
                for hh2 in range(NH):
                    gd.use(("oall", hh2), sm)
            C.wait("vector", sm, sm.n)
            xslice = xcur[:, oc * 512:(oc + 1) * 512]
            C.op("vector", lambda e, pb_i=pb_i, xslice=xslice, oc=oc:
                 e.scalar_tensor_tensor(xslice, bank(pb_i),
                                        bias_sb[:, 48 + oc: 49 + oc], xslice,
                                        ALU.add, ALU.add), sem=sv)
            gd.use(("bank", pb_i), sv)
            gd.use(("st", pb_i), sv)
        C.wait("sync", sv, sv.n)
        for oc in range(NFC):
            C.dma("sync", scr["xspill"][oc, :, slot * CH:(slot + 1) * CH],
                  xcur[:, oc * 512:(oc + 1) * 512], sxs)
        gd.use(("xcur",), sxs)
    sxr = C.sem("p3xr")
    C.wait("sync", sxs, sxs.n)
    for fc in range(NFC):
        C.dma("sync", S["x_fm"][:, fc * TOK:(fc + 1) * TOK], scr["xspill"][fc, :, :],
              sxr)
    return [(sxr, sxr.n), (sm, sm.n), (sv, sv.n), (ss, ss.n)]


def phase_P9(S, fence):
    C, bank = S["C"], S["bank"]
    x_fm, xout = S["x_fm"], S["xout"]
    identf = S["cb"]["identf"]
    sd = [C.sem(f"p9d{i}") for i in range(2)]
    sm = C.sem("p9m")
    sv = C.sem("p9v")
    ot = [S["arAf"][:, 0:2048], S["arAf"][:, 2048:4096]]
    C.barrier(fence, engines=("tensor", "vector", "scalar", "sync", "gpsimd"))
    gi = 0
    for tt in range(16):
        bi = tt % 2
        buf = ot[bi]
        for g in range(4):
            bk = bank(gi % 2)
            if gi >= 2:
                C.wait("tensor", sv, gi - 1)
            for f4 in range(4):
                fc = g * 4 + f4
                C.op("tensor", lambda e, bk=bk, fc=fc, f4=f4, tt=tt:
                     e.transpose(bk[:, ts(f4)],
                                 x_fm[:, fc * TOK + tt * 128: fc * TOK + (tt + 1) * 128],
                                 identf[:, :]),
                     sem=(sm if f4 == 3 else None))
            C.wait("vector", sm, sm.n)
            if tt >= 2 and g == 0:
                C.wait("vector", sd[bi], 16 * (tt // 2))
            C.op("vector", lambda e, bk=bk, buf=buf, g=g:
                 e.tensor_copy(buf[:, g * 512:(g + 1) * 512], bk), sem=sv, drain=True)
            gi += 1
        C.wait("sync", sv, sv.n)
        C.dma("sync", xout[ts(tt), :], buf, sd[bi])
    C.wait("sync", sd[0], sd[0].n)
    C.wait("sync", sd[1], sd[1].n)


def kernel(**inputs):
    xfull = np.ascontiguousarray(np.asarray(inputs["L0_attn_a_x"], dtype=np.float32))
    W = {k: np.asarray(inputs[k], dtype=np.float32) for k in
         ("L0_attn_a_Wq", "L0_attn_a_Wk", "L0_attn_a_Wv", "L0_attn_a_Wo",
          "L0_attn_b_Wq", "L0_attn_b_Wk", "L0_attn_b_Wv", "L0_attn_b_Wo",
          "L0_ffn_w1", "L0_ffn_w2", "L0_ffn_w3",
          "L0_attn_b_bq", "L0_attn_b_bk", "L0_attn_b_bv", "L0_attn_b_bo")}
    fw1p = np.zeros((6144, D), dtype=np.float32)
    fw1p[:HID] = W["L0_ffn_w1"]
    fw2p = np.zeros((6144, D), dtype=np.float32)
    fw2p[:HID] = W["L0_ffn_w2"]
    fw3p = np.zeros((D, 6144), dtype=np.float32)
    fw3p[:, :HID] = W["L0_ffn_w3"]
    nc = build_nc(_PHASES)
    in_maps = []
    for c in range(8):
        b = c // 2
        pos = np.concatenate([np.arange(t0, t0 + CH) for t0 in chunk_starts(c)])
        m = {"xin": np.ascontiguousarray(xfull[b, pos, :])}
        m["aWq"] = np.ascontiguousarray(W["L0_attn_a_Wq"][256 * c:256 * (c + 1), :])
        m["aWk"] = np.ascontiguousarray(W["L0_attn_a_Wk"][128 * (c % 4):128 * (c % 4 + 1), :])
        m["aWv"] = np.ascontiguousarray(W["L0_attn_a_Wv"][128 * (c % 4):128 * (c % 4 + 1), :])
        m["aWo"] = np.ascontiguousarray(W["L0_attn_a_Wo"][256 * c:256 * (c + 1), :])
        for k, nm in (("bWq", "L0_attn_b_Wq"), ("bWk", "L0_attn_b_Wk"),
                      ("bWv", "L0_attn_b_Wv"), ("bWo", "L0_attn_b_Wo")):
            m[k] = np.ascontiguousarray(W[nm][256 * c:256 * (c + 1), :])
        m["fw1"] = np.ascontiguousarray(fw1p[768 * c:768 * (c + 1), :])
        m["fw2"] = np.ascontiguousarray(fw2p[768 * c:768 * (c + 1), :])
        m["fw3"] = np.ascontiguousarray(fw3p[:, 768 * c:768 * (c + 1)])
        for k, nm in (("bbq", "L0_attn_b_bq"), ("bbk", "L0_attn_b_bk"),
                      ("bbv", "L0_attn_b_bv"), ("bbo", "L0_attn_b_bo")):
            m[k] = np.ascontiguousarray(W[nm].reshape(16, 128))
        hc = host_consts(c)
        for k in ("ropecos", "ropesin", "rotT", "identf", "identb", "onesb", "crow",
                  "ehmat", "masks", "kbias", "kvoff", "self4", "mdyn"):
            m[k] = hc[k]
        in_maps.append(m)
    res = run_bass_kernel_spmd(nc, in_maps, core_ids=list(range(8)))
    out = np.empty((B, S, D), dtype=np.float32)
    for c in range(8):
        b = c // 2
        pos = np.concatenate([np.arange(t0, t0 + CH) for t0 in chunk_starts(c)])
        out[b, pos, :] = res.results[c]["xout"]
    return out


# version 08 (final)
# speedup vs baseline: 1.3205x; 1.0001x over previous
import math
from contextlib import ExitStack

import numpy as np
import ml_dtypes

import concourse.bass as bass
import concourse.mybir as mybir
from concourse.bass_utils import run_bass_kernel_spmd

F32 = mybir.dt.float32
BF16 = mybir.dt.bfloat16
I32 = mybir.dt.int32
AF = mybir.ActivationFunctionType
ALU = mybir.AluOpType
AX = mybir.AxisListType
bfdt = ml_dtypes.bfloat16

B, S, D = 4, 4096, 2048
HID = 5632
NH, NKV, HD = 16, 4, 128
TOK = 2048
CH = 512
NTC = TOK // CH
NFC = D // 128
NHC = HID // 128
SQD = math.sqrt(128.0)
ISQD = 1.0 / SQD
ACONST = -16.0
KTN = [12, 16, 28, 32]
NEG = -1.0e30
LN_EPS = 1e-5
Q_EPS = 1e-8
NGRP = 8
FFN_GRPS = [list(range(g, min(g + 6, NHC))) for g in range(0, NHC, 6)]

PHASES_ALL = ("P0", "PW", "PG", "P1", "P2", "P3", "P4", "P9")
_PHASES = PHASES_ALL


def chunk_starts(core):
    return [0, 512, 3072, 3584] if core % 2 == 0 else [1024, 1536, 2048, 2560]


def _maskpos():
    re_ = [(t0 + CH) // 128 for t0 in chunk_starts(0)]
    ro_ = [(t0 + CH) // 128 for t0 in chunk_starts(1)]
    out = []
    for s_i in range(NTC):
        out.append(sorted(set(range(re_[s_i] - 4, re_[s_i])) |
                          set(range(ro_[s_i] - 4, ro_[s_i]))))
    return out


MASKPOS = _maskpos()

def g2rank_idx(g):
    if g < 1024:
        return 0, g
    if g < 3072:
        return 1, g - 1024
    return 0, g - 2048


def _slopes16():
    start = 2.0 ** (-(2.0 ** -(math.log2(16) - 3)))
    return np.array([start * start ** i for i in range(16)], dtype=np.float32)


def host_consts(core):
    c = {}
    pos = np.concatenate([np.arange(t0, t0 + CH) for t0 in chunk_starts(core)])
    ar = np.arange(0, HD, 2, dtype=np.float32) / np.float32(HD)
    inv = (np.float32(1.0) / (np.float32(10000.0) ** ar)).astype(np.float32)
    t = np.arange(S, dtype=np.float32)
    freqs = (t[:, None] * inv[None, :]).astype(np.float32)
    emb = np.concatenate([freqs, freqs], axis=1)
    c["ropecos"] = np.ascontiguousarray(np.cos(emb).astype(np.float32).T[:, pos])
    c["ropesin"] = np.ascontiguousarray(np.sin(emb).astype(np.float32).T[:, pos])
    R = np.zeros((HD, HD), dtype=np.float32)
    for d in range(64):
        R[d, d + 64] = -1.0
        R[64 + d, d] = 1.0
    c["rotT"] = np.ascontiguousarray(R.T).astype(bfdt)
    c["identf"] = np.eye(128, dtype=np.float32)
    c["identb"] = np.eye(128, dtype=bfdt)
    c["onesb"] = np.ones((128, 128), dtype=bfdt)
    slopes = _slopes16()
    j = np.arange(CH, dtype=np.float32)
    c["crow"] = (-(SQD) * slopes[:, None] * j[None, :]).astype(bfdt)
    eh = np.zeros((128, NH * 128), dtype=np.float32)
    for h in range(NH):
        eh[h, h * 128:(h + 1) * 128] = 1.0
    c["ehmat"] = eh.astype(bfdt)
    kk = np.arange(128)[:, None]
    jj = np.arange(CH)[None, :]
    masks = np.concatenate([(kk + d > jj).astype(np.float32) * NEG
                            for d in (0, 128, 256, 384)], axis=1)
    c["masks"] = np.ascontiguousarray(masks.astype(np.float32))
    starts = chunk_starts(core)
    kb = np.zeros((NTC, 36, 128, NH), dtype=np.float32)
    kkv = np.arange(128, dtype=np.float32)
    for s_i, t0 in enumerate(starts):
        R_real = (t0 + CH) // 128
        for it in range(KTN[s_i]):
            if it < R_real:
                gk = 128 * it + kkv
                kb[s_i, it, :, :] = slopes[None, :] * (gk[:, None] - t0)
            else:
                kb[s_i, it] = NEG
    c["kbias"] = kb
    c["maskpos"] = MASKPOS
    mdyn = np.zeros((NTC, 8, 128, CH), dtype=np.float32)
    kk2 = np.arange(128)[:, None]
    jj2 = np.arange(CH)[None, :]
    for s_i, t0 in enumerate(starts):
        R_real = (t0 + CH) // 128
        for j, it in enumerate(MASKPOS[s_i]):
            if R_real - 4 <= it < R_real:
                d = 128 * it - t0
                mdyn[s_i, j] = (kk2 + d > jj2).astype(np.float32) * NEG
    c["mdyn"] = mdyn.astype(bfdt)
    off = np.zeros((NTC, 36, 2), dtype=np.int64)
    c["kvoff"] = off.astype(np.int32)
    c["self4"] = np.concatenate(
        [np.tile((np.arange(4) == w).astype(np.float32)[:, None], (1, 128))
         for w in range(3)], axis=1)
    return c


ENGS = ("tensor", "vector", "scalar", "gpsimd", "sync")


class Sem:
    __slots__ = ("h", "n", "name")

    def __init__(self, h, name):
        self.h, self.n, self.name = h, 0, name

    def bump(self, k=1):
        self.n += k
        return self.n


class Ctx:
    def __init__(self, nc, st):
        self.nc = nc
        self.st = st
        self.q = {e: [] for e in ENGS}
        self._semn = 0
        self.chain = {e: None for e in ("vector", "scalar", "gpsimd")}
        self.last = {e: None for e in ("vector", "scalar", "gpsimd")}

    def sem(self, name=None):
        self._semn += 1
        h = self.st.enter_context(self.nc.semaphore(name or f"s{self._semn}"))
        return Sem(h, name)

    def sb(self, name, shape, dt):
        return self.st.enter_context(self.nc.sbuf_tensor(name, shape, dt))

    def emit(self, eng, fn):
        self.q[eng].append(fn)

    def wait(self, eng, sem, val):
        if val > 0:
            self.emit(eng, lambda e, s=sem.h, v=val: e.wait_ge(s, v))

    def op(self, eng, fn, sem=None, k=1, drain=False):
        tgt = sem.bump(k) if sem is not None else None
        if eng in self.chain:
            if self.chain[eng] is None:
                self.chain[eng] = self.sem("ch_" + eng)
            ch = self.chain[eng]
            prev = self.last[eng]
            if sem is not None:
                semh, tv, delta = sem.h, tgt, k
            else:
                tv = ch.bump(1)
                semh, delta = ch.h, 1
            if prev is not None:
                self.emit(eng, lambda e, s=prev[0], v=prev[1]: e.wait_ge(s, v))

            def run(e, fn=fn, semh=semh, delta=delta):
                ins = fn(e)
                (ins if ins is not None else e.nop()).then_inc(semh, delta)
            self.emit(eng, run)
            self.last[eng] = (semh, tv)
        else:
            def run(e, fn=fn, sem=sem, k=k, drain=drain):
                ins = fn(e)
                if sem is not None:
                    if drain:
                        e.drain().then_inc(sem.h, k)
                    elif ins is not None:
                        ins.then_inc(sem.h, k)
                    else:
                        e.nop().then_inc(sem.h, k)
            self.emit(eng, run)
        return tgt

    def dma(self, eng, out, in_, sem, **kw):
        tgt = sem.bump(16)
        self.emit(eng, lambda e, out=out, in_=in_, kw=kw, s=sem.h:
                  e.dma_start(out=out, in_=in_, **kw).then_inc(s, 16))
        return tgt

    def barrier(self, sems_vals, engines=ENGS):
        for e in engines:
            for sem, v in sems_vals:
                self.wait(e, sem, v)


def ts(i, n=128):
    return slice(i * n, (i + 1) * n)


class Guard:

    def __init__(self, C):
        self.C = C
        self.m = {}

    def use(self, key, sem):
        self.m.setdefault(key, {})[sem] = sem.n

    def acquire(self, eng, key):
        for sem, v in self.m.pop(key, {}).items():
            self.C.wait(eng, sem, v)


def _ag(C, in_ap, out_ap, groups, sem):
    tgt = sem.bump(1)
    C.emit("gpsimd", lambda e, i=in_ap, o=out_ap, g=groups, s=sem.h:
           e.collective_compute("AllGather", ALU.bypass, replica_groups=g,
                                ins=[i.opt()], outs=[o.opt()]).then_inc(s, 1))
    return tgt


G8 = [[0, 1, 2, 3, 4, 5, 6, 7]]
G4 = [[0, 1, 2, 3], [4, 5, 6, 7]]
G2 = [[0, 1], [2, 3], [4, 5], [6, 7]]


def build_nc(phases=None):
    if phases is None:
        phases = _PHASES
    nc = bass.Bass("TRN2", target_bir_lowering=False)
    st = ExitStack()
    C = Ctx(nc, st)

    ein = {}

    def EIN(name, shape, dt):
        ein[name] = nc.dram_tensor(name, shape, dt, kind="ExternalInput")
        return ein[name]

    xin = EIN("xin", [TOK, D], F32)
    for nm, shape in [("aWq", [256, D]), ("aWk", [128, D]), ("aWv", [128, D]),
                      ("aWo", [256, D]), ("bWq", [256, D]), ("bWk", [256, D]),
                      ("bWv", [256, D]), ("bWo", [256, D]),
                      ("fw1", [768, D]), ("fw2", [768, D]), ("fw3", [D, 768]),
                      ("bbq", [16, 128]), ("bbk", [16, 128]), ("bbv", [16, 128]),
                      ("bbo", [16, 128]),
                      ("ropecos", [128, TOK]), ("ropesin", [128, TOK]),
                      ("identf", [128, 128]), ("masks", [128, 4 * CH]),
                      ("kbias", [NTC, 36, 128, NH]), ("self4", [4, 384])]:
        EIN(nm, shape, F32)
    for nm, shape in [("rotT", [128, 128]), ("identb", [128, 128]),
                      ("onesb", [128, 128]), ("crow", [16, CH]),
                      ("ehmat", [128, NH * 128]), ("mdyn", [NTC, 8, 128, CH])]:
        EIN(nm, shape, BF16)
    EIN("kvoff", [NTC, 36, 2], I32)

    xout = nc.dram_tensor("xout", [TOK, D], F32, kind="ExternalOutput")

    DR = lambda name, shape, dt: nc.dram_tensor(name, shape, dt)
    scr = {}
    for nm, shape, dt in [
        ("aWqT", [D, D], BF16), ("aWkT", [512, D], BF16),
        ("aWvT", [512, D], BF16), ("aWoT", [D, D], BF16),
        ("bWqb", [D, D], BF16), ("bWkb", [D, D], BF16), ("bWvb", [D, D], BF16),
        ("bWob", [D, D], BF16),
        ("qw1", [6144, D], BF16), ("qw2", [6144, D], BF16), ("qw3", [6144, D], BF16),
        ("awq_in", [256, D], BF16), ("awk_in", [128, D], BF16),
        ("awv_in", [128, D], BF16), ("awo_in", [256, D], BF16),
        ("bwq_in", [256, D], BF16), ("bwk_in", [256, D], BF16),
        ("bwv_in", [256, D], BF16), ("bwo_in", [256, D], BF16),
        ("qw1_in", [768, D], BF16), ("qw2_in", [768, D], BF16),
        ("qw3_in", [768, D], BF16),
        ("gpart_in", [1, 4], F32), ("gpart_out", [8, 4], F32),
        ("h_scr", [NFC, 128, TOK], BF16),
        ("qa_scr", [D, TOK], BF16), ("qb_scr", [D, TOK], BF16),
        ("xspill", [NFC, 128, TOK], F32),
        ("kina", [512, TOK], BF16), ("vina", [TOK, 512], BF16),
        ("kga", [2 * 512, TOK], BF16), ("vga", [2 * TOK, 512], BF16),
        ("kinb", [D, TOK], BF16), ("vinb", [TOK, D], BF16),
        ("kgb", [2 * D, TOK], BF16), ("vgb", [2 * TOK, D], BF16),
    ]:
        scr[nm] = DR(nm, shape, dt)

    cb = {}
    for nm, shape, dt in [("rotT", [128, 128], BF16), ("identf", [128, 128], F32),
                          ("identb", [128, 128], BF16), ("onesb", [128, 128], BF16)]:
        cb[nm] = C.sb("c_" + nm, shape, dt)
    bias_sb = C.sb("bias_sb", [128, 4 * 16], F32)
    crow_sb = C.sb("crow_sb", [128, CH], BF16)

    arX = C.sb("arX", [128, NFC * TOK], F32)
    arXb = arX[:, :].bitcast(BF16)
    arA = C.sb("arA", [128, 8192], BF16)
    arAf = arA[:, :].bitcast(F32)
    arB = C.sb("arB", [128, 4096], BF16)
    arBf = arB[:, :].bitcast(F32)
    arC = C.sb("arC", [128, 24576], BF16)
    arCf = arC[:, :].bitcast(F32)
    psA = st.enter_context(nc.psum_tensor("psA", [128, 4096], F32))
    psAb = psA[:, :].bitcast(BF16)

    def bank(b, n=1):
        return psA[:, b * 512:(b + n) * 512]

    def bankb(b, n=1):
        return psAb[:, b * 1024:(b + n) * 1024]

    constv = C.sb("constv", [128, 4], F32)
    sCN = C.sem("cn")
    for i, val in enumerate((0.0, LN_EPS, ACONST)):
        C.op("vector", lambda e, i=i, val=val: e.memset(constv[:, i:i + 1], val),
             sem=(sCN if i == 2 else None), drain=(i == 2))
        nc.const_aps.register(F32, val, constv[:, i:i + 1])
    C.wait("scalar", sCN, 1)
    C.wait("tensor", sCN, 1)

    sLD = C.sem("ld")
    for nm in ("rotT", "identf", "identb", "onesb"):
        C.dma("gpsimd", cb[nm][:, :], ein[nm][:, :], sLD)
    sMS = C.sem("ms")
    vms = C.op("vector", lambda e: e.memset(crow_sb[:, :], 0.0), sem=sMS, drain=True)
    C.wait("gpsimd", sMS, vms)
    C.dma("gpsimd", crow_sb[0:16, :], ein["crow"][:, :], sLD)
    vLD = sLD.n

    S = dict(nc=nc, C=C, ein=ein, scr=scr, cb=cb, bias_sb=bias_sb,
             crow_sb=crow_sb, x_fm=arX, arX=arX, arXb=arXb, arA=arA, arAf=arAf,
             arB=arB, arBf=arBf, arC=arC, arCf=arCf, bank=bank, bankb=bankb,
             sLD=sLD, vLD=vLD, xin=xin, xout=xout)

    fence = phase_P0(S)
    fa, fb = phase_PW(S) if "PW" in phases else ([], [])
    fence_g = phase_PG(S) if "PG" in phases else []
    if "P1" in phases:
        fence = phase_P1(S, fence + fa)
    if "P2" in phases:
        fence = phase_P2(S, fence + fence_g, "f2")
    if "P3" in phases:
        fence = phase_P3(S, fence + fb)
    if "P4" in phases:
        fence = phase_P2(S, fence, "f4")
    phase_P9(S, fence)

    block = st.enter_context(nc.Block())

    def _mk(eng_name):
        def _run(e):
            for fn in C.q[eng_name]:
                fn(e)
        return _run

    block.tensor(_mk("tensor"))
    block.vector(_mk("vector"))
    block.scalar(_mk("scalar"))
    block.gpsimd(_mk("gpsimd"))
    block.sync(_mk("sync"))

    st.close()
    return nc


def phase_P0(S):
    C, bank = S["C"], S["bank"]
    x_fm, xin = S["x_fm"], S["xin"]
    identf = S["cb"]["identf"]
    sd = [C.sem(f"p0d{i}") for i in range(2)]
    sm = C.sem("p0m")
    sv = C.sem("p0v")
    xt = [S["arAf"][:, 0:2048], S["arAf"][:, 2048:4096]]
    C.wait("tensor", S["sLD"], S["vLD"])
    gi = 0
    for tt in range(16):
        bi = tt % 2
        buf = xt[bi]
        if tt >= 2:
            C.wait("sync", sm, 4 * (tt - 1))
        vland = C.dma("sync", buf, xin[ts(tt), :], sd[bi])
        C.wait("tensor", sd[bi], vland)
        for g in range(4):
            bk = bank(gi % 2)
            if gi >= 2:
                C.wait("tensor", sv, gi - 1)
            for f4 in range(4):
                fc = g * 4 + f4
                C.op("tensor", lambda e, bk=bk, buf=buf, fc=fc, f4=f4, identf=identf:
                     e.transpose(bk[:, ts(f4)], buf[:, ts(fc)], identf[:, :]),
                     sem=(sm if f4 == 3 else None))
            C.wait("vector", sm, sm.n)
            out_ap = x_fm[:, g * 4 * TOK:(g * 4 + 4) * TOK] \
                .rearrange("p (f l) -> p f l", f=4)[:, :, tt * 128:(tt + 1) * 128]
            in_ap = bk[:, :].rearrange("p (f l) -> p f l", f=4)
            C.op("vector", lambda e, o=out_ap, i=in_ap: e.tensor_copy(o, i),
                 sem=sv, drain=True)
            gi += 1
    return [(sv, sv.n)]


def phase_PW(S):
    C, ein, scr, cb, bankb = S["C"], S["ein"], S["scr"], S["cb"], S["bankb"]
    identb, identf = cb["identb"], cb["identf"]
    sdw = C.sem("pwd")
    smw = C.sem("pwm")
    svw = C.sem("pwv")
    sso = C.sem("pwo")
    sgw = C.sem("pwg")
    wb = S["arB"][:, 0:2048]
    tb = S["arB"][:, 2048:4096]
    bk = bankb(2)[:, 0:512]
    C.wait("tensor", S["sLD"], S["vLD"])

    def prep_tile(src_ap, dst_ap):
        vw = sso.n
        C.wait("gpsimd", smw, smw.n)
        vd = C.dma("gpsimd", wb, src_ap, sdw)
        C.wait("tensor", sdw, vd)
        for g in range(4):
            C.wait("tensor", svw, svw.n)
            for f4 in range(4):
                fc = g * 4 + f4
                C.op("tensor", lambda e, fc=fc, f4=f4:
                     e.transpose(bk[:, ts(f4)], wb[:, ts(fc)], identb[:, :]),
                     sem=(smw if f4 == 3 else None))
            C.wait("vector", smw, smw.n)
            if g == 0:
                C.wait("vector", sso, vw)
            C.op("vector", lambda e, g=g:
                 e.tensor_copy(tb[:, g * 512:(g + 1) * 512], bk), sem=svw, drain=True)
        C.wait("sync", svw, svw.n)
        return C.dma("sync", dst_ap, tb, sso)

    for nm, inn, agout, ntile, groups in [
        ("aWk", "awk_in", "aWkT", 1, G4), ("aWv", "awv_in", "aWvT", 1, G4),
        ("aWq", "awq_in", "aWqT", 2, G8), ("aWo", "awo_in", "aWoT", 2, G8),
    ]:
        for ot in range(ntile):
            vst = prep_tile(ein[nm][ts(ot), :], scr[inn][ts(ot), :])
        C.wait("gpsimd", sso, vst)
        _ag(C, scr[inn].ap(), scr[agout].ap(), groups, sgw)
    for nm, inn, agout in [("bWq", "bwq_in", "bWqb"), ("bWk", "bwk_in", "bWkb"),
                           ("bWv", "bwv_in", "bWvb"), ("bWo", "bwo_in", "bWob")]:
        vc = C.dma("gpsimd", scr[inn][:, :], ein[nm][:, :], sdw)
        C.wait("gpsimd", sdw, vc)
        _ag(C, scr[inn].ap(), scr[agout].ap(), G8, sgw)
    sbb = C.sem("pwb")
    ssb = C.sem("pwbs")
    bbs = S["arBf"][0:16, 0:128]
    pb = S["bank"](2)[:, 0:16]
    for wi, nm in enumerate(("bbq", "bbk", "bbv", "bbo")):
        C.wait("sync", smw, smw.n)
        vb = C.dma("sync", bbs, ein[nm][:, :], sbb)
        C.wait("tensor", sbb, vb)
        C.wait("tensor", ssb, ssb.n)
        C.op("tensor", lambda e: e.transpose(pb, bbs, identf[0:16, 0:16]), sem=smw)
        C.wait("scalar", smw, smw.n)
        C.op("scalar", lambda e, wi=wi:
             e.copy(S["bias_sb"][:, wi * 16:(wi + 1) * 16], pb), sem=ssb, drain=True)
    fa = [(sgw, sgw.n)]
    fb = [(sgw, sgw.n), (ssb, ssb.n)]
    S["pw_done"] = [(sgw, sgw.n), (ssb, ssb.n), (svw, svw.n), (smw, smw.n)]
    return fa, fb


def phase_PG(S):
    C, ein, scr, cb = S["C"], S["ein"], S["scr"], S["cb"]
    identb, identf = cb["identb"], cb["identf"]
    bank, bankb = S["bank"], S["bankb"]
    sdq = C.sem("pgd")
    ssc = C.sem("pgs")
    svq = C.sem("pgv")
    sgp = C.sem("pgp")
    smq = C.sem("pgm")
    sgg = C.sem("pgg")
    sst = C.sem("pgt")
    wf = S["arCf"][:, 0:2048]
    mp = S["arC"][:, 4096:6144]
    mn = S["arC"][:, 6144:8192]
    qt = S["arC"][:, 8192:10240]
    tb = S["arC"][:, 10240:12288]
    acc = C.sb("pg_acc", [128, 32], F32)
    red = C.sb("pg_red", [128, 4], F32)
    gp_sb = C.sb("pg_gp", [8, 4], F32)
    tot = C.sb("pg_tot", [4, 1], F32)
    thr4 = C.sb("pg_thr4", [4, 1], F32)
    nthr4 = C.sb("pg_nthr4", [4, 1], F32)
    thr_sb = C.sb("pg_thr", [128, 8], F32)
    self4 = C.sb("pg_self4", [4, 384], F32)
    bk = bankb(6)[:, 0:512]
    pr = bank(7)[:, 0:128]
    pth = bank(7)[:, 128:129]
    C.dma("gpsimd", self4[:, :], ein["self4"][:, :], S["sLD"])
    S["vLD"] = S["sLD"].n

    tiles = [("fw1", ein["fw1"][ts(t), :], D) for t in range(6)] + \
            [("fw2", ein["fw2"][ts(t), :], D) for t in range(6)] + \
            [("fw3", ein["fw3"][ts(t), :], 768) for t in range(16)]
    slot = 0
    for nm, ap, fd in tiles:
        vd = C.dma("sync", wf[:, 0:fd], ap, sdq)
        C.wait("scalar", sdq, vd)
        C.op("scalar", lambda e, fd=fd, slot=slot:
             e.activation(mp[:, 0:fd], wf[:, 0:fd], AF.Abs,
                          accum_out=acc[:, slot:slot + 1]), sem=ssc, drain=True)
        C.wait("sync", ssc, ssc.n)
        slot += 1
    C.wait("vector", ssc, ssc.n)
    C.op("vector", lambda e: e.tensor_reduce(red[:, 0:1], acc[:, 0:6], AX.X, ALU.add))
    C.op("vector", lambda e: e.tensor_reduce(red[:, 1:2], acc[:, 6:12], AX.X, ALU.add))
    C.op("vector", lambda e: e.tensor_reduce(red[:, 2:3], acc[:, 12:28], AX.X, ALU.add))
    C.op("vector", lambda e: e.memset(red[:, 3:4], 0.0), sem=svq, drain=True)
    C.wait("tensor", svq, svq.n)
    C.wait("tensor", S["sLD"], S["vLD"])
    C.op("tensor", lambda e: e.transpose(pr[0:4, :], red[:, :], identf[:, :]), sem=smq)
    C.wait("vector", smq, smq.n)
    C.op("vector", lambda e: e.tensor_reduce(red[0:4, 0:1], pr[0:4, :], AX.X, ALU.add),
         sem=svq, drain=True)
    C.wait("sync", svq, svq.n)
    vst = C.dma("sync", scr["gpart_in"][0:1, 0:4], red[0:4, 0:1], sst)
    C.wait("gpsimd", sst, vst)
    vgg = _ag(C, scr["gpart_in"].ap(), scr["gpart_out"].ap(), G8, sgg)
    C.wait("gpsimd", sgg, vgg)
    sgl = C.sem("pgl")
    vgl = C.dma("gpsimd", gp_sb[:, :], scr["gpart_out"][:, :], sgl)
    C.wait("tensor", sgl, vgl)
    C.wait("tensor", svq, svq.n)
    C.op("tensor", lambda e: e.transpose(pr[0:4, 0:8], gp_sb[:, :], identf[0:8, 0:8]),
         sem=smq)
    C.wait("vector", smq, smq.n)
    C.op("vector", lambda e: e.tensor_reduce(tot[:, 0:1], pr[0:4, 0:8], AX.X, ALU.add))
    C.op("vector", lambda e: e.tensor_scalar(thr4[:, :], tot[:, :],
                                             1.0 / (2.0 * HID * D), 0.5 * Q_EPS,
                                             ALU.mult, ALU.add))
    C.op("vector", lambda e: e.tensor_scalar(nthr4[:, :], thr4[:, :], -1.0, None,
                                             ALU.mult), sem=svq, drain=True)
    C.wait("tensor", svq, svq.n)
    for wi in range(3):
        for neg in range(2):
            src = nthr4 if neg else thr4
            C.op("tensor", lambda e, wi=wi, src=src:
                 e.matmul(pth, self4[:, wi * 128:(wi + 1) * 128], src[:, :],
                          start=True, stop=True), sem=smq)
            C.wait("scalar", smq, smq.n)
            col = 3 * neg + wi
            C.op("scalar", lambda e, col=col:
                 e.copy(thr_sb[:, col:col + 1], pth), sem=ssc, drain=True)
            C.wait("tensor", ssc, ssc.n)
    C.wait("vector", ssc, ssc.n)

    def quant_tile(src_ap, fd, wi):
        C.wait("sync", svq, svq.n)
        vd = C.dma("sync", wf[:, 0:fd], src_ap, sdq)
        C.wait("vector", sdq, vd)
        C.wait("vector", sgp, sgp.n)
        C.op("vector", lambda e, fd=fd, wi=wi:
             e.tensor_scalar(mp[:, 0:fd], wf[:, 0:fd], thr_sb[:, wi:wi + 1], None,
                             ALU.is_ge))
        C.op("vector", lambda e, fd=fd, wi=wi:
             e.tensor_scalar(mn[:, 0:fd], wf[:, 0:fd], thr_sb[:, 3 + wi:4 + wi], None,
                             ALU.is_le), sem=svq, drain=True)
        C.wait("gpsimd", svq, svq.n)
        C.wait("gpsimd", smq, smq.n)
        C.op("gpsimd", lambda e, fd=fd:
             e.tensor_tensor(qt[:, 0:fd], mp[:, 0:fd], mn[:, 0:fd], ALU.subtract),
             sem=sgp, drain=True)
        return sgp.n

    for wi, nm, inn in [(0, "fw1", "qw1_in"), (1, "fw2", "qw2_in")]:
        for t in range(6):
            vq = quant_tile(ein[nm][ts(t), :], D, wi)
            C.wait("tensor", sgp, vq)
            for g in range(4):
                C.wait("tensor", svq, svq.n)
                C.wait("tensor", sst, sst.n)
                for f4 in range(4):
                    fc = g * 4 + f4
                    C.op("tensor", lambda e, fc=fc, f4=f4:
                         e.transpose(bk[:, ts(f4)], qt[:, ts(fc)], identb[:, :]),
                         sem=(smq if f4 == 3 else None))
                C.wait("vector", smq, smq.n)
                C.op("vector", lambda e, g=g:
                     e.tensor_copy(tb[:, g * 512:(g + 1) * 512], bk),
                     sem=svq, drain=True)
            C.wait("sync", svq, svq.n)
            C.dma("sync", scr[inn][ts(t), :], tb, sst)
        C.wait("gpsimd", sst, sst.n)
        _ag(C, scr[inn].ap(), scr["qw" + str(wi + 1)].ap(), G8, sgg)
    for oc in range(16):
        vq = quant_tile(ein["fw3"][ts(oc), :], 768, 2)
        C.wait("tensor", sgp, vq)
        for g in range(2):
            nm_ = 4 if g == 0 else 2
            C.wait("tensor", svq, svq.n)
            C.wait("tensor", sst, sst.n)
            for m in range(nm_):
                hcl = g * 4 + m
                C.op("tensor", lambda e, hcl=hcl, m=m:
                     e.transpose(bk[:, ts(m)], qt[:, ts(hcl)], identb[:, :]),
                     sem=(smq if m == nm_ - 1 else None))
            C.wait("vector", smq, smq.n)
            C.op("vector", lambda e, g=g, nm_=nm_:
                 e.tensor_copy(tb[:, g * 512:g * 512 + nm_ * 128],
                               bk[:, 0:nm_ * 128]), sem=svq, drain=True)
        C.wait("sync", svq, svq.n)
        for hcl in range(6):
            g, m = divmod(hcl, 4)
            C.dma("sync", scr["qw3_in"][hcl * 128:(hcl + 1) * 128, ts(oc)],
                  tb[:, g * 512 + m * 128:g * 512 + (m + 1) * 128], sst)
    C.wait("gpsimd", sst, sst.n)
    vlast = _ag(C, scr["qw3_in"].ap(), scr["qw3"].ap(), G8, sgg)
    return [(sgg, vlast)]


def emit_ln(S, sems, tc, hname):
    C, bank, x_fm = S["C"], S["bank"], S["x_fm"]
    onesb = S["cb"]["onesb"]
    sm, sv, ss, sg = sems["m"], sems["v"], sems["s"], sems["g"]
    so2 = sems["o2"]
    xb2 = S["arB"][:, 0:1024]
    sq2 = S["arB"][:, 1024:2048]
    hb2 = S["arB"][:, 2048:3072]
    m_sb = S["arBf"][:, 1536:2048]
    rstd = S["arCf"][:, 3072:3584]
    ms_sb = S["arCf"][:, 3584:4096]
    sx, sq = bank(0), bank(1)
    C.wait("tensor", sv, sv.n)
    for fc in range(NFC):
        xs = x_fm[:, fc * TOK + tc * CH: fc * TOK + (tc + 1) * CH]
        sl = fc % 2
        C.wait("scalar", sm, sm.n)
        C.op("scalar", lambda e, xs=xs, sl=sl:
             e.copy(xb2[:, sl * 512:(sl + 1) * 512], xs), sem=ss, drain=True)
        C.wait("gpsimd", sm, sm.n)
        C.op("gpsimd", lambda e, xs=xs, sl=sl:
             e.tensor_tensor(sq2[:, sl * 512:(sl + 1) * 512], xs, xs, ALU.mult),
             sem=sg, drain=True)
        C.wait("tensor", ss, ss.n)
        C.op("tensor", lambda e, sl=sl, fc=fc:
             e.matmul(sx, onesb[:, :], xb2[:, sl * 512:(sl + 1) * 512],
                      start=(fc == 0), stop=(fc == NFC - 1)), sem=sm)
        C.wait("tensor", sg, sg.n)
        C.op("tensor", lambda e, sl=sl, fc=fc:
             e.matmul(sq, onesb[:, :], sq2[:, sl * 512:(sl + 1) * 512],
                      start=(fc == 0), stop=(fc == NFC - 1)), sem=sm)
    C.wait("vector", sm, sm.n)
    C.op("vector", lambda e: e.tensor_scalar(m_sb, sx, 1.0 / D, None, ALU.mult))
    C.op("vector", lambda e: e.tensor_tensor(ms_sb, m_sb, m_sb, ALU.mult))
    C.op("vector", lambda e: e.scalar_tensor_tensor(rstd, sq, 1.0 / D, ms_sb,
                                                    ALU.mult, ALU.subtract),
         sem=sv, drain=True)
    C.wait("scalar", sv, sv.n)
    C.op("scalar", lambda e: e.activation(rstd, rstd, AF.Ln, bias=LN_EPS, scale=1.0))
    C.op("scalar", lambda e: e.activation(rstd, rstd, AF.Exp, scale=-0.5),
         sem=ss, drain=True)
    C.wait("vector", ss, ss.n)
    for fc in range(NFC):
        xs = x_fm[:, fc * TOK + tc * CH: fc * TOK + (tc + 1) * CH]
        sl = fc % 2
        C.wait("vector", so2[sl], so2[sl].n)
        C.op("vector", lambda e, xs=xs, sl=sl:
             e.tensor_tensor(hb2[:, sl * 512:(sl + 1) * 512], xs, m_sb, ALU.subtract))
        C.op("vector", lambda e, sl=sl:
             e.tensor_tensor(hb2[:, sl * 512:(sl + 1) * 512],
                             hb2[:, sl * 512:(sl + 1) * 512], rstd, ALU.mult),
             sem=sv, drain=True)
        C.wait("sync", sv, sv.n)
        C.dma("sync", S["scr"][hname][fc, :, tc * CH:(tc + 1) * CH],
              hb2[:, sl * 512:(sl + 1) * 512], so2[sl])


def phase_P1(S, fence):
    C, bank, bankb = S["C"], S["bank"], S["bankb"]
    scr, ein, cb = S["scr"], S["ein"], S["cb"]
    x_fm, arXb = S["x_fm"], S["arXb"]
    onesb, rotT, identb = cb["onesb"], cb["rotT"], cb["identb"]
    sm = C.sem("p1m")
    sv = C.sem("p1v")
    ss = C.sem("p1s")
    sg = C.sem("p1g")
    so2 = [C.sem("p1o0"), C.sem("p1o1")]
    sp = C.sem("p1p")
    sld = [C.sem("p1l0"), C.sem("p1l1")]
    sq_ = C.sem("p1q")
    scc = C.sem("p1cc")
    sko2 = [C.sem("p1ko0"), C.sem("p1ko1")]
    skv = [C.sem("p1kv0"), C.sem("p1kv1")]
    sxs = C.sem("p1xs")
    shl = C.sem("p1hl")
    gd = Guard(C)

    C.barrier(fence, engines=("tensor", "vector", "scalar", "gpsimd", "sync"))
    lnsems = dict(m=sm, v=sv, s=ss, g=sg, o2=so2)
    for tc in range(NTC):
        emit_ln(S, lnsems, tc, "h_scr")
    C.wait("scalar", sv, sv.n)
    C.wait("scalar", sg, sg.n)
    for fc in range(NFC):
        C.dma("scalar", scr["xspill"][fc, :, :], x_fm[:, fc * TOK:(fc + 1) * TOK], sp)
    h_all = arXb[:, 0:NFC * TOK]
    cosb = S["arX"][:, 16384:18432]
    sinb = S["arX"][:, 18432:20480]
    t1 = S["arX"][:, 20480:20992]
    t2 = S["arX"][:, 20992:21504]
    outb = arXb[:, 45056:47104]
    wbuf = [arXb[:, 47104 + i * 2048: 47104 + (i + 1) * 2048] for i in range(2)]
    vtb = arXb[:, 51200:53248]
    C.wait("sync", sp, sp.n)
    C.wait("sync", so2[0], so2[0].n)
    C.wait("sync", so2[1], so2[1].n)
    for fc in range(NFC):
        C.dma("sync", h_all[:, fc * TOK:(fc + 1) * TOK], scr["h_scr"][fc, :, :], shl)
    C.dma("sync", cosb, ein["ropecos"][:, :], shl)
    C.dma("sync", sinb, ein["ropesin"][:, :], shl)
    C.barrier([(shl, shl.n)], engines=("tensor", "vector", "scalar"))

    ldtgt = {}

    def load_w(src_rows, li):
        sl = li % 2
        gd.acquire("sync", ("wbuf", sl))
        ldtgt[sl] = C.dma("sync", wbuf[sl], src_rows, sld[sl])

    def proj_psum(li, tc, pb_i):
        w = wbuf[li % 2]
        C.wait("tensor", sld[li % 2], ldtgt[li % 2])
        gd.acquire("tensor", ("bank", pb_i))
        for fc in range(NFC):
            C.op("tensor", lambda e, w=w, fc=fc, tc=tc, pb_i=pb_i:
                 e.matmul(bank(pb_i), w[:, ts(fc)],
                          h_all[:, fc * TOK + tc * CH: fc * TOK + (tc + 1) * CH],
                          start=(fc == 0), stop=(fc == NFC - 1)),
                 sem=(sm if fc == NFC - 1 else None))
        gd.use(("wbuf", li % 2), sm)

    def rope_unit(pb_i, rb_i, tc, sl):
        dst_bf = outb[:, sl * 512:(sl + 1) * 512]
        C.wait("scalar", sm, sm.n)
        gd.acquire("scalar", ("outb", sl))
        C.op("scalar", lambda e, pb_i=pb_i, dst_bf=dst_bf:
             e.copy(dst_bf, bank(pb_i)), sem=ss)
        C.wait("tensor", ss, ss.n)
        gd.acquire("tensor", ("bank", rb_i))
        C.op("tensor", lambda e, rb_i=rb_i, dst_bf=dst_bf:
             e.matmul(bank(rb_i), rotT[:, :], dst_bf, start=True, stop=True), sem=sm)
        C.wait("vector", sm, sm.n)
        cs = cosb[:, tc * CH:(tc + 1) * CH]
        sn = sinb[:, tc * CH:(tc + 1) * CH]
        C.op("vector", lambda e, pb_i=pb_i, cs=cs:
             e.tensor_tensor(t1, bank(pb_i), cs, ALU.mult))
        C.op("vector", lambda e, rb_i=rb_i, sn=sn:
             e.tensor_tensor(t2, bank(rb_i), sn, ALU.mult))
        C.op("vector", lambda e, dst_bf=dst_bf:
             e.tensor_tensor(dst_bf, t1, t2, ALU.add), sem=sv)
        gd.use(("bank", pb_i), ss)
        gd.use(("bank", pb_i), sv)
        gd.use(("bank", rb_i), sv)
        return dst_bf

    def store_out(dst_ap, dst_bf, sl):
        C.wait("scalar", sv, sv.n)
        C.dma("scalar", dst_ap, dst_bf, sko2[sl])
        gd.use(("outb", sl), sko2[sl])

    li = 0
    load_w(scr["aWkT"][ts(0), :], 0)
    for dc in range(4):
        if dc + 1 < 4:
            load_w(scr["aWkT"][ts(dc + 1), :], li + 1)
        for tc in range(NTC):
            proj_psum(li, tc, 2 + (tc % 2))
            dst = rope_unit(2 + (tc % 2), 4 + (tc % 2), tc, tc % 2)
            store_out(scr["kina"][ts(dc), tc * CH:(tc + 1) * CH], dst, tc % 2)
        li += 1
    for tc in range(NTC):
        load_w(scr["aWvT"][ts(0), :], li)
        for dc in range(4):
            if dc + 1 < 4:
                load_w(scr["aWvT"][ts(dc + 1), :], li + 1)
            pb_i = 2 + (dc % 2)
            proj_psum(li, tc, pb_i)
            C.wait("scalar", sm, sm.n)
            gd.acquire("scalar", ("vtb", dc))
            C.op("scalar", lambda e, pb_i=pb_i, dc=dc:
                 e.copy(vtb[:, dc * 512:(dc + 1) * 512], bank(pb_i)), sem=ss)
            gd.use(("bank", pb_i), ss)
            li += 1
        C.wait("tensor", ss, ss.n)
        for tt in range(4):
            tb_i = 4 + (tt % 2)
            gd.acquire("tensor", ("bank", tb_i))
            for dc in range(4):
                C.op("tensor", lambda e, tb_i=tb_i, dc=dc, tt=tt:
                     e.transpose(bankb(tb_i)[:, 0:512][:, ts(dc)],
                                 vtb[:, dc * 512 + tt * 128: dc * 512 + (tt + 1) * 128],
                                 identb[:, :]),
                     sem=(sm if dc == 3 else None))
                gd.use(("vtb", dc), sm)
            C.wait("vector", sm, sm.n)
            sl = tt % 2
            vout = outb[:, sl * 512:(sl + 1) * 512]
            gd.acquire("vector", ("outb", sl))
            C.op("vector", lambda e, tb_i=tb_i, vout=vout:
                 e.tensor_copy(vout, bankb(tb_i)[:, 0:512]), sem=sv)
            gd.use(("bank", tb_i), sv)
            store_out(scr["vina"][tc * CH + tt * 128: tc * CH + (tt + 1) * 128, :],
                      vout, sl)
    load_w(scr["aWqT"][ts(0), :], li)
    for oc in range(NFC):
        if oc + 1 < NFC:
            load_w(scr["aWqT"][ts(oc + 1), :], li + 1)
        for tc in range(NTC):
            proj_psum(li, tc, 2 + (tc % 2))
            dst = rope_unit(2 + (tc % 2), 4 + (tc % 2), tc, tc % 2)
            store_out(scr["qa_scr"][ts(oc), tc * CH:(tc + 1) * CH], dst, tc % 2)
        li += 1
    C.wait("gpsimd", sko2[0], sko2[0].n)
    C.wait("gpsimd", sko2[1], sko2[1].n)
    _ag(C, scr["kina"].ap(), scr["kga"].ap(), G2, scc)
    _ag(C, scr["vina"].ap(), scr["vga"].ap(), G2, scc)
    C.barrier([(scc, 2), (sm, sm.n), (sv, sv.n), (ss, ss.n)],
              engines=("tensor", "scalar", "vector", "sync"))

    kkb = [arXb[:, i * 4096: (i + 1) * 4096] for i in range(2)]
    vvb = [arXb[:, 8192 + i * 4096: 8192 + (i + 1) * 4096] for i in range(2)]
    qsb = arXb[:, 16384:24576]
    o_all = arXb[:, 24576:32768]
    p_sb = [arXb[:, 32768 + i * 1024: 32768 + (i + 1) * 1024] for i in range(4)]
    rec_sb = S["arX"][:, 18432:18944]
    lnt = S["arX"][:, 18944:19456]
    xcur = S["arX"][:, 19456:27648]
    wov = [arXb[:, 55296 + i * 2048: 55296 + (i + 1) * 2048] for i in range(2)]

    def load_kv(kvh, slot):
        sem = skv[slot]
        gd.acquire("sync", ("kv", slot))
        for r in range(2):
            C.dma("sync", kkb[slot][:, r * 2048:(r + 1) * 2048],
                  scr["kga"][r * 512 + kvh * 128: r * 512 + (kvh + 1) * 128, :], sem)
        for r in range(2):
            for hh in range(2):
                C.dma("sync",
                      vvb[slot][:, (r * 16 + hh * 8) * 128:
                                (r * 16 + (hh + 1) * 8) * 128]
                      .rearrange("p (a d) -> p a d", a=8),
                      scr["vga"][r * TOK + hh * 1024: r * TOK + (hh + 1) * 1024,
                                 kvh * 128:(kvh + 1) * 128]
                      .rearrange("(a p) d -> p a d", p=128), sem)
        return sem.n

    recp = [rec_sb, S["arX"][:, 29696:30208]]
    lntp = [lnt, S["arX"][:, 30208:30720]]
    sqp = C.sem("p1qp")
    kvt = [None, None]
    kvt[0] = load_kv(0, 0)
    vq_t = None
    for qc in range(NTC):
        if qc == 0:
            gd.acquire("sync", ("qsb",))
            vq_t = C.dma("sync", qsb.rearrange("p (oc t) -> p oc t", oc=16),
                         scr["qa_scr"].ap().rearrange("(oc p) t -> p oc t", p=128)
                         [:, :, qc * CH:(qc + 1) * CH], sqp)
        C.barrier([(sqp, vq_t)], engines=("tensor",))
        units = [(h, kp, h // 4) for h in range(16) for kp in range(16)]
        ss_t = {}

        def emit_qk(ui, h, kp, kvh, qc=qc):
            qh = qsb[:, h * 512:(h + 1) * 512]
            sti = 2 * (ui % 2)
            st2 = bank(sti, 2)
            gd.acquire("tensor", ("st", sti))
            for j in range(2):
                kt = 2 * kp + j
                C.op("tensor", lambda e, st2=st2, j=j, kt=kt, kvh=kvh, qh=qh:
                     e.matmul(st2[:, j * 512:(j + 1) * 512],
                              kkb[kvh % 2][:, kt * 128:(kt + 1) * 128],
                              qh, start=True, stop=True),
                     sem=(sm if j == 1 else None))
            C.wait("scalar", sm, sm.n)
            gd.acquire("scalar", ("p", ui % 4))
            ss_t[ui] = C.op("scalar", lambda e, st2=st2, ui=ui:
                            e.activation(p_sb[ui % 4], st2, AF.Exp, bias=ACONST,
                                         scale=ISQD), sem=ss)
            gd.use(("st", sti), ss)

        def emit_pv(ui, h, kp, kvh, qc=qc):
            pb2 = p_sb[ui % 4]
            ob_, dnb = 4 + 2 * (h % 2), 5 + 2 * (h % 2)
            if kp == 0:
                gd.acquire("tensor", ("bank", ob_))
                gd.acquire("tensor", ("bank", dnb))
            C.wait("tensor", ss, ss_t[ui])
            for j in range(2):
                kt = 2 * kp + j
                C.op("tensor", lambda e, pb2=pb2, j=j, kt=kt, kvh=kvh, ob_=ob_:
                     e.matmul(bank(ob_), vvb[kvh % 2][:, kt * 128:(kt + 1) * 128],
                              pb2[:, j * 512:(j + 1) * 512],
                              start=(kt == 0), stop=(kt == 31)))
                C.op("tensor", lambda e, pb2=pb2, j=j, kt=kt, dnb=dnb:
                     e.matmul(bank(dnb), onesb[:, :],
                              pb2[:, j * 512:(j + 1) * 512],
                              start=(kt == 0), stop=(kt == 31)),
                     sem=(sm if j == 1 else None))
            gd.use(("p", ui % 4), sm)
            if kp == 15:
                gd.use(("kv", kvh % 2), sm)
                rc, lt = recp[h % 2], lntp[h % 2]
                C.wait("scalar", sm, sm.n)
                gd.acquire("scalar", ("rec", h % 2))
                C.op("scalar", lambda e, dnb=dnb, lt=lt:
                     e.activation(lt, bank(dnb), AF.Ln))
                t_rec = C.op("scalar", lambda e, rc=rc, lt=lt:
                             e.activation(rc, lt, AF.Exp, scale=-1.0), sem=ss)
                gd.use(("bank", dnb), ss)
                C.wait("vector", ss, t_rec)
                oh = o_all[:, h * 512:(h + 1) * 512]
                gd.acquire("vector", ("oall", h))
                C.op("vector", lambda e, oh=oh, ob_=ob_, rc=rc:
                     e.tensor_tensor(oh, bank(ob_), rc, ALU.mult), sem=sv)
                gd.use(("bank", ob_), sv)
                gd.use(("rec", h % 2), sv)

        for ui, (h, kp, kvh) in enumerate(units):
            if kp == 0 and h % 4 == 0:
                C.wait("tensor", skv[kvh % 2], kvt[kvh % 2])
            elif kp == 2 and h % 4 == 0:
                if kvh + 1 < 4:
                    kvt[(kvh + 1) % 2] = load_kv(kvh + 1, (kvh + 1) % 2)
                elif qc + 1 < NTC:
                    kvt[0] = load_kv(0, 0)
            emit_qk(ui, h, kp, kvh)
            if ui >= 2:
                emit_pv(ui - 2, *units[ui - 2])
        gd.use(("qsb",), sm)
        for ui in range(len(units) - 2, len(units)):
            emit_pv(ui, *units[ui])
        if qc + 1 < NTC:
            gd.acquire("scalar", ("qsb",))
            vq_t = C.dma("scalar", qsb.rearrange("p (oc t) -> p oc t", oc=16),
                         scr["qa_scr"].ap().rearrange("(oc p) t -> p oc t", p=128)
                         [:, :, (qc + 1) * CH:(qc + 2) * CH], sqp)
        C.wait("tensor", sv, sv.n)
        C.wait("sync", sxs, sxs.n)
        gd.acquire("sync", ("xcur",))
        vx = C.dma("sync", xcur.rearrange("p (oc t) -> p oc t", oc=16),
                   scr["xspill"].ap().rearrange("oc p t -> p oc t")
                   [:, :, qc * CH:(qc + 1) * CH], sq_)
        C.barrier([(sq_, vx)], engines=("vector",))
        def wov_load(oc):
            sl = oc % 2
            gd.acquire("sync", ("wov", sl))
            C.dma("sync", wov[sl], scr["aWoT"][ts(oc), :], sld[sl])
            return sld[sl].n

        wovt = [None, None]
        wovt[0] = wov_load(0)
        for oc in range(NFC):
            sl = oc % 2
            if oc + 1 < NFC:
                wovt[(oc + 1) % 2] = wov_load(oc + 1)
            C.wait("tensor", sld[sl], wovt[sl])
            pb_i = 2 + sl
            gd.acquire("tensor", ("bank", pb_i))
            for dc in range(NFC):
                C.op("tensor", lambda e, pb_i=pb_i, sl=sl, dc=dc:
                     e.matmul(bank(pb_i), wov[sl][:, ts(dc)],
                              o_all[:, dc * 512:(dc + 1) * 512],
                              start=(dc == 0), stop=(dc == NFC - 1)),
                     sem=(sm if dc == NFC - 1 else None))
            gd.use(("wov", sl), sm)
            if oc == NFC - 1:
                for hh in range(NH):
                    gd.use(("oall", hh), sm)
            C.wait("vector", sm, sm.n)
            xslice = xcur[:, oc * 512:(oc + 1) * 512]
            C.op("vector", lambda e, pb_i=pb_i, xslice=xslice:
                 e.tensor_tensor(xslice, bank(pb_i), xslice, ALU.add), sem=sv)
            gd.use(("bank", pb_i), sv)
            gd.use(("st", 2), sv)
        C.wait("sync", sv, sv.n)
        for oc in range(NFC):
            C.dma("sync", scr["xspill"][oc, :, qc * CH:(qc + 1) * CH],
                  xcur[:, oc * 512:(oc + 1) * 512], sxs)
        gd.use(("xcur",), sxs)
    sxr = C.sem("p1xr")
    C.wait("sync", sxs, sxs.n)
    for fc in range(NFC):
        C.dma("sync", S["x_fm"][:, fc * TOK:(fc + 1) * TOK], scr["xspill"][fc, :, :],
              sxr)
    return [(sxr, sxr.n), (sm, sm.n), (sv, sv.n), (ss, ss.n)]


def phase_P2(S, fence, tag):
    C, bank, arA, arB, arC = S["C"], S["bank"], S["arA"], S["arB"], S["arC"]
    scr = S["scr"]
    sm = C.sem(tag + "m")
    sv = C.sem(tag + "v")
    ss = C.sem(tag + "s")
    sg = C.sem(tag + "g")
    so2 = [C.sem(tag + "o0"), C.sem(tag + "o1")]
    swq = C.sem(tag + "wq")
    sw3 = [C.sem(tag + "w30"), C.sem(tag + "w31")]
    sh2 = [C.sem(tag + "h0"), C.sem(tag + "h1")]
    gd = Guard(C)
    C.barrier(fence, engines=("tensor", "vector", "scalar", "gpsimd", "sync"))
    lnsems = dict(m=sm, v=sv, s=ss, g=sg, o2=so2)
    for tc in range(NTC):
        emit_ln(S, lnsems, tc, "h_scr")
    C.wait("sync", so2[0], so2[0].n)
    C.wait("sync", so2[1], so2[1].n)
    qw1b = arC[:, 0:8192]
    qw2b = arC[:, 8192:16384]
    hid = arC[:, 16384:24576]
    h2b = [arA[:, 0:4096], arA[:, 4096:8192]]
    s1b = [arB[:, i * 512:(i + 1) * 512] for i in range(2)]
    qw3b = [arB[:, 1024 + i * 512: 1024 + (i + 1) * 512] for i in range(2)]
    h_scr = scr["h_scr"]
    def load_qw12(g, eng):
        gd.acquire(eng, ("qw12",))
        for hcl, hc in enumerate([4 * g + i for i in range(4)]):
            C.dma(eng, qw1b[:, hcl * 2048:(hcl + 1) * 2048],
                  scr["qw1"][ts(hc), :], swq)
            C.dma(eng, qw2b[:, hcl * 2048:(hcl + 1) * 2048],
                  scr["qw2"][ts(hc), :], swq)
        return swq.n

    qwtgt = load_qw12(0, "sync")
    for g in range(11):
        C.wait("tensor", swq, qwtgt)
        for tc in range(NTC):
            for half in range(2):
                qeng = "sync" if half == 0 else "scalar"
                gd.acquire(qeng, ("h2", half))
                C.dma(qeng,
                      h2b[half].rearrange("p (f j) -> p f j", f=8),
                      h_scr.ap()[half * 8:(half + 1) * 8, :,
                                 tc * CH:(tc + 1) * CH]
                      .rearrange("f p j -> p f j"), sh2[half])
            C.wait("tensor", sh2[0], sh2[0].n)
            C.wait("tensor", sh2[1], sh2[1].n)
            for hcl in range(4):
                b1 = (hcl % 2) * 2
                b2 = (hcl % 2) * 2 + 1
                gd.acquire("tensor", ("bank", b1))
                for fc in range(NFC):
                    C.op("tensor", lambda e, hcl=hcl, fc=fc, b1=b1:
                         e.matmul(bank(b1), qw1b[:, hcl * 2048 + fc * 128:
                                               hcl * 2048 + (fc + 1) * 128],
                                  h2b[fc // 8][:, (fc % 8) * 512:(fc % 8 + 1) * 512],
                                  start=(fc == 0), stop=(fc == NFC - 1)),
                         sem=(sm if fc == NFC - 1 else None))
                C.wait("scalar", sm, sm.n)
                gd.acquire("scalar", ("s1", hcl % 2))
                C.op("scalar", lambda e, hcl=hcl, b1=b1:
                     e.activation(s1b[hcl % 2], bank(b1), AF.Sigmoid), sem=ss)
                gd.use(("bank", b1), ss)
                gd.acquire("tensor", ("bank", b2))
                for fc in range(NFC):
                    C.op("tensor", lambda e, hcl=hcl, fc=fc, b2=b2:
                         e.matmul(bank(b2), qw2b[:, hcl * 2048 + fc * 128:
                                               hcl * 2048 + (fc + 1) * 128],
                                  h2b[fc // 8][:, (fc % 8) * 512:(fc % 8 + 1) * 512],
                                  start=(fc == 0), stop=(fc == NFC - 1)),
                         sem=(sm if fc == NFC - 1 else None))
                C.wait("vector", sm, sm.n)
                C.wait("vector", ss, ss.n)
                gd.acquire("vector", ("hid",))
                C.op("vector", lambda e, hcl=hcl, tc=tc, b2=b2:
                     e.tensor_tensor(hid[:, (hcl * 4 + tc) * 512:
                                         (hcl * 4 + tc + 1) * 512],
                                     s1b[hcl % 2], bank(b2), ALU.mult), sem=sv)
                gd.use(("bank", b2), sv)
                gd.use(("s1", hcl % 2), sv)
            gd.use(("h2", 0), sm)
            gd.use(("h2", 1), sm)
        gd.use(("qw12",), sm)
        if g + 1 < 11:
            qwtgt = load_qw12(g + 1, "scalar")
        C.wait("tensor", sv, sv.n)
        def qw3_load(oc):
            sl = oc % 2
            gd.acquire("sync", ("qw3", sl))
            C.dma("sync",
                  qw3b[sl].rearrange("p (h i) -> p h i", h=4),
                  scr["qw3"].ap().rearrange("(h p) o -> h p o", p=128)
                  [4 * g:4 * g + 4, :, oc * 128:(oc + 1) * 128]
                  .rearrange("h p o -> p h o"), sw3[sl])
            return sw3[sl].n

        qw3t = [None, None]
        qw3t[0] = qw3_load(0)
        for oc in range(NFC):
            sl = oc % 2
            if oc + 1 < NFC:
                qw3t[(oc + 1) % 2] = qw3_load(oc + 1)
            C.wait("tensor", sw3[sl], qw3t[sl])
            for tc in range(NTC):
                ob = (tc % 2)
                gd.acquire("tensor", ("bank", ob))
                for hcl in range(4):
                    C.op("tensor", lambda e, sl=sl, hcl=hcl, tc=tc, ob=ob:
                         e.matmul(bank(ob), qw3b[sl][:, hcl * 128:(hcl + 1) * 128],
                                  hid[:, (hcl * 4 + tc) * 512:
                                      (hcl * 4 + tc + 1) * 512],
                                  start=(hcl == 0), stop=(hcl == 3)),
                         sem=(sm if hcl == 3 else None))
                C.wait("vector", sm, sm.n)
                xsl = S["x_fm"][:, oc * TOK + tc * CH: oc * TOK + (tc + 1) * CH]
                C.op("vector", lambda e, xsl=xsl, ob=ob:
                     e.tensor_tensor(xsl, bank(ob), xsl, ALU.add), sem=sv)
                gd.use(("bank", ob), sv)
            gd.use(("qw3", sl), sm)
        gd.use(("hid",), sm)
    return [(sm, sm.n), (sv, sv.n), (ss, ss.n)]


def phase_P3(S, fence):
    C, bank, bankb = S["C"], S["bank"], S["bankb"]
    scr, ein, cb = S["scr"], S["ein"], S["cb"]
    x_fm, arXb = S["x_fm"], S["arXb"]
    onesb, identb = cb["onesb"], cb["identb"]
    ehmat = arXb[:, 57344:59392]
    crow_sb, bias_sb = S["crow_sb"], S["bias_sb"]
    sm = C.sem("p3m")
    sv = C.sem("p3v")
    ss = C.sem("p3s")
    sg = C.sem("p3g")
    so2 = [C.sem("p3o0"), C.sem("p3o1")]
    sp = C.sem("p3p")
    sld = [C.sem("p3l0"), C.sem("p3l1")]
    sq_ = C.sem("p3q")
    scc = C.sem("p3cc")
    sko2 = [C.sem("p3ko0"), C.sem("p3ko1")]
    skv = [C.sem("p3kv0"), C.sem("p3kv1")]
    sxs = C.sem("p3xs")
    shl = C.sem("p3hl")
    sms = C.sem("p3ms")
    gd = Guard(C)

    C.barrier(fence, engines=("tensor", "vector", "scalar", "gpsimd", "sync"))
    lnsems = dict(m=sm, v=sv, s=ss, g=sg, o2=so2)
    for tc in range(NTC):
        emit_ln(S, lnsems, tc, "h_scr")
    C.wait("scalar", sv, sv.n)
    C.wait("scalar", sg, sg.n)
    for fc in range(NFC):
        C.dma("scalar", scr["xspill"][fc, :, :], x_fm[:, fc * TOK:(fc + 1) * TOK], sp)
    h_all = arXb[:, 0:NFC * TOK]
    vchunk = arXb[:, 32768:40960]
    vout = arXb[:, 40960:45056]
    outb = arXb[:, 45056:47104]
    wbuf = [arXb[:, 47104 + i * 2048: 47104 + (i + 1) * 2048] for i in range(2)]
    C.wait("sync", sp, sp.n)
    C.wait("sync", so2[0], so2[0].n)
    C.wait("sync", so2[1], so2[1].n)
    for fc in range(NFC):
        C.dma("sync", h_all[:, fc * TOK:(fc + 1) * TOK], scr["h_scr"][fc, :, :], shl)
    C.dma("sync", ehmat, ein["ehmat"][:, :], shl)
    C.barrier([(shl, shl.n)], engines=("tensor", "vector", "scalar"))

    ldtgt3 = {}

    def load_wb(wname, oc, li):
        sl = li % 2
        gd.acquire("sync", ("wbuf", sl))
        for hf in range(2):
            C.dma("sync",
                  wbuf[sl][:, hf * 1024:(hf + 1) * 1024]
                  .rearrange("p (c i) -> p c i", c=8),
                  scr[wname].ap().rearrange("(c p) o -> c p o", p=128)
                  [hf * 8:(hf + 1) * 8, :, oc * 128:(oc + 1) * 128]
                  .rearrange("c p o -> p c o"), sld[sl])
        ldtgt3[sl] = sld[sl].n

    def proj3(li, tc, pb_i):
        w = wbuf[li % 2]
        C.wait("tensor", sld[li % 2], ldtgt3[li % 2])
        gd.acquire("tensor", ("bank", pb_i))
        for cc in range(NFC):
            C.op("tensor", lambda e, w=w, cc=cc, tc=tc, pb_i=pb_i:
                 e.matmul(bank(pb_i), w[:, ts(cc)],
                          h_all[:, cc * TOK + tc * CH: cc * TOK + (tc + 1) * CH],
                          start=(cc == 0), stop=(cc == NFC - 1)),
                 sem=(sm if cc == NFC - 1 else None))
        gd.use(("wbuf", li % 2), sm)

    li = 0
    for wi, wname, dst in ((0, "bWqb", "qb_scr"), (1, "bWkb", "kinb")):
        load_wb(wname, 0, li)
        for oc in range(NFC):
            if oc + 1 < NFC:
                load_wb(wname, oc + 1, li + 1)
            for tc in range(NTC):
                pb_i = 2 + (tc % 2)
                proj3(li, tc, pb_i)
                sl = tc % 2
                C.wait("vector", sm, sm.n)
                gd.acquire("vector", ("outb", sl))
                dst_bf = outb[:, sl * 512:(sl + 1) * 512]
                C.op("vector", lambda e, pb_i=pb_i, dst_bf=dst_bf, wi=wi, oc=oc:
                     e.tensor_scalar(dst_bf, bank(pb_i),
                                     bias_sb[:, wi * 16 + oc: wi * 16 + oc + 1],
                                     None, ALU.add), sem=sv)
                gd.use(("bank", pb_i), sv)
                C.wait("scalar", sv, sv.n)
                C.dma("scalar", scr[dst][ts(oc), tc * CH:(tc + 1) * CH], dst_bf,
                      sko2[sl])
                gd.use(("outb", sl), sko2[sl])
            li += 1
    for tc in range(NTC):
        load_wb("bWvb", 0, li)
        for oc in range(NFC):
            if oc + 1 < NFC:
                load_wb("bWvb", oc + 1, li + 1)
            pb_i = 2 + (oc % 2)
            proj3(li, tc, pb_i)
            C.wait("vector", sm, sm.n)
            gd.acquire("vector", ("vchunk", oc))
            C.op("vector", lambda e, pb_i=pb_i, oc=oc:
                 e.tensor_scalar(vchunk[:, oc * 512:(oc + 1) * 512], bank(pb_i),
                                 bias_sb[:, 32 + oc: 33 + oc], None, ALU.add),
                 sem=sv)
            gd.use(("bank", pb_i), sv)
            li += 1
        C.wait("tensor", sv, sv.n)
        for tt in range(4):
            for g4 in range(4):
                tb_i = 4 + (g4 % 2)
                gd.acquire("tensor", ("bank", tb_i))
                for m in range(4):
                    oc = g4 * 4 + m
                    C.op("tensor", lambda e, tb_i=tb_i, oc=oc, tt=tt, m=m:
                         e.transpose(bankb(tb_i)[:, 0:512][:, ts(m)],
                                     vchunk[:, oc * 512 + tt * 128:
                                            oc * 512 + (tt + 1) * 128],
                                     identb[:, :]),
                         sem=(sm if m == 3 else None))
                    gd.use(("vchunk", oc), sm)
                C.wait("vector", sm, sm.n)
                if g4 == 0:
                    gd.acquire("vector", ("vout",))
                C.op("vector", lambda e, tb_i=tb_i, g4=g4:
                     e.tensor_copy(vout[:, g4 * 512:(g4 + 1) * 512],
                                   bankb(tb_i)[:, 0:512]), sem=sv)
                gd.use(("bank", tb_i), sv)
            C.wait("scalar", sv, sv.n)
            C.dma("scalar", scr["vinb"][tc * CH + tt * 128: tc * CH + (tt + 1) * 128, :],
                  vout[:, 0:2048], sko2[0])
            gd.use(("vout",), sko2[0])
    C.wait("gpsimd", sko2[0], sko2[0].n)
    C.wait("gpsimd", sko2[1], sko2[1].n)
    _ag(C, scr["kinb"].ap(), scr["kgb"].ap(), G2, scc)
    _ag(C, scr["vinb"].ap(), scr["vgb"].ap(), G2, scc)
    C.barrier([(scc, 2), (sm, sm.n), (sv, sv.n), (ss, ss.n)],
              engines=("tensor", "scalar", "vector", "sync"))

    qsb = arXb[:, 0:8192]
    o_all = arXb[:, 8192:16384]
    kvr = [arXb[:, 16384 + i * 4096: 16384 + (i + 1) * 4096] for i in range(2)]
    p_sb = [arXb[:, 24576 + i * 512: 24576 + (i + 1) * 512] for i in range(4)]
    s_sb = [S["arX"][:, 13312 + i * 512: 13312 + (i + 1) * 512] for i in range(2)]
    masks8 = arXb[:, 28672:32768]
    rec_sb = S["arX"][:, 16384:16896]
    lnt = S["arX"][:, 16896:17408]
    kb_sb = S["arX"][:, 17664:18240]
    xcur = S["arX"][:, 18432:26624]
    wov = [arXb[:, 53248 + i * 2048: 53248 + (i + 1) * 2048] for i in range(2)]

    def g2ri(gkt):
        return g2rank_idx(128 * gkt)

    sqp3 = C.sem("p3qp")
    vq_t = vms_t = None
    for slot in range(NTC):
        ktn = KTN[slot]
        pieces = [(p0, min(p0 + 8, ktn)) for p0 in range(0, ktn, 8)]
        if slot == 0:
            gd.acquire("sync", ("qsb",))
            vq_t = C.dma("sync", qsb.rearrange("p (oc t) -> p oc t", oc=16),
                         scr["qb_scr"].ap().rearrange("(oc p) t -> p oc t", p=128)
                         [:, :, slot * CH:(slot + 1) * CH], sqp3)
            gd.acquire("sync", ("masks8",))
            C.dma("sync", masks8.rearrange("p (j x) -> p j x", j=8),
                  ein["mdyn"].ap()[slot].rearrange("j p x -> p j x"), sms)
            gd.acquire("sync", ("kb",))
            C.dma("sync", kb_sb[:, 0:ktn * 16].rearrange("p (i h) -> p i h", i=ktn),
                  ein["kbias"].ap()[slot, 0:ktn].rearrange("i p h -> p i h"), sms)
            vms_t = sms.n
        C.barrier([(sqp3, vq_t)], engines=("tensor",))
        C.barrier([(sms, vms_t)], engines=("tensor", "vector", "scalar"))

        def load_piece(pi, slot_buf, hg):
            p0, p1 = pieces[pi]
            L = p1 - p0
            sem = skv[slot_buf]
            gd.acquire("sync", ("kv", slot_buf))
            runs = []
            r0 = p0
            while r0 < p1:
                rk, idx = g2ri(r0)
                r1 = r0
                while r1 + 1 < p1 and g2ri(r1 + 1)[0] == rk:
                    r1 += 1
                runs.append((r0, r1 + 1, rk, idx))
                r0 = r1 + 1
            Lp = p1 - p0
            for (a, b, rk, idx) in runs:
                Lr = b - a
                for hh in range(2):
                    gh = hg * 2 + hh
                    C.dma("sync",
                          kvr[slot_buf][:, hh * Lp * 128 + (a - p0) * 128:
                                        hh * Lp * 128 + (b - p0) * 128],
                          scr["kgb"].ap()
                          .rearrange("(r c p) t -> r c p t", r=2, p=128)
                          [rk, gh, :, idx:idx + Lr * 128], sem)
                C.dma("sync",
                      kvr[slot_buf][:, 2048 + (a - p0) * 256:
                                    2048 + (a - p0) * 256 + Lr * 256]
                      .rearrange("p (l x) -> p l x", l=Lr),
                      scr["vgb"].ap()
                      .rearrange("(r l p) d -> r l p d", r=2, p=128)
                      [rk, idx // 128: idx // 128 + Lr, :,
                       (hg * 2) * 128:(hg * 2 + 2) * 128]
                      .rearrange("l p d -> p l d"), sem)
            return sem.n

        kvt = [None, None]
        kvt[0] = load_piece(0, 0, 0)
        for hg in range(8):
            units = []
            for pi, (p0, p1) in enumerate(pieces):
                for itl in range(p1 - p0):
                    for hh in range(2):
                        units.append((p0 + itl, hh, pi, pi % 2, itl, p1 - p0))
            ss_t3 = {}

            def emit_qk3(ui, u):
                it, hh, pi, sbuf_i, itl, Lp_ = u
                gh = hg * 2 + hh
                sti = (it * 2 + hh) % 4
                gd.acquire("tensor", ("st", sti))
                qh = qsb[:, gh * 512:(gh + 1) * 512]
                C.op("tensor", lambda e, sti=sti, sbuf_i=sbuf_i, itl=itl,
                     hh=hh, qh=qh, Lp_=Lp_:
                     e.matmul(bank(sti),
                              kvr[sbuf_i][:, hh * Lp_ * 128 + itl * 128:
                                          hh * Lp_ * 128 + (itl + 1) * 128],
                              qh, start=True, stop=False))
                C.op("tensor", lambda e, sti=sti, gh=gh:
                     e.matmul(bank(sti), ehmat[:, ts(gh)], crow_sb[:, :],
                              start=False, stop=True), sem=sm)
                kbv = kb_sb[:, it * 16 + gh: it * 16 + gh + 1]
                if it in MASKPOS[slot]:
                    mj = MASKPOS[slot].index(it)
                    C.wait("vector", sm, sm.n)
                    gd.acquire("vector", ("ssb", it % 2))
                    C.op("vector", lambda e, sti=sti, mj=mj, it=it:
                         e.tensor_tensor(s_sb[it % 2], bank(sti),
                                         masks8[:, mj * 512:(mj + 1) * 512],
                                         ALU.add), sem=sv)
                    gd.use(("st", sti), sv)
                    C.wait("scalar", sv, sv.n)
                    gd.acquire("scalar", ("p", (it * 2 + hh) % 4))
                    ss_t3[ui] = C.op("scalar", lambda e, it=it, hh=hh, kbv=kbv:
                                     e.activation(p_sb[(it * 2 + hh) % 4],
                                                  s_sb[it % 2],
                                                  AF.Exp, bias=kbv, scale=ISQD),
                                     sem=ss)
                    gd.use(("ssb", it % 2), ss)
                else:
                    C.wait("scalar", sm, sm.n)
                    gd.acquire("scalar", ("p", (it * 2 + hh) % 4))
                    ss_t3[ui] = C.op("scalar", lambda e, sti=sti, it=it, hh=hh,
                                     kbv=kbv:
                                     e.activation(p_sb[(it * 2 + hh) % 4],
                                                  bank(sti),
                                                  AF.Exp, bias=kbv, scale=ISQD),
                                     sem=ss)
                    gd.use(("st", sti), ss)

            def emit_pv3(ui, u, last_of_piece):
                it, hh, pi, sbuf_i, itl, Lp_ = u
                pb = p_sb[(it * 2 + hh) % 4]
                if it == 0:
                    gd.acquire("tensor", ("bank", 4 + hh))
                    gd.acquire("tensor", ("bank", 6 + hh))
                C.wait("tensor", ss, ss_t3[ui])
                C.op("tensor", lambda e, hh=hh, sbuf_i=sbuf_i, itl=itl, pb=pb,
                     it=it, Lp_=Lp_, ktn=ktn:
                     e.matmul(bank(4 + hh),
                              kvr[sbuf_i][:, 2048 + itl * 256 + hh * 128:
                                          2048 + itl * 256 + (hh + 1) * 128],
                              pb, start=(it == 0), stop=(it == ktn - 1)))
                C.op("tensor", lambda e, hh=hh, pb=pb, it=it, ktn=ktn:
                     e.matmul(bank(6 + hh), onesb[:, :], pb,
                              start=(it == 0), stop=(it == ktn - 1)), sem=sm)
                gd.use(("p", (it * 2 + hh) % 4), sm)
                if last_of_piece:
                    gd.use(("kv", sbuf_i), sm)

            for ui, u in enumerate(units):
                it, hh, pi, sbuf_i, itl, Lp_ = u
                if hh == 0 and itl == 0:
                    C.wait("tensor", skv[sbuf_i], kvt[sbuf_i])
                emit_qk3(ui, u)
                if ui >= 2:
                    up = units[ui - 2]
                    emit_pv3(ui - 2, up, up[2] != units[ui - 1][2])
                if hh == 1 and itl == 0:
                    if pi + 1 < len(pieces):
                        kvt[(pi + 1) % 2] = load_piece(pi + 1, (pi + 1) % 2, hg)
                    elif hg + 1 < 8:
                        kvt[0] = load_piece(0, 0, hg + 1)
            emit_pv3(len(units) - 2, units[-2], units[-2][2] != units[-1][2])
            emit_pv3(len(units) - 1, units[-1], True)
            for hh in range(2):
                gh = hg * 2 + hh
                C.wait("scalar", sm, sm.n)
                gd.acquire("scalar", ("rec",))
                C.op("scalar", lambda e, hh=hh: e.activation(lnt, bank(6 + hh), AF.Ln))
                t_rec = C.op("scalar", lambda e:
                             e.activation(rec_sb, lnt, AF.Exp, scale=-1.0), sem=ss)
                gd.use(("bank", 6 + hh), ss)
                C.wait("vector", ss, t_rec)
                gd.acquire("vector", ("oall", gh))
                C.op("vector", lambda e, gh=gh, hh=hh:
                     e.tensor_tensor(o_all[:, gh * 512:(gh + 1) * 512], bank(4 + hh),
                                     rec_sb, ALU.mult), sem=sv)
                gd.use(("bank", 4 + hh), sv)
                gd.use(("rec",), sv)
        gd.use(("qsb",), sm)
        gd.use(("masks8",), sv)
        gd.use(("kb",), ss)
        if slot + 1 < NTC:
            ktn2 = KTN[slot + 1]
            gd.acquire("scalar", ("qsb",))
            vq_t = C.dma("scalar", qsb.rearrange("p (oc t) -> p oc t", oc=16),
                         scr["qb_scr"].ap().rearrange("(oc p) t -> p oc t", p=128)
                         [:, :, (slot + 1) * CH:(slot + 2) * CH], sqp3)
            gd.acquire("scalar", ("masks8",))
            C.dma("scalar", masks8.rearrange("p (j x) -> p j x", j=8),
                  ein["mdyn"].ap()[slot + 1].rearrange("j p x -> p j x"), sms)
            gd.acquire("scalar", ("kb",))
            C.dma("scalar",
                  kb_sb[:, 0:ktn2 * 16].rearrange("p (i h) -> p i h", i=ktn2),
                  ein["kbias"].ap()[slot + 1, 0:ktn2].rearrange("i p h -> p i h"),
                  sms)
            vms_t = sms.n
        C.wait("tensor", sv, sv.n)
        C.wait("sync", sxs, sxs.n)
        gd.acquire("sync", ("xcur",))
        vx = C.dma("sync", xcur.rearrange("p (oc t) -> p oc t", oc=16),
                   scr["xspill"].ap().rearrange("oc p t -> p oc t")
                   [:, :, slot * CH:(slot + 1) * CH], sq_)
        C.barrier([(sq_, vx)], engines=("vector",))
        def wov3_load(oc):
            sl = oc % 2
            gd.acquire("sync", ("wov", sl))
            for hf in range(2):
                C.dma("sync",
                      wov[sl][:, hf * 1024:(hf + 1) * 1024]
                      .rearrange("p (c i) -> p c i", c=8),
                      scr["bWob"].ap().rearrange("(c p) o -> c p o", p=128)
                      [hf * 8:(hf + 1) * 8, :, oc * 128:(oc + 1) * 128]
                      .rearrange("c p o -> p c o"), sld[sl])
            return sld[sl].n

        wovt3 = [None, None]
        wovt3[0] = wov3_load(0)
        for oc in range(NFC):
            sl = oc % 2
            if oc + 1 < NFC:
                wovt3[(oc + 1) % 2] = wov3_load(oc + 1)
            C.wait("tensor", sld[sl], wovt3[sl])
            pb_i = 2 + sl
            gd.acquire("tensor", ("bank", pb_i))
            for dc in range(NFC):
                C.op("tensor", lambda e, pb_i=pb_i, sl=sl, dc=dc:
                     e.matmul(bank(pb_i), wov[sl][:, ts(dc)],
                              o_all[:, dc * 512:(dc + 1) * 512],
                              start=(dc == 0), stop=(dc == NFC - 1)),
                     sem=(sm if dc == NFC - 1 else None))
            gd.use(("wov", sl), sm)
            if oc == NFC - 1:
                for hh2 in range(NH):
                    gd.use(("oall", hh2), sm)
            C.wait("vector", sm, sm.n)
            xslice = xcur[:, oc * 512:(oc + 1) * 512]
            C.op("vector", lambda e, pb_i=pb_i, xslice=xslice, oc=oc:
                 e.scalar_tensor_tensor(xslice, bank(pb_i),
                                        bias_sb[:, 48 + oc: 49 + oc], xslice,
                                        ALU.add, ALU.add), sem=sv)
            gd.use(("bank", pb_i), sv)
            gd.use(("st", pb_i), sv)
        C.wait("sync", sv, sv.n)
        for oc in range(NFC):
            C.dma("sync", scr["xspill"][oc, :, slot * CH:(slot + 1) * CH],
                  xcur[:, oc * 512:(oc + 1) * 512], sxs)
        gd.use(("xcur",), sxs)
    sxr = C.sem("p3xr")
    C.wait("sync", sxs, sxs.n)
    for fc in range(NFC):
        C.dma("sync", S["x_fm"][:, fc * TOK:(fc + 1) * TOK], scr["xspill"][fc, :, :],
              sxr)
    return [(sxr, sxr.n), (sm, sm.n), (sv, sv.n), (ss, ss.n)]


def phase_P9(S, fence):
    C, bank = S["C"], S["bank"]
    x_fm, xout = S["x_fm"], S["xout"]
    identf = S["cb"]["identf"]
    sd = [C.sem(f"p9d{i}") for i in range(2)]
    sm = C.sem("p9m")
    sv = C.sem("p9v")
    ot = [S["arAf"][:, 0:2048], S["arAf"][:, 2048:4096]]
    C.barrier(fence, engines=("tensor", "vector", "scalar", "sync", "gpsimd"))
    gi = 0
    for tt in range(16):
        bi = tt % 2
        buf = ot[bi]
        for g in range(4):
            bk = bank(gi % 2)
            if gi >= 2:
                C.wait("tensor", sv, gi - 1)
            for f4 in range(4):
                fc = g * 4 + f4
                C.op("tensor", lambda e, bk=bk, fc=fc, f4=f4, tt=tt:
                     e.transpose(bk[:, ts(f4)],
                                 x_fm[:, fc * TOK + tt * 128: fc * TOK + (tt + 1) * 128],
                                 identf[:, :]),
                     sem=(sm if f4 == 3 else None))
            C.wait("vector", sm, sm.n)
            if tt >= 2 and g == 0:
                C.wait("vector", sd[bi], 16 * (tt // 2))
            C.op("vector", lambda e, bk=bk, buf=buf, g=g:
                 e.tensor_copy(buf[:, g * 512:(g + 1) * 512], bk), sem=sv, drain=True)
            gi += 1
        C.wait("sync", sv, sv.n)
        C.dma("sync", xout[ts(tt), :], buf, sd[bi])
    C.wait("sync", sd[0], sd[0].n)
    C.wait("sync", sd[1], sd[1].n)


def kernel(**inputs):
    xfull = np.ascontiguousarray(np.asarray(inputs["L0_attn_a_x"], dtype=np.float32))
    W = {k: np.asarray(inputs[k], dtype=np.float32) for k in
         ("L0_attn_a_Wq", "L0_attn_a_Wk", "L0_attn_a_Wv", "L0_attn_a_Wo",
          "L0_attn_b_Wq", "L0_attn_b_Wk", "L0_attn_b_Wv", "L0_attn_b_Wo",
          "L0_ffn_w1", "L0_ffn_w2", "L0_ffn_w3",
          "L0_attn_b_bq", "L0_attn_b_bk", "L0_attn_b_bv", "L0_attn_b_bo")}
    fw1p = np.zeros((6144, D), dtype=np.float32)
    fw1p[:HID] = W["L0_ffn_w1"]
    fw2p = np.zeros((6144, D), dtype=np.float32)
    fw2p[:HID] = W["L0_ffn_w2"]
    fw3p = np.zeros((D, 6144), dtype=np.float32)
    fw3p[:, :HID] = W["L0_ffn_w3"]
    nc = build_nc(_PHASES)
    in_maps = []
    for c in range(8):
        b = c // 2
        pos = np.concatenate([np.arange(t0, t0 + CH) for t0 in chunk_starts(c)])
        m = {"xin": np.ascontiguousarray(xfull[b, pos, :])}
        m["aWq"] = np.ascontiguousarray(W["L0_attn_a_Wq"][256 * c:256 * (c + 1), :])
        m["aWk"] = np.ascontiguousarray(W["L0_attn_a_Wk"][128 * (c % 4):128 * (c % 4 + 1), :])
        m["aWv"] = np.ascontiguousarray(W["L0_attn_a_Wv"][128 * (c % 4):128 * (c % 4 + 1), :])
        m["aWo"] = np.ascontiguousarray(W["L0_attn_a_Wo"][256 * c:256 * (c + 1), :])
        for k, nm in (("bWq", "L0_attn_b_Wq"), ("bWk", "L0_attn_b_Wk"),
                      ("bWv", "L0_attn_b_Wv"), ("bWo", "L0_attn_b_Wo")):
            m[k] = np.ascontiguousarray(W[nm][256 * c:256 * (c + 1), :])
        m["fw1"] = np.ascontiguousarray(fw1p[768 * c:768 * (c + 1), :])
        m["fw2"] = np.ascontiguousarray(fw2p[768 * c:768 * (c + 1), :])
        m["fw3"] = np.ascontiguousarray(fw3p[:, 768 * c:768 * (c + 1)])
        for k, nm in (("bbq", "L0_attn_b_bq"), ("bbk", "L0_attn_b_bk"),
                      ("bbv", "L0_attn_b_bv"), ("bbo", "L0_attn_b_bo")):
            m[k] = np.ascontiguousarray(W[nm].reshape(16, 128))
        hc = host_consts(c)
        for k in ("ropecos", "ropesin", "rotT", "identf", "identb", "onesb", "crow",
                  "ehmat", "masks", "kbias", "kvoff", "self4", "mdyn"):
            m[k] = hc[k]
        in_maps.append(m)
    res = run_bass_kernel_spmd(nc, in_maps, core_ids=list(range(8)))
    out = np.empty((B, S, D), dtype=np.float32)
    for c in range(8):
        b = c // 2
        pos = np.concatenate([np.arange(t0, t0 + CH) for t0 in chunk_starts(c)])
        out[b, pos, :] = res.results[c]["xout"]
    return out


# version 09
# speedup vs baseline: 1.3427x; 1.0168x over previous
import math
from contextlib import ExitStack

import numpy as np
import ml_dtypes

import concourse.bass as bass
import concourse.mybir as mybir
from concourse.bass_utils import run_bass_kernel_spmd

F32 = mybir.dt.float32
BF16 = mybir.dt.bfloat16
I32 = mybir.dt.int32
AF = mybir.ActivationFunctionType
ALU = mybir.AluOpType
AX = mybir.AxisListType
bfdt = ml_dtypes.bfloat16

B, S, D = 4, 4096, 2048
HID = 5632
NH, NKV, HD = 16, 4, 128
TOK = 2048
CH = 512
NTC = TOK // CH
NFC = D // 128
NHC = HID // 128
SQD = math.sqrt(128.0)
ISQD = 1.0 / SQD
ACONST = -16.0
KTN = [12, 16, 28, 32]
NEG = -1.0e30
LN_EPS = 1e-5
Q_EPS = 1e-8
NGRP = 8
FFN_GRPS = [list(range(g, min(g + 6, NHC))) for g in range(0, NHC, 6)]

PHASES_ALL = ("P0", "PW", "PG", "P1", "P2", "P3", "P4", "P9")
_PHASES = PHASES_ALL


def chunk_starts(core):
    return [0, 512, 3072, 3584] if core % 2 == 0 else [1024, 1536, 2048, 2560]


def _maskpos():
    re_ = [(t0 + CH) // 128 for t0 in chunk_starts(0)]
    ro_ = [(t0 + CH) // 128 for t0 in chunk_starts(1)]
    out = []
    for s_i in range(NTC):
        out.append(sorted(set(range(re_[s_i] - 4, re_[s_i])) |
                          set(range(ro_[s_i] - 4, ro_[s_i]))))
    return out


MASKPOS = _maskpos()

def g2rank_idx(g):
    if g < 1024:
        return 0, g
    if g < 3072:
        return 1, g - 1024
    return 0, g - 2048


def _slopes16():
    start = 2.0 ** (-(2.0 ** -(math.log2(16) - 3)))
    return np.array([start * start ** i for i in range(16)], dtype=np.float32)


def host_consts(core):
    c = {}
    pos = np.concatenate([np.arange(t0, t0 + CH) for t0 in chunk_starts(core)])
    ar = np.arange(0, HD, 2, dtype=np.float32) / np.float32(HD)
    inv = (np.float32(1.0) / (np.float32(10000.0) ** ar)).astype(np.float32)
    t = np.arange(S, dtype=np.float32)
    freqs = (t[:, None] * inv[None, :]).astype(np.float32)
    emb = np.concatenate([freqs, freqs], axis=1)
    c["ropecos"] = np.ascontiguousarray(np.cos(emb).astype(np.float32).T[:, pos])
    c["ropesin"] = np.ascontiguousarray(np.sin(emb).astype(np.float32).T[:, pos])
    R = np.zeros((HD, HD), dtype=np.float32)
    for d in range(64):
        R[d, d + 64] = -1.0
        R[64 + d, d] = 1.0
    c["rotT"] = np.ascontiguousarray(R.T).astype(bfdt)
    c["identf"] = np.eye(128, dtype=np.float32)
    c["identb"] = np.eye(128, dtype=bfdt)
    c["onesb"] = np.ones((128, 128), dtype=bfdt)
    slopes = _slopes16()
    j = np.arange(CH, dtype=np.float32)
    c["crow"] = (-(SQD) * slopes[:, None] * j[None, :]).astype(bfdt)
    eh = np.zeros((128, NH * 128), dtype=np.float32)
    for h in range(NH):
        eh[h, h * 128:(h + 1) * 128] = 1.0
    c["ehmat"] = eh.astype(bfdt)
    kk = np.arange(128)[:, None]
    jj = np.arange(CH)[None, :]
    masks = np.concatenate([(kk + d > jj).astype(np.float32) * NEG
                            for d in (0, 128, 256, 384)], axis=1)
    c["masks"] = np.ascontiguousarray(masks.astype(np.float32))
    starts = chunk_starts(core)
    kb = np.zeros((NTC, 36, 128, NH), dtype=np.float32)
    kkv = np.arange(128, dtype=np.float32)
    for s_i, t0 in enumerate(starts):
        R_real = (t0 + CH) // 128
        for it in range(KTN[s_i]):
            if it < R_real:
                gk = 128 * it + kkv
                kb[s_i, it, :, :] = slopes[None, :] * (gk[:, None] - t0)
            else:
                kb[s_i, it] = NEG
    c["kbias"] = kb
    c["maskpos"] = MASKPOS
    mdyn = np.zeros((NTC, 8, 128, CH), dtype=np.float32)
    kk2 = np.arange(128)[:, None]
    jj2 = np.arange(CH)[None, :]
    for s_i, t0 in enumerate(starts):
        R_real = (t0 + CH) // 128
        for j, it in enumerate(MASKPOS[s_i]):
            if R_real - 4 <= it < R_real:
                d = 128 * it - t0
                mdyn[s_i, j] = (kk2 + d > jj2).astype(np.float32) * NEG
    c["mdyn"] = mdyn.astype(bfdt)
    off = np.zeros((NTC, 36, 2), dtype=np.int64)
    c["kvoff"] = off.astype(np.int32)
    c["self4"] = np.concatenate(
        [np.tile((np.arange(4) == w).astype(np.float32)[:, None], (1, 128))
         for w in range(3)], axis=1)
    return c


ENGS = ("tensor", "vector", "scalar", "gpsimd", "sync")


class Sem:
    __slots__ = ("h", "n", "name")

    def __init__(self, h, name):
        self.h, self.n, self.name = h, 0, name

    def bump(self, k=1):
        self.n += k
        return self.n


class Ctx:
    def __init__(self, nc, st):
        self.nc = nc
        self.st = st
        self.q = {e: [] for e in ENGS}
        self._semn = 0
        self.chain = {e: None for e in ("vector", "scalar", "gpsimd")}
        self.last = {e: None for e in ("vector", "scalar", "gpsimd")}

    def sem(self, name=None):
        self._semn += 1
        h = self.st.enter_context(self.nc.semaphore(name or f"s{self._semn}"))
        return Sem(h, name)

    def sb(self, name, shape, dt):
        return self.st.enter_context(self.nc.sbuf_tensor(name, shape, dt))

    def emit(self, eng, fn):
        self.q[eng].append(fn)

    def wait(self, eng, sem, val):
        if val > 0:
            self.emit(eng, lambda e, s=sem.h, v=val: e.wait_ge(s, v))

    def op(self, eng, fn, sem=None, k=1, drain=False):
        tgt = sem.bump(k) if sem is not None else None
        if eng in self.chain:
            if self.chain[eng] is None:
                self.chain[eng] = self.sem("ch_" + eng)
            ch = self.chain[eng]
            prev = self.last[eng]
            if sem is not None:
                semh, tv, delta = sem.h, tgt, k
            else:
                tv = ch.bump(1)
                semh, delta = ch.h, 1
            if prev is not None:
                self.emit(eng, lambda e, s=prev[0], v=prev[1]: e.wait_ge(s, v))

            def run(e, fn=fn, semh=semh, delta=delta):
                ins = fn(e)
                (ins if ins is not None else e.nop()).then_inc(semh, delta)
            self.emit(eng, run)
            self.last[eng] = (semh, tv)
        else:
            def run(e, fn=fn, sem=sem, k=k, drain=drain):
                ins = fn(e)
                if sem is not None:
                    if drain:
                        e.drain().then_inc(sem.h, k)
                    elif ins is not None:
                        ins.then_inc(sem.h, k)
                    else:
                        e.nop().then_inc(sem.h, k)
            self.emit(eng, run)
        return tgt

    def dma(self, eng, out, in_, sem, **kw):
        tgt = sem.bump(16)
        self.emit(eng, lambda e, out=out, in_=in_, kw=kw, s=sem.h:
                  e.dma_start(out=out, in_=in_, **kw).then_inc(s, 16))
        return tgt

    def barrier(self, sems_vals, engines=ENGS):
        for e in engines:
            for sem, v in sems_vals:
                self.wait(e, sem, v)


def ts(i, n=128):
    return slice(i * n, (i + 1) * n)


class Guard:

    def __init__(self, C):
        self.C = C
        self.m = {}

    def use(self, key, sem):
        self.m.setdefault(key, {})[sem] = sem.n

    def acquire(self, eng, key):
        for sem, v in self.m.pop(key, {}).items():
            self.C.wait(eng, sem, v)


def _ag(C, in_ap, out_ap, groups, sem):
    tgt = sem.bump(1)
    C.emit("gpsimd", lambda e, i=in_ap, o=out_ap, g=groups, s=sem.h:
           e.collective_compute("AllGather", ALU.bypass, replica_groups=g,
                                ins=[i.opt()], outs=[o.opt()]).then_inc(s, 1))
    return tgt


G8 = [[0, 1, 2, 3, 4, 5, 6, 7]]
G4 = [[0, 1, 2, 3], [4, 5, 6, 7]]
G2 = [[0, 1], [2, 3], [4, 5], [6, 7]]


def build_nc(phases=None):
    if phases is None:
        phases = _PHASES
    nc = bass.Bass("TRN2", target_bir_lowering=False)
    st = ExitStack()
    C = Ctx(nc, st)

    ein = {}

    def EIN(name, shape, dt):
        ein[name] = nc.dram_tensor(name, shape, dt, kind="ExternalInput")
        return ein[name]

    xin = EIN("xin", [TOK, D], F32)
    for nm, shape in [("aWq", [256, D]), ("aWk", [128, D]), ("aWv", [128, D]),
                      ("aWo", [256, D]), ("bWq", [256, D]), ("bWk", [256, D]),
                      ("bWv", [256, D]), ("bWo", [256, D]),
                      ("fw1", [768, D]), ("fw2", [768, D]), ("fw3", [D, 768]),
                      ("bbq", [16, 128]), ("bbk", [16, 128]), ("bbv", [16, 128]),
                      ("bbo", [16, 128]),
                      ("ropecos", [128, TOK]), ("ropesin", [128, TOK]),
                      ("identf", [128, 128]), ("masks", [128, 4 * CH]),
                      ("kbias", [NTC, 36, 128, NH]), ("self4", [4, 384])]:
        EIN(nm, shape, F32)
    for nm, shape in [("rotT", [128, 128]), ("identb", [128, 128]),
                      ("onesb", [128, 128]), ("crow", [16, CH]),
                      ("ehmat", [128, NH * 128]), ("mdyn", [NTC, 8, 128, CH])]:
        EIN(nm, shape, BF16)
    EIN("kvoff", [NTC, 36, 2], I32)

    xout = nc.dram_tensor("xout", [TOK, D], F32, kind="ExternalOutput")

    DR = lambda name, shape, dt: nc.dram_tensor(name, shape, dt)
    scr = {}
    for nm, shape, dt in [
        ("aWqT", [D, D], BF16), ("aWkT", [512, D], BF16),
        ("aWvT", [512, D], BF16), ("aWoT", [D, D], BF16),
        ("bWqb", [D, D], BF16), ("bWkb", [D, D], BF16), ("bWvb", [D, D], BF16),
        ("bWob", [D, D], BF16),
        ("qw1", [6144, D], BF16), ("qw2", [6144, D], BF16), ("qw3", [6144, D], BF16),
        ("awq_in", [256, D], BF16), ("awk_in", [128, D], BF16),
        ("awv_in", [128, D], BF16), ("awo_in", [256, D], BF16),
        ("bwq_in", [256, D], BF16), ("bwk_in", [256, D], BF16),
        ("bwv_in", [256, D], BF16), ("bwo_in", [256, D], BF16),
        ("qw1_in", [768, D], BF16), ("qw2_in", [768, D], BF16),
        ("qw3_in", [768, D], BF16),
        ("gpart_in", [1, 4], F32), ("gpart_out", [8, 4], F32),
        ("h_scr", [NFC, 128, TOK], BF16),
        ("qa_scr", [D, TOK], BF16), ("qb_scr", [D, TOK], BF16),
        ("xspill", [NFC, 128, TOK], F32),
        ("kina", [512, TOK], BF16), ("vina", [TOK, 512], BF16),
        ("kga", [2 * 512, TOK], BF16), ("vga", [2 * TOK, 512], BF16),
        ("kinb", [D, TOK], BF16), ("vinb", [TOK, D], BF16),
        ("kgb", [2 * D, TOK], BF16), ("vgb", [2 * TOK, D], BF16),
    ]:
        scr[nm] = DR(nm, shape, dt)

    cb = {}
    for nm, shape, dt in [("rotT", [128, 128], BF16), ("identf", [128, 128], F32),
                          ("identb", [128, 128], BF16), ("onesb", [128, 128], BF16)]:
        cb[nm] = C.sb("c_" + nm, shape, dt)
    bias_sb = C.sb("bias_sb", [128, 4 * 16], F32)
    crow_sb = C.sb("crow_sb", [128, CH], BF16)

    arX = C.sb("arX", [128, NFC * TOK], F32)
    arXb = arX[:, :].bitcast(BF16)
    arA = C.sb("arA", [128, 8192], BF16)
    arAf = arA[:, :].bitcast(F32)
    arB = C.sb("arB", [128, 4096], BF16)
    arBf = arB[:, :].bitcast(F32)
    arC = C.sb("arC", [128, 24576], BF16)
    arCf = arC[:, :].bitcast(F32)
    psA = st.enter_context(nc.psum_tensor("psA", [128, 4096], F32))
    psAb = psA[:, :].bitcast(BF16)

    def bank(b, n=1):
        return psA[:, b * 512:(b + n) * 512]

    def bankb(b, n=1):
        return psAb[:, b * 1024:(b + n) * 1024]

    constv = C.sb("constv", [128, 4], F32)
    sCN = C.sem("cn")
    for i, val in enumerate((0.0, LN_EPS, ACONST)):
        C.op("vector", lambda e, i=i, val=val: e.memset(constv[:, i:i + 1], val),
             sem=(sCN if i == 2 else None), drain=(i == 2))
        nc.const_aps.register(F32, val, constv[:, i:i + 1])
    C.wait("scalar", sCN, 1)
    C.wait("tensor", sCN, 1)

    sLD = C.sem("ld")
    for nm in ("rotT", "identf", "identb", "onesb"):
        C.dma("gpsimd", cb[nm][:, :], ein[nm][:, :], sLD)
    sMS = C.sem("ms")
    vms = C.op("vector", lambda e: e.memset(crow_sb[:, :], 0.0), sem=sMS, drain=True)
    C.wait("gpsimd", sMS, vms)
    C.dma("gpsimd", crow_sb[0:16, :], ein["crow"][:, :], sLD)
    vLD = sLD.n

    S = dict(nc=nc, C=C, ein=ein, scr=scr, cb=cb, bias_sb=bias_sb,
             crow_sb=crow_sb, x_fm=arX, arX=arX, arXb=arXb, arA=arA, arAf=arAf,
             arB=arB, arBf=arBf, arC=arC, arCf=arCf, bank=bank, bankb=bankb,
             sLD=sLD, vLD=vLD, xin=xin, xout=xout)

    fence = phase_P0(S)
    fa, fb = phase_PW(S) if "PW" in phases else ([], [])
    fence_g = phase_PG(S) if "PG" in phases else []
    if "P1" in phases:
        fence = phase_P1(S, fence + fa)
    if "P2" in phases:
        fence = phase_P2(S, fence + fence_g, "f2")
    if "P3" in phases:
        fence = phase_P3(S, fence + fb)
    if "P4" in phases:
        fence = phase_P2(S, fence, "f4")
    phase_P9(S, fence)

    block = st.enter_context(nc.Block())

    def _mk(eng_name):
        def _run(e):
            for fn in C.q[eng_name]:
                fn(e)
        return _run

    block.tensor(_mk("tensor"))
    block.vector(_mk("vector"))
    block.scalar(_mk("scalar"))
    block.gpsimd(_mk("gpsimd"))
    block.sync(_mk("sync"))

    st.close()
    return nc


def phase_P0(S):
    C, bank = S["C"], S["bank"]
    x_fm, xin = S["x_fm"], S["xin"]
    identf = S["cb"]["identf"]
    sd = [C.sem(f"p0d{i}") for i in range(2)]
    sm = C.sem("p0m")
    sv = C.sem("p0v")
    xt = [S["arAf"][:, 0:2048], S["arAf"][:, 2048:4096]]
    C.wait("tensor", S["sLD"], S["vLD"])
    gi = 0
    for tt in range(16):
        bi = tt % 2
        buf = xt[bi]
        if tt >= 2:
            C.wait("sync", sm, 4 * (tt - 1))
        vland = C.dma("sync", buf, xin[ts(tt), :], sd[bi])
        C.wait("tensor", sd[bi], vland)
        for g in range(4):
            bk = bank(gi % 2)
            if gi >= 2:
                C.wait("tensor", sv, gi - 1)
            for f4 in range(4):
                fc = g * 4 + f4
                C.op("tensor", lambda e, bk=bk, buf=buf, fc=fc, f4=f4, identf=identf:
                     e.transpose(bk[:, ts(f4)], buf[:, ts(fc)], identf[:, :]),
                     sem=(sm if f4 == 3 else None))
            C.wait("vector", sm, sm.n)
            out_ap = x_fm[:, g * 4 * TOK:(g * 4 + 4) * TOK] \
                .rearrange("p (f l) -> p f l", f=4)[:, :, tt * 128:(tt + 1) * 128]
            in_ap = bk[:, :].rearrange("p (f l) -> p f l", f=4)
            C.op("vector", lambda e, o=out_ap, i=in_ap: e.tensor_copy(o, i),
                 sem=sv, drain=True)
            gi += 1
    return [(sv, sv.n)]


def phase_PW(S):
    C, ein, scr, cb, bankb = S["C"], S["ein"], S["scr"], S["cb"], S["bankb"]
    identb, identf = cb["identb"], cb["identf"]
    sdw = C.sem("pwd")
    smw = C.sem("pwm")
    svw = C.sem("pwv")
    sso = C.sem("pwo")
    sgw = C.sem("pwg")
    wb = S["arB"][:, 0:2048]
    tb = S["arB"][:, 2048:4096]
    bk = bankb(2)[:, 0:512]
    C.wait("tensor", S["sLD"], S["vLD"])

    def prep_tile(src_ap, dst_ap):
        vw = sso.n
        C.wait("gpsimd", smw, smw.n)
        vd = C.dma("gpsimd", wb, src_ap, sdw)
        C.wait("tensor", sdw, vd)
        for g in range(4):
            C.wait("tensor", svw, svw.n)
            for f4 in range(4):
                fc = g * 4 + f4
                C.op("tensor", lambda e, fc=fc, f4=f4:
                     e.transpose(bk[:, ts(f4)], wb[:, ts(fc)], identb[:, :]),
                     sem=(smw if f4 == 3 else None))
            C.wait("vector", smw, smw.n)
            if g == 0:
                C.wait("vector", sso, vw)
            C.op("vector", lambda e, g=g:
                 e.tensor_copy(tb[:, g * 512:(g + 1) * 512], bk), sem=svw, drain=True)
        C.wait("sync", svw, svw.n)
        return C.dma("sync", dst_ap, tb, sso)

    for nm, inn, agout, ntile, groups in [
        ("aWk", "awk_in", "aWkT", 1, G4), ("aWv", "awv_in", "aWvT", 1, G4),
        ("aWq", "awq_in", "aWqT", 2, G8), ("aWo", "awo_in", "aWoT", 2, G8),
    ]:
        for ot in range(ntile):
            vst = prep_tile(ein[nm][ts(ot), :], scr[inn][ts(ot), :])
        C.wait("gpsimd", sso, vst)
        _ag(C, scr[inn].ap(), scr[agout].ap(), groups, sgw)
    for nm, inn, agout in [("bWq", "bwq_in", "bWqb"), ("bWk", "bwk_in", "bWkb"),
                           ("bWv", "bwv_in", "bWvb"), ("bWo", "bwo_in", "bWob")]:
        vc = C.dma("gpsimd", scr[inn][:, :], ein[nm][:, :], sdw)
        C.wait("gpsimd", sdw, vc)
        _ag(C, scr[inn].ap(), scr[agout].ap(), G8, sgw)
    sbb = C.sem("pwb")
    ssb = C.sem("pwbs")
    bbs = S["arBf"][0:16, 0:128]
    pb = S["bank"](2)[:, 0:16]
    for wi, nm in enumerate(("bbq", "bbk", "bbv", "bbo")):
        C.wait("sync", smw, smw.n)
        vb = C.dma("sync", bbs, ein[nm][:, :], sbb)
        C.wait("tensor", sbb, vb)
        C.wait("tensor", ssb, ssb.n)
        C.op("tensor", lambda e: e.transpose(pb, bbs, identf[0:16, 0:16]), sem=smw)
        C.wait("scalar", smw, smw.n)
        C.op("scalar", lambda e, wi=wi:
             e.copy(S["bias_sb"][:, wi * 16:(wi + 1) * 16], pb), sem=ssb, drain=True)
    fa = [(sgw, sgw.n)]
    fb = [(sgw, sgw.n), (ssb, ssb.n)]
    S["pw_done"] = [(sgw, sgw.n), (ssb, ssb.n), (svw, svw.n), (smw, smw.n)]
    return fa, fb


def phase_PG(S):
    C, ein, scr, cb = S["C"], S["ein"], S["scr"], S["cb"]
    identb, identf = cb["identb"], cb["identf"]
    bank, bankb = S["bank"], S["bankb"]
    sdq = C.sem("pgd")
    ssc = C.sem("pgs")
    svq = C.sem("pgv")
    sgp = C.sem("pgp")
    smq = C.sem("pgm")
    sgg = C.sem("pgg")
    sst = C.sem("pgt")
    wf = S["arCf"][:, 0:2048]
    mp = S["arC"][:, 4096:6144]
    mn = S["arC"][:, 6144:8192]
    qt = S["arC"][:, 8192:10240]
    tb = S["arC"][:, 10240:12288]
    acc = C.sb("pg_acc", [128, 32], F32)
    red = C.sb("pg_red", [128, 4], F32)
    gp_sb = C.sb("pg_gp", [8, 4], F32)
    tot = C.sb("pg_tot", [4, 1], F32)
    thr4 = C.sb("pg_thr4", [4, 1], F32)
    nthr4 = C.sb("pg_nthr4", [4, 1], F32)
    thr_sb = C.sb("pg_thr", [128, 8], F32)
    self4 = C.sb("pg_self4", [4, 384], F32)
    bk = bankb(6)[:, 0:512]
    pr = bank(7)[:, 0:128]
    pth = bank(7)[:, 128:129]
    C.dma("gpsimd", self4[:, :], ein["self4"][:, :], S["sLD"])
    S["vLD"] = S["sLD"].n

    tiles = [("fw1", ein["fw1"][ts(t), :], D) for t in range(6)] + \
            [("fw2", ein["fw2"][ts(t), :], D) for t in range(6)] + \
            [("fw3", ein["fw3"][ts(t), :], 768) for t in range(16)]
    slot = 0
    for nm, ap, fd in tiles:
        vd = C.dma("sync", wf[:, 0:fd], ap, sdq)
        C.wait("scalar", sdq, vd)
        C.op("scalar", lambda e, fd=fd, slot=slot:
             e.activation(mp[:, 0:fd], wf[:, 0:fd], AF.Abs,
                          accum_out=acc[:, slot:slot + 1]), sem=ssc, drain=True)
        C.wait("sync", ssc, ssc.n)
        slot += 1
    C.wait("vector", ssc, ssc.n)
    C.op("vector", lambda e: e.tensor_reduce(red[:, 0:1], acc[:, 0:6], AX.X, ALU.add))
    C.op("vector", lambda e: e.tensor_reduce(red[:, 1:2], acc[:, 6:12], AX.X, ALU.add))
    C.op("vector", lambda e: e.tensor_reduce(red[:, 2:3], acc[:, 12:28], AX.X, ALU.add))
    C.op("vector", lambda e: e.memset(red[:, 3:4], 0.0), sem=svq, drain=True)
    C.wait("tensor", svq, svq.n)
    C.wait("tensor", S["sLD"], S["vLD"])
    C.op("tensor", lambda e: e.transpose(pr[0:4, :], red[:, :], identf[:, :]), sem=smq)
    C.wait("vector", smq, smq.n)
    C.op("vector", lambda e: e.tensor_reduce(red[0:4, 0:1], pr[0:4, :], AX.X, ALU.add),
         sem=svq, drain=True)
    C.wait("sync", svq, svq.n)
    vst = C.dma("sync", scr["gpart_in"][0:1, 0:4], red[0:4, 0:1], sst)
    C.wait("gpsimd", sst, vst)
    vgg = _ag(C, scr["gpart_in"].ap(), scr["gpart_out"].ap(), G8, sgg)
    C.wait("gpsimd", sgg, vgg)
    sgl = C.sem("pgl")
    vgl = C.dma("gpsimd", gp_sb[:, :], scr["gpart_out"][:, :], sgl)
    C.wait("tensor", sgl, vgl)
    C.wait("tensor", svq, svq.n)
    C.op("tensor", lambda e: e.transpose(pr[0:4, 0:8], gp_sb[:, :], identf[0:8, 0:8]),
         sem=smq)
    C.wait("vector", smq, smq.n)
    C.op("vector", lambda e: e.tensor_reduce(tot[:, 0:1], pr[0:4, 0:8], AX.X, ALU.add))
    C.op("vector", lambda e: e.tensor_scalar(thr4[:, :], tot[:, :],
                                             1.0 / (2.0 * HID * D), 0.5 * Q_EPS,
                                             ALU.mult, ALU.add))
    C.op("vector", lambda e: e.tensor_scalar(nthr4[:, :], thr4[:, :], -1.0, None,
                                             ALU.mult), sem=svq, drain=True)
    C.wait("tensor", svq, svq.n)
    for wi in range(3):
        for neg in range(2):
            src = nthr4 if neg else thr4
            C.op("tensor", lambda e, wi=wi, src=src:
                 e.matmul(pth, self4[:, wi * 128:(wi + 1) * 128], src[:, :],
                          start=True, stop=True), sem=smq)
            C.wait("scalar", smq, smq.n)
            col = 3 * neg + wi
            C.op("scalar", lambda e, col=col:
                 e.copy(thr_sb[:, col:col + 1], pth), sem=ssc, drain=True)
            C.wait("tensor", ssc, ssc.n)
    C.wait("vector", ssc, ssc.n)

    def quant_tile(src_ap, fd, wi):
        C.wait("sync", svq, svq.n)
        vd = C.dma("sync", wf[:, 0:fd], src_ap, sdq)
        C.wait("vector", sdq, vd)
        C.wait("vector", sgp, sgp.n)
        C.op("vector", lambda e, fd=fd, wi=wi:
             e.tensor_scalar(mp[:, 0:fd], wf[:, 0:fd], thr_sb[:, wi:wi + 1], None,
                             ALU.is_ge))
        C.op("vector", lambda e, fd=fd, wi=wi:
             e.tensor_scalar(mn[:, 0:fd], wf[:, 0:fd], thr_sb[:, 3 + wi:4 + wi], None,
                             ALU.is_le), sem=svq, drain=True)
        C.wait("gpsimd", svq, svq.n)
        C.wait("gpsimd", smq, smq.n)
        C.op("gpsimd", lambda e, fd=fd:
             e.tensor_tensor(qt[:, 0:fd], mp[:, 0:fd], mn[:, 0:fd], ALU.subtract),
             sem=sgp, drain=True)
        return sgp.n

    for wi, nm, inn in [(0, "fw1", "qw1_in"), (1, "fw2", "qw2_in")]:
        for t in range(6):
            vq = quant_tile(ein[nm][ts(t), :], D, wi)
            C.wait("tensor", sgp, vq)
            for g in range(4):
                C.wait("tensor", svq, svq.n)
                C.wait("tensor", sst, sst.n)
                for f4 in range(4):
                    fc = g * 4 + f4
                    C.op("tensor", lambda e, fc=fc, f4=f4:
                         e.transpose(bk[:, ts(f4)], qt[:, ts(fc)], identb[:, :]),
                         sem=(smq if f4 == 3 else None))
                C.wait("vector", smq, smq.n)
                C.op("vector", lambda e, g=g:
                     e.tensor_copy(tb[:, g * 512:(g + 1) * 512], bk),
                     sem=svq, drain=True)
            C.wait("sync", svq, svq.n)
            C.dma("sync", scr[inn][ts(t), :], tb, sst)
        C.wait("gpsimd", sst, sst.n)
        _ag(C, scr[inn].ap(), scr["qw" + str(wi + 1)].ap(), G8, sgg)
    for oc in range(16):
        vq = quant_tile(ein["fw3"][ts(oc), :], 768, 2)
        C.wait("tensor", sgp, vq)
        for g in range(2):
            nm_ = 4 if g == 0 else 2
            C.wait("tensor", svq, svq.n)
            C.wait("tensor", sst, sst.n)
            for m in range(nm_):
                hcl = g * 4 + m
                C.op("tensor", lambda e, hcl=hcl, m=m:
                     e.transpose(bk[:, ts(m)], qt[:, ts(hcl)], identb[:, :]),
                     sem=(smq if m == nm_ - 1 else None))
            C.wait("vector", smq, smq.n)
            C.op("vector", lambda e, g=g, nm_=nm_:
                 e.tensor_copy(tb[:, g * 512:g * 512 + nm_ * 128],
                               bk[:, 0:nm_ * 128]), sem=svq, drain=True)
        C.wait("sync", svq, svq.n)
        for hcl in range(6):
            g, m = divmod(hcl, 4)
            C.dma("sync", scr["qw3_in"][hcl * 128:(hcl + 1) * 128, ts(oc)],
                  tb[:, g * 512 + m * 128:g * 512 + (m + 1) * 128], sst)
    C.wait("gpsimd", sst, sst.n)
    vlast = _ag(C, scr["qw3_in"].ap(), scr["qw3"].ap(), G8, sgg)
    return [(sgg, vlast)]


def emit_ln(S, sems, tc, hname):
    C, bank, x_fm = S["C"], S["bank"], S["x_fm"]
    onesb = S["cb"]["onesb"]
    sm, sv, ss, sg = sems["m"], sems["v"], sems["s"], sems["g"]
    so2 = sems["o2"]
    xb2 = S["arB"][:, 0:1024]
    sq2 = S["arB"][:, 1024:2048]
    hb2 = S["arB"][:, 2048:3072]
    m_sb = S["arBf"][:, 1536:2048]
    rstd = S["arCf"][:, 3072:3584]
    ms_sb = S["arCf"][:, 3584:4096]
    sx, sq = bank(0), bank(1)
    C.wait("tensor", sv, sv.n)
    for fc in range(NFC):
        xs = x_fm[:, fc * TOK + tc * CH: fc * TOK + (tc + 1) * CH]
        sl = fc % 2
        C.wait("scalar", sm, sm.n)
        C.op("scalar", lambda e, xs=xs, sl=sl:
             e.copy(xb2[:, sl * 512:(sl + 1) * 512], xs), sem=ss, drain=True)
        C.wait("gpsimd", sm, sm.n)
        C.op("gpsimd", lambda e, xs=xs, sl=sl:
             e.tensor_tensor(sq2[:, sl * 512:(sl + 1) * 512], xs, xs, ALU.mult),
             sem=sg, drain=True)
        C.wait("tensor", ss, ss.n)
        C.op("tensor", lambda e, sl=sl, fc=fc:
             e.matmul(sx, onesb[:, :], xb2[:, sl * 512:(sl + 1) * 512],
                      start=(fc == 0), stop=(fc == NFC - 1)), sem=sm)
        C.wait("tensor", sg, sg.n)
        C.op("tensor", lambda e, sl=sl, fc=fc:
             e.matmul(sq, onesb[:, :], sq2[:, sl * 512:(sl + 1) * 512],
                      start=(fc == 0), stop=(fc == NFC - 1)), sem=sm)
    C.wait("vector", sm, sm.n)
    C.op("vector", lambda e: e.tensor_scalar(m_sb, sx, 1.0 / D, None, ALU.mult))
    C.op("vector", lambda e: e.tensor_tensor(ms_sb, m_sb, m_sb, ALU.mult))
    C.op("vector", lambda e: e.scalar_tensor_tensor(rstd, sq, 1.0 / D, ms_sb,
                                                    ALU.mult, ALU.subtract),
         sem=sv, drain=True)
    C.wait("scalar", sv, sv.n)
    C.op("scalar", lambda e: e.activation(rstd, rstd, AF.Ln, bias=LN_EPS, scale=1.0))
    C.op("scalar", lambda e: e.activation(rstd, rstd, AF.Exp, scale=-0.5),
         sem=ss, drain=True)
    C.wait("vector", ss, ss.n)
    for fc in range(NFC):
        xs = x_fm[:, fc * TOK + tc * CH: fc * TOK + (tc + 1) * CH]
        sl = fc % 2
        C.wait("vector", so2[sl], so2[sl].n)
        C.op("vector", lambda e, xs=xs, sl=sl:
             e.tensor_tensor(hb2[:, sl * 512:(sl + 1) * 512], xs, m_sb, ALU.subtract))
        C.op("vector", lambda e, sl=sl:
             e.tensor_tensor(hb2[:, sl * 512:(sl + 1) * 512],
                             hb2[:, sl * 512:(sl + 1) * 512], rstd, ALU.mult),
             sem=sv, drain=True)
        C.wait("sync", sv, sv.n)
        C.dma("sync", S["scr"][hname][fc, :, tc * CH:(tc + 1) * CH],
              hb2[:, sl * 512:(sl + 1) * 512], so2[sl])


def phase_P1(S, fence):
    C, bank, bankb = S["C"], S["bank"], S["bankb"]
    scr, ein, cb = S["scr"], S["ein"], S["cb"]
    x_fm, arXb = S["x_fm"], S["arXb"]
    onesb, rotT, identb = cb["onesb"], cb["rotT"], cb["identb"]
    sm = C.sem("p1m")
    sv = C.sem("p1v")
    ss = C.sem("p1s")
    sg = C.sem("p1g")
    so2 = [C.sem("p1o0"), C.sem("p1o1")]
    sp = C.sem("p1p")
    sld = [C.sem("p1l0"), C.sem("p1l1")]
    sq_ = C.sem("p1q")
    scc = C.sem("p1cc")
    sko2 = [C.sem("p1ko0"), C.sem("p1ko1")]
    skv = [C.sem("p1kv0"), C.sem("p1kv1")]
    sxs = C.sem("p1xs")
    shl = C.sem("p1hl")
    gd = Guard(C)

    C.barrier(fence, engines=("tensor", "vector", "scalar", "gpsimd", "sync"))
    lnsems = dict(m=sm, v=sv, s=ss, g=sg, o2=so2)
    for tc in range(NTC):
        emit_ln(S, lnsems, tc, "h_scr")
    C.wait("scalar", sv, sv.n)
    C.wait("scalar", sg, sg.n)
    for fc in range(NFC):
        C.dma("scalar", scr["xspill"][fc, :, :], x_fm[:, fc * TOK:(fc + 1) * TOK], sp)
    h_all = arXb[:, 0:NFC * TOK]
    cosb = S["arX"][:, 16384:18432]
    sinb = S["arX"][:, 18432:20480]
    t1 = S["arX"][:, 20480:20992]
    t2 = S["arX"][:, 20992:21504]
    outb = arXb[:, 45056:47104]
    wbuf = [arXb[:, 47104 + i * 2048: 47104 + (i + 1) * 2048] for i in range(2)]
    vtb = arXb[:, 51200:53248]
    C.wait("sync", sp, sp.n)
    C.wait("sync", so2[0], so2[0].n)
    C.wait("sync", so2[1], so2[1].n)
    for fc in range(NFC):
        C.dma("sync", h_all[:, fc * TOK:(fc + 1) * TOK], scr["h_scr"][fc, :, :], shl)
    C.dma("sync", cosb, ein["ropecos"][:, :], shl)
    C.dma("sync", sinb, ein["ropesin"][:, :], shl)
    C.barrier([(shl, shl.n)], engines=("tensor", "vector", "scalar"))

    ldtgt = {}

    def load_w(src_rows, li):
        sl = li % 2
        gd.acquire("sync", ("wbuf", sl))
        ldtgt[sl] = C.dma("sync", wbuf[sl], src_rows, sld[sl])

    def proj_psum(li, tc, pb_i):
        w = wbuf[li % 2]
        C.wait("tensor", sld[li % 2], ldtgt[li % 2])
        gd.acquire("tensor", ("bank", pb_i))
        for fc in range(NFC):
            C.op("tensor", lambda e, w=w, fc=fc, tc=tc, pb_i=pb_i:
                 e.matmul(bank(pb_i), w[:, ts(fc)],
                          h_all[:, fc * TOK + tc * CH: fc * TOK + (tc + 1) * CH],
                          start=(fc == 0), stop=(fc == NFC - 1)),
                 sem=(sm if fc == NFC - 1 else None))
        gd.use(("wbuf", li % 2), sm)

    def rope_unit(pb_i, rb_i, tc, sl):
        dst_bf = outb[:, sl * 512:(sl + 1) * 512]
        C.wait("scalar", sm, sm.n)
        gd.acquire("scalar", ("outb", sl))
        C.op("scalar", lambda e, pb_i=pb_i, dst_bf=dst_bf:
             e.copy(dst_bf, bank(pb_i)), sem=ss)
        C.wait("tensor", ss, ss.n)
        gd.acquire("tensor", ("bank", rb_i))
        C.op("tensor", lambda e, rb_i=rb_i, dst_bf=dst_bf:
             e.matmul(bank(rb_i), rotT[:, :], dst_bf, start=True, stop=True), sem=sm)
        C.wait("vector", sm, sm.n)
        cs = cosb[:, tc * CH:(tc + 1) * CH]
        sn = sinb[:, tc * CH:(tc + 1) * CH]
        C.op("vector", lambda e, pb_i=pb_i, cs=cs:
             e.tensor_tensor(t1, bank(pb_i), cs, ALU.mult))
        C.op("vector", lambda e, rb_i=rb_i, sn=sn:
             e.tensor_tensor(t2, bank(rb_i), sn, ALU.mult))
        C.op("vector", lambda e, dst_bf=dst_bf:
             e.tensor_tensor(dst_bf, t1, t2, ALU.add), sem=sv)
        gd.use(("bank", pb_i), ss)
        gd.use(("bank", pb_i), sv)
        gd.use(("bank", rb_i), sv)
        return dst_bf

    def store_out(dst_ap, dst_bf, sl):
        C.wait("scalar", sv, sv.n)
        C.dma("scalar", dst_ap, dst_bf, sko2[sl])
        gd.use(("outb", sl), sko2[sl])

    li = 0
    load_w(scr["aWkT"][ts(0), :], 0)
    for dc in range(4):
        if dc + 1 < 4:
            load_w(scr["aWkT"][ts(dc + 1), :], li + 1)
        for tc in range(NTC):
            proj_psum(li, tc, 2 + (tc % 2))
            dst = rope_unit(2 + (tc % 2), 4 + (tc % 2), tc, tc % 2)
            store_out(scr["kina"][ts(dc), tc * CH:(tc + 1) * CH], dst, tc % 2)
        li += 1
    for tc in range(NTC):
        load_w(scr["aWvT"][ts(0), :], li)
        for dc in range(4):
            if dc + 1 < 4:
                load_w(scr["aWvT"][ts(dc + 1), :], li + 1)
            pb_i = 2 + (dc % 2)
            proj_psum(li, tc, pb_i)
            C.wait("scalar", sm, sm.n)
            gd.acquire("scalar", ("vtb", dc))
            C.op("scalar", lambda e, pb_i=pb_i, dc=dc:
                 e.copy(vtb[:, dc * 512:(dc + 1) * 512], bank(pb_i)), sem=ss)
            gd.use(("bank", pb_i), ss)
            li += 1
        C.wait("tensor", ss, ss.n)
        for tt in range(4):
            tb_i = 4 + (tt % 2)
            gd.acquire("tensor", ("bank", tb_i))
            for dc in range(4):
                C.op("tensor", lambda e, tb_i=tb_i, dc=dc, tt=tt:
                     e.transpose(bankb(tb_i)[:, 0:512][:, ts(dc)],
                                 vtb[:, dc * 512 + tt * 128: dc * 512 + (tt + 1) * 128],
                                 identb[:, :]),
                     sem=(sm if dc == 3 else None))
                gd.use(("vtb", dc), sm)
            C.wait("vector", sm, sm.n)
            sl = tt % 2
            vout = outb[:, sl * 512:(sl + 1) * 512]
            gd.acquire("vector", ("outb", sl))
            C.op("vector", lambda e, tb_i=tb_i, vout=vout:
                 e.tensor_copy(vout, bankb(tb_i)[:, 0:512]), sem=sv)
            gd.use(("bank", tb_i), sv)
            store_out(scr["vina"][tc * CH + tt * 128: tc * CH + (tt + 1) * 128, :],
                      vout, sl)
    load_w(scr["aWqT"][ts(0), :], li)
    for oc in range(NFC):
        if oc + 1 < NFC:
            load_w(scr["aWqT"][ts(oc + 1), :], li + 1)
        for tc in range(NTC):
            proj_psum(li, tc, 2 + (tc % 2))
            dst = rope_unit(2 + (tc % 2), 4 + (tc % 2), tc, tc % 2)
            store_out(scr["qa_scr"][ts(oc), tc * CH:(tc + 1) * CH], dst, tc % 2)
        li += 1
    C.wait("gpsimd", sko2[0], sko2[0].n)
    C.wait("gpsimd", sko2[1], sko2[1].n)
    _ag(C, scr["kina"].ap(), scr["kga"].ap(), G2, scc)
    _ag(C, scr["vina"].ap(), scr["vga"].ap(), G2, scc)
    C.barrier([(scc, 2), (sm, sm.n), (sv, sv.n), (ss, ss.n)],
              engines=("tensor", "scalar", "vector", "sync"))

    kkb = [arXb[:, i * 4096: (i + 1) * 4096] for i in range(2)]
    vvb = [arXb[:, 8192 + i * 4096: 8192 + (i + 1) * 4096] for i in range(2)]
    qsb = arXb[:, 16384:24576]
    o_all = arXb[:, 24576:32768]
    p_sb = [arXb[:, 32768 + i * 1024: 32768 + (i + 1) * 1024] for i in range(4)]
    rec_sb = S["arX"][:, 18432:18944]
    lnt = S["arX"][:, 18944:19456]
    xcur = S["arX"][:, 19456:27648]
    wov = [arXb[:, 55296 + i * 2048: 55296 + (i + 1) * 2048] for i in range(2)]

    def load_kv(kvh, slot):
        sem = skv[slot]
        gd.acquire("sync", ("kv", slot))
        for r in range(2):
            C.dma("sync", kkb[slot][:, r * 2048:(r + 1) * 2048],
                  scr["kga"][r * 512 + kvh * 128: r * 512 + (kvh + 1) * 128, :], sem)
        for r in range(2):
            for hh in range(2):
                C.dma("sync",
                      vvb[slot][:, (r * 16 + hh * 8) * 128:
                                (r * 16 + (hh + 1) * 8) * 128]
                      .rearrange("p (a d) -> p a d", a=8),
                      scr["vga"][r * TOK + hh * 1024: r * TOK + (hh + 1) * 1024,
                                 kvh * 128:(kvh + 1) * 128]
                      .rearrange("(a p) d -> p a d", p=128), sem)
        return sem.n

    recp = [rec_sb, S["arX"][:, 29696:30208]]
    lntp = [lnt, S["arX"][:, 30208:30720]]
    sqp = C.sem("p1qp")
    kvt = [None, None]
    kvt[0] = load_kv(0, 0)
    vq_t = None
    for qc in range(NTC):
        if qc == 0:
            gd.acquire("sync", ("qsb",))
            vq_t = C.dma("sync", qsb.rearrange("p (oc t) -> p oc t", oc=16),
                         scr["qa_scr"].ap().rearrange("(oc p) t -> p oc t", p=128)
                         [:, :, qc * CH:(qc + 1) * CH], sqp)
        C.barrier([(sqp, vq_t)], engines=("tensor",))
        units = [(h, kp, h // 4) for h in range(16) for kp in range(16)]
        ss_t = {}

        def emit_qk(ui, h, kp, kvh, qc=qc):
            qh = qsb[:, h * 512:(h + 1) * 512]
            sti = 2 * (ui % 2)
            st2 = bank(sti, 2)
            gd.acquire("tensor", ("st", sti))
            for j in range(2):
                kt = 2 * kp + j
                C.op("tensor", lambda e, st2=st2, j=j, kt=kt, kvh=kvh, qh=qh:
                     e.matmul(st2[:, j * 512:(j + 1) * 512],
                              kkb[kvh % 2][:, kt * 128:(kt + 1) * 128],
                              qh, start=True, stop=True),
                     sem=(sm if j == 1 else None))
            C.wait("scalar", sm, sm.n)
            gd.acquire("scalar", ("p", ui % 4))
            ss_t[ui] = C.op("scalar", lambda e, st2=st2, ui=ui:
                            e.activation(p_sb[ui % 4], st2, AF.Exp, bias=ACONST,
                                         scale=ISQD), sem=ss)
            gd.use(("st", sti), ss)

        def emit_pv(ui, h, kp, kvh, qc=qc):
            pb2 = p_sb[ui % 4]
            ob_, dnb = 4 + 2 * (h % 2), 5 + 2 * (h % 2)
            if kp == 0:
                gd.acquire("tensor", ("bank", ob_))
                gd.acquire("tensor", ("bank", dnb))
            C.wait("tensor", ss, ss_t[ui])
            for j in range(2):
                kt = 2 * kp + j
                C.op("tensor", lambda e, pb2=pb2, j=j, kt=kt, kvh=kvh, ob_=ob_:
                     e.matmul(bank(ob_), vvb[kvh % 2][:, kt * 128:(kt + 1) * 128],
                              pb2[:, j * 512:(j + 1) * 512],
                              start=(kt == 0), stop=(kt == 31)))
                C.op("tensor", lambda e, pb2=pb2, j=j, kt=kt, dnb=dnb:
                     e.matmul(bank(dnb), onesb[:, :],
                              pb2[:, j * 512:(j + 1) * 512],
                              start=(kt == 0), stop=(kt == 31)),
                     sem=(sm if j == 1 else None))
            gd.use(("p", ui % 4), sm)
            if kp == 15:
                gd.use(("kv", kvh % 2), sm)
                rc, lt = recp[h % 2], lntp[h % 2]
                C.wait("scalar", sm, sm.n)
                gd.acquire("scalar", ("rec", h % 2))
                C.op("scalar", lambda e, dnb=dnb, lt=lt:
                     e.activation(lt, bank(dnb), AF.Ln))
                t_rec = C.op("scalar", lambda e, rc=rc, lt=lt:
                             e.activation(rc, lt, AF.Exp, scale=-1.0), sem=ss)
                gd.use(("bank", dnb), ss)
                C.wait("vector", ss, t_rec)
                oh = o_all[:, h * 512:(h + 1) * 512]
                gd.acquire("vector", ("oall", h))
                C.op("vector", lambda e, oh=oh, ob_=ob_, rc=rc:
                     e.tensor_tensor(oh, bank(ob_), rc, ALU.mult), sem=sv)
                gd.use(("bank", ob_), sv)
                gd.use(("rec", h % 2), sv)

        for ui, (h, kp, kvh) in enumerate(units):
            if kp == 0 and h % 4 == 0:
                C.wait("tensor", skv[kvh % 2], kvt[kvh % 2])
            elif kp == 2 and h % 4 == 0:
                if kvh + 1 < 4:
                    kvt[(kvh + 1) % 2] = load_kv(kvh + 1, (kvh + 1) % 2)
                elif qc + 1 < NTC:
                    kvt[0] = load_kv(0, 0)
            emit_qk(ui, h, kp, kvh)
            if ui >= 2:
                emit_pv(ui - 2, *units[ui - 2])
        gd.use(("qsb",), sm)
        for ui in range(len(units) - 2, len(units)):
            emit_pv(ui, *units[ui])
        if qc + 1 < NTC:
            gd.acquire("scalar", ("qsb",))
            vq_t = C.dma("scalar", qsb.rearrange("p (oc t) -> p oc t", oc=16),
                         scr["qa_scr"].ap().rearrange("(oc p) t -> p oc t", p=128)
                         [:, :, (qc + 1) * CH:(qc + 2) * CH], sqp)
        C.wait("tensor", sv, sv.n)
        C.wait("sync", sxs, sxs.n)
        gd.acquire("sync", ("xcur",))
        vx = C.dma("sync", xcur.rearrange("p (oc t) -> p oc t", oc=16),
                   scr["xspill"].ap().rearrange("oc p t -> p oc t")
                   [:, :, qc * CH:(qc + 1) * CH], sq_)
        C.barrier([(sq_, vx)], engines=("vector",))
        def wov_load(oc):
            sl = oc % 2
            gd.acquire("sync", ("wov", sl))
            C.dma("sync", wov[sl], scr["aWoT"][ts(oc), :], sld[sl])
            return sld[sl].n

        wovt = [None, None]
        wovt[0] = wov_load(0)
        for oc in range(NFC):
            sl = oc % 2
            if oc + 1 < NFC:
                wovt[(oc + 1) % 2] = wov_load(oc + 1)
            C.wait("tensor", sld[sl], wovt[sl])
            pb_i = 2 + sl
            gd.acquire("tensor", ("bank", pb_i))
            for dc in range(NFC):
                C.op("tensor", lambda e, pb_i=pb_i, sl=sl, dc=dc:
                     e.matmul(bank(pb_i), wov[sl][:, ts(dc)],
                              o_all[:, dc * 512:(dc + 1) * 512],
                              start=(dc == 0), stop=(dc == NFC - 1)),
                     sem=(sm if dc == NFC - 1 else None))
            gd.use(("wov", sl), sm)
            if oc == NFC - 1:
                for hh in range(NH):
                    gd.use(("oall", hh), sm)
            C.wait("vector", sm, sm.n)
            xslice = xcur[:, oc * 512:(oc + 1) * 512]
            C.op("vector", lambda e, pb_i=pb_i, xslice=xslice:
                 e.tensor_tensor(xslice, bank(pb_i), xslice, ALU.add), sem=sv)
            gd.use(("bank", pb_i), sv)
            gd.use(("st", 2), sv)
        C.wait("sync", sv, sv.n)
        for oc in range(NFC):
            C.dma("sync", scr["xspill"][oc, :, qc * CH:(qc + 1) * CH],
                  xcur[:, oc * 512:(oc + 1) * 512], sxs)
        gd.use(("xcur",), sxs)
    sxr = C.sem("p1xr")
    C.wait("sync", sxs, sxs.n)
    for fc in range(NFC):
        C.dma("sync", S["x_fm"][:, fc * TOK:(fc + 1) * TOK], scr["xspill"][fc, :, :],
              sxr)
    return [(sxr, sxr.n), (sm, sm.n), (sv, sv.n), (ss, ss.n)]


def phase_P2(S, fence, tag):
    C, bank, arA, arB, arC = S["C"], S["bank"], S["arA"], S["arB"], S["arC"]
    scr = S["scr"]
    sm = C.sem(tag + "m")
    sv = C.sem(tag + "v")
    ss = C.sem(tag + "s")
    sg = C.sem(tag + "g")
    so2 = [C.sem(tag + "o0"), C.sem(tag + "o1")]
    swq = C.sem(tag + "wq")
    sw3 = [C.sem(tag + "w30"), C.sem(tag + "w31")]
    sh2 = [C.sem(tag + "h0"), C.sem(tag + "h1")]
    gd = Guard(C)
    C.barrier(fence, engines=("tensor", "vector", "scalar", "gpsimd", "sync"))
    lnsems = dict(m=sm, v=sv, s=ss, g=sg, o2=so2)
    for tc in range(NTC):
        emit_ln(S, lnsems, tc, "h_scr")
    C.wait("sync", so2[0], so2[0].n)
    C.wait("sync", so2[1], so2[1].n)
    NG = 22
    qw1b = arC[:, 0:4096]
    qw2b = arC[:, 4096:8192]
    hid = arC[:, 8192:12288]
    h2q = [[arA[:, 0:4096], arA[:, 4096:8192]],
           [arC[:, 12288:16384], arC[:, 16384:20480]]]
    s1b = [arB[:, i * 512:(i + 1) * 512] for i in range(2)]
    qw3b = [arB[:, 1024 + i * 512: 1024 + (i + 1) * 512] for i in range(2)]
    h_scr = scr["h_scr"]

    hseq = []
    for g in range(NG):
        for tc in (range(NTC) if g % 2 == 0 else range(NTC - 1, -1, -1)):
            hseq.append(tc)
    plan = []
    nload = 0
    prev = None
    for tc in hseq:
        if tc == prev:
            plan.append((False, (nload - 1) % 2))
        else:
            plan.append((True, nload % 2))
            nload += 1
        prev = tc

    h2t = {}

    def load_h2(step):
        b = plan[step][1]
        tc_ = hseq[step]
        for half in range(2):
            qeng = "sync" if half == 0 else "scalar"
            gd.acquire(qeng, ("h2", b, half))
            C.dma(qeng,
                  h2q[b][half].rearrange("p (f j) -> p f j", f=8),
                  h_scr.ap()[half * 8:(half + 1) * 8, :,
                             tc_ * CH:(tc_ + 1) * CH]
                  .rearrange("f p j -> p f j"), sh2[b])
        h2t[step] = sh2[b].n

    def next_load_step(after):
        for i in range(after + 1, len(plan)):
            if plan[i][0]:
                return i
        return None

    def load_qw12(g, eng):
        gd.acquire(eng, ("qw12",))
        for hcl, hc in enumerate([2 * g, 2 * g + 1]):
            C.dma(eng, qw1b[:, hcl * 2048:(hcl + 1) * 2048],
                  scr["qw1"][ts(hc), :], swq)
            C.dma(eng, qw2b[:, hcl * 2048:(hcl + 1) * 2048],
                  scr["qw2"][ts(hc), :], swq)
        return swq.n

    load_h2(0)
    pend = next_load_step(0)
    if pend is not None:
        load_h2(pend)
    qwtgt = load_qw12(0, "sync")
    for g in range(NG):
        C.wait("tensor", swq, qwtgt)
        for ti in range(NTC):
            step = g * NTC + ti
            tc = hseq[step]
            b = plan[step][1]
            h2b = h2q[b]
            C.wait("tensor", sh2[b], h2t[step if plan[step][0]
                                         else step - 1])
            if plan[step][0] and pend == step:
                pend = next_load_step(step)
                if pend is not None:
                    load_h2(pend)
            for hcl in range(2):
                b1 = hcl * 2
                b2 = hcl * 2 + 1
                gd.acquire("tensor", ("bank", b1))
                for fc in range(NFC):
                    C.op("tensor", lambda e, hcl=hcl, fc=fc, b1=b1, h2b=h2b:
                         e.matmul(bank(b1), qw1b[:, hcl * 2048 + fc * 128:
                                               hcl * 2048 + (fc + 1) * 128],
                                  h2b[fc // 8][:, (fc % 8) * 512:(fc % 8 + 1) * 512],
                                  start=(fc == 0), stop=(fc == NFC - 1)),
                         sem=(sm if fc == NFC - 1 else None))
                C.wait("scalar", sm, sm.n)
                gd.acquire("scalar", ("s1", hcl))
                C.op("scalar", lambda e, hcl=hcl, b1=b1:
                     e.activation(s1b[hcl], bank(b1), AF.Sigmoid), sem=ss)
                gd.use(("bank", b1), ss)
                gd.acquire("tensor", ("bank", b2))
                for fc in range(NFC):
                    C.op("tensor", lambda e, hcl=hcl, fc=fc, b2=b2, h2b=h2b:
                         e.matmul(bank(b2), qw2b[:, hcl * 2048 + fc * 128:
                                               hcl * 2048 + (fc + 1) * 128],
                                  h2b[fc // 8][:, (fc % 8) * 512:(fc % 8 + 1) * 512],
                                  start=(fc == 0), stop=(fc == NFC - 1)),
                         sem=(sm if fc == NFC - 1 else None))
                C.wait("vector", sm, sm.n)
                C.wait("vector", ss, ss.n)
                gd.acquire("vector", ("hid",))
                C.op("vector", lambda e, hcl=hcl, tc=tc, b2=b2:
                     e.tensor_tensor(hid[:, (hcl * 4 + tc) * 512:
                                         (hcl * 4 + tc + 1) * 512],
                                     s1b[hcl], bank(b2), ALU.mult), sem=sv)
                gd.use(("bank", b2), sv)
                gd.use(("s1", hcl), sv)
            gd.use(("h2", b, 0), sm)
            gd.use(("h2", b, 1), sm)
        gd.use(("qw12",), sm)
        if g + 1 < NG:
            qwtgt = load_qw12(g + 1, "scalar")
        C.wait("tensor", sv, sv.n)
        def qw3_load(oc):
            sl = oc % 2
            gd.acquire("sync", ("qw3", sl))
            C.dma("sync",
                  qw3b[sl][:, 0:256].rearrange("p (h i) -> p h i", h=2),
                  scr["qw3"].ap().rearrange("(h p) o -> h p o", p=128)
                  [2 * g:2 * g + 2, :, oc * 128:(oc + 1) * 128]
                  .rearrange("h p o -> p h o"), sw3[sl])
            return sw3[sl].n

        qw3t = [None, None]
        qw3t[0] = qw3_load(0)
        for oc in range(NFC):
            sl = oc % 2
            if oc + 1 < NFC:
                qw3t[(oc + 1) % 2] = qw3_load(oc + 1)
            C.wait("tensor", sw3[sl], qw3t[sl])
            for tc in range(NTC):
                ob = (tc % 2)
                gd.acquire("tensor", ("bank", ob))
                for hcl in range(2):
                    C.op("tensor", lambda e, sl=sl, hcl=hcl, tc=tc, ob=ob:
                         e.matmul(bank(ob), qw3b[sl][:, hcl * 128:(hcl + 1) * 128],
                                  hid[:, (hcl * 4 + tc) * 512:
                                      (hcl * 4 + tc + 1) * 512],
                                  start=(hcl == 0), stop=(hcl == 1)),
                         sem=(sm if hcl == 1 else None))
                C.wait("vector", sm, sm.n)
                xsl = S["x_fm"][:, oc * TOK + tc * CH: oc * TOK + (tc + 1) * CH]
                C.op("vector", lambda e, xsl=xsl, ob=ob:
                     e.tensor_tensor(xsl, bank(ob), xsl, ALU.add), sem=sv)
                gd.use(("bank", ob), sv)
            gd.use(("qw3", sl), sm)
        gd.use(("hid",), sm)
    return [(sm, sm.n), (sv, sv.n), (ss, ss.n)]


def phase_P3(S, fence):
    C, bank, bankb = S["C"], S["bank"], S["bankb"]
    scr, ein, cb = S["scr"], S["ein"], S["cb"]
    x_fm, arXb = S["x_fm"], S["arXb"]
    onesb, identb = cb["onesb"], cb["identb"]
    ehmat = arXb[:, 57344:59392]
    crow_sb, bias_sb = S["crow_sb"], S["bias_sb"]
    sm = C.sem("p3m")
    sv = C.sem("p3v")
    ss = C.sem("p3s")
    sg = C.sem("p3g")
    so2 = [C.sem("p3o0"), C.sem("p3o1")]
    sp = C.sem("p3p")
    sld = [C.sem("p3l0"), C.sem("p3l1")]
    sq_ = C.sem("p3q")
    scc = C.sem("p3cc")
    sko2 = [C.sem("p3ko0"), C.sem("p3ko1")]
    skv = [C.sem("p3kv0"), C.sem("p3kv1")]
    sxs = C.sem("p3xs")
    shl = C.sem("p3hl")
    sms = C.sem("p3ms")
    gd = Guard(C)

    C.barrier(fence, engines=("tensor", "vector", "scalar", "gpsimd", "sync"))
    lnsems = dict(m=sm, v=sv, s=ss, g=sg, o2=so2)
    for tc in range(NTC):
        emit_ln(S, lnsems, tc, "h_scr")
    C.wait("scalar", sv, sv.n)
    C.wait("scalar", sg, sg.n)
    for fc in range(NFC):
        C.dma("scalar", scr["xspill"][fc, :, :], x_fm[:, fc * TOK:(fc + 1) * TOK], sp)
    h_all = arXb[:, 0:NFC * TOK]
    vchunk = arXb[:, 32768:40960]
    vout = arXb[:, 40960:45056]
    outb = arXb[:, 45056:47104]
    wbuf = [arXb[:, 47104 + i * 2048: 47104 + (i + 1) * 2048] for i in range(2)]
    C.wait("sync", sp, sp.n)
    C.wait("sync", so2[0], so2[0].n)
    C.wait("sync", so2[1], so2[1].n)
    for fc in range(NFC):
        C.dma("sync", h_all[:, fc * TOK:(fc + 1) * TOK], scr["h_scr"][fc, :, :], shl)
    C.dma("sync", ehmat, ein["ehmat"][:, :], shl)
    C.barrier([(shl, shl.n)], engines=("tensor", "vector", "scalar"))

    ldtgt3 = {}

    def load_wb(wname, oc, li):
        sl = li % 2
        gd.acquire("sync", ("wbuf", sl))
        for hf in range(2):
            C.dma("sync",
                  wbuf[sl][:, hf * 1024:(hf + 1) * 1024]
                  .rearrange("p (c i) -> p c i", c=8),
                  scr[wname].ap().rearrange("(c p) o -> c p o", p=128)
                  [hf * 8:(hf + 1) * 8, :, oc * 128:(oc + 1) * 128]
                  .rearrange("c p o -> p c o"), sld[sl])
        ldtgt3[sl] = sld[sl].n

    def proj3(li, tc, pb_i):
        w = wbuf[li % 2]
        C.wait("tensor", sld[li % 2], ldtgt3[li % 2])
        gd.acquire("tensor", ("bank", pb_i))
        for cc in range(NFC):
            C.op("tensor", lambda e, w=w, cc=cc, tc=tc, pb_i=pb_i:
                 e.matmul(bank(pb_i), w[:, ts(cc)],
                          h_all[:, cc * TOK + tc * CH: cc * TOK + (tc + 1) * CH],
                          start=(cc == 0), stop=(cc == NFC - 1)),
                 sem=(sm if cc == NFC - 1 else None))
        gd.use(("wbuf", li % 2), sm)

    li = 0
    for wi, wname, dst in ((0, "bWqb", "qb_scr"), (1, "bWkb", "kinb")):
        load_wb(wname, 0, li)
        for oc in range(NFC):
            if oc + 1 < NFC:
                load_wb(wname, oc + 1, li + 1)
            for tc in range(NTC):
                pb_i = 2 + (tc % 2)
                proj3(li, tc, pb_i)
                sl = tc % 2
                C.wait("vector", sm, sm.n)
                gd.acquire("vector", ("outb", sl))
                dst_bf = outb[:, sl * 512:(sl + 1) * 512]
                C.op("vector", lambda e, pb_i=pb_i, dst_bf=dst_bf, wi=wi, oc=oc:
                     e.tensor_scalar(dst_bf, bank(pb_i),
                                     bias_sb[:, wi * 16 + oc: wi * 16 + oc + 1],
                                     None, ALU.add), sem=sv)
                gd.use(("bank", pb_i), sv)
                C.wait("scalar", sv, sv.n)
                C.dma("scalar", scr[dst][ts(oc), tc * CH:(tc + 1) * CH], dst_bf,
                      sko2[sl])
                gd.use(("outb", sl), sko2[sl])
            li += 1
    for tc in range(NTC):
        load_wb("bWvb", 0, li)
        for oc in range(NFC):
            if oc + 1 < NFC:
                load_wb("bWvb", oc + 1, li + 1)
            pb_i = 2 + (oc % 2)
            proj3(li, tc, pb_i)
            C.wait("vector", sm, sm.n)
            gd.acquire("vector", ("vchunk", oc))
            C.op("vector", lambda e, pb_i=pb_i, oc=oc:
                 e.tensor_scalar(vchunk[:, oc * 512:(oc + 1) * 512], bank(pb_i),
                                 bias_sb[:, 32 + oc: 33 + oc], None, ALU.add),
                 sem=sv)
            gd.use(("bank", pb_i), sv)
            li += 1
        C.wait("tensor", sv, sv.n)
        for tt in range(4):
            for g4 in range(4):
                tb_i = 4 + (g4 % 2)
                gd.acquire("tensor", ("bank", tb_i))
                for m in range(4):
                    oc = g4 * 4 + m
                    C.op("tensor", lambda e, tb_i=tb_i, oc=oc, tt=tt, m=m:
                         e.transpose(bankb(tb_i)[:, 0:512][:, ts(m)],
                                     vchunk[:, oc * 512 + tt * 128:
                                            oc * 512 + (tt + 1) * 128],
                                     identb[:, :]),
                         sem=(sm if m == 3 else None))
                    gd.use(("vchunk", oc), sm)
                C.wait("vector", sm, sm.n)
                if g4 == 0:
                    gd.acquire("vector", ("vout",))
                C.op("vector", lambda e, tb_i=tb_i, g4=g4:
                     e.tensor_copy(vout[:, g4 * 512:(g4 + 1) * 512],
                                   bankb(tb_i)[:, 0:512]), sem=sv)
                gd.use(("bank", tb_i), sv)
            C.wait("scalar", sv, sv.n)
            C.dma("scalar", scr["vinb"][tc * CH + tt * 128: tc * CH + (tt + 1) * 128, :],
                  vout[:, 0:2048], sko2[0])
            gd.use(("vout",), sko2[0])
    C.wait("gpsimd", sko2[0], sko2[0].n)
    C.wait("gpsimd", sko2[1], sko2[1].n)
    _ag(C, scr["kinb"].ap(), scr["kgb"].ap(), G2, scc)
    _ag(C, scr["vinb"].ap(), scr["vgb"].ap(), G2, scc)
    C.barrier([(scc, 2), (sm, sm.n), (sv, sv.n), (ss, ss.n)],
              engines=("tensor", "scalar", "vector", "sync"))

    qsb = arXb[:, 0:8192]
    o_all = arXb[:, 8192:16384]
    kvr = [arXb[:, 16384 + i * 4096: 16384 + (i + 1) * 4096] for i in range(2)]
    p_sb = [arXb[:, 24576 + i * 512: 24576 + (i + 1) * 512] for i in range(4)]
    s_sb = [S["arX"][:, 13312 + i * 512: 13312 + (i + 1) * 512] for i in range(2)]
    masks8 = arXb[:, 28672:32768]
    rec_sb = S["arX"][:, 16384:16896]
    lnt = S["arX"][:, 16896:17408]
    kb_sb = S["arX"][:, 17664:18240]
    xcur = S["arX"][:, 18432:26624]
    wov = [arXb[:, 53248 + i * 2048: 53248 + (i + 1) * 2048] for i in range(2)]

    def g2ri(gkt):
        return g2rank_idx(128 * gkt)

    sqp3 = C.sem("p3qp")
    vq_t = vms_t = None
    for slot in range(NTC):
        ktn = KTN[slot]
        pieces = [(p0, min(p0 + 8, ktn)) for p0 in range(0, ktn, 8)]
        if slot == 0:
            gd.acquire("sync", ("qsb",))
            vq_t = C.dma("sync", qsb.rearrange("p (oc t) -> p oc t", oc=16),
                         scr["qb_scr"].ap().rearrange("(oc p) t -> p oc t", p=128)
                         [:, :, slot * CH:(slot + 1) * CH], sqp3)
            gd.acquire("sync", ("masks8",))
            C.dma("sync", masks8.rearrange("p (j x) -> p j x", j=8),
                  ein["mdyn"].ap()[slot].rearrange("j p x -> p j x"), sms)
            gd.acquire("sync", ("kb",))
            C.dma("sync", kb_sb[:, 0:ktn * 16].rearrange("p (i h) -> p i h", i=ktn),
                  ein["kbias"].ap()[slot, 0:ktn].rearrange("i p h -> p i h"), sms)
            vms_t = sms.n
        C.barrier([(sqp3, vq_t)], engines=("tensor",))
        C.barrier([(sms, vms_t)], engines=("tensor", "vector", "scalar"))

        def load_piece(pi, slot_buf, hg):
            p0, p1 = pieces[pi]
            L = p1 - p0
            sem = skv[slot_buf]
            gd.acquire("sync", ("kv", slot_buf))
            runs = []
            r0 = p0
            while r0 < p1:
                rk, idx = g2ri(r0)
                r1 = r0
                while r1 + 1 < p1 and g2ri(r1 + 1)[0] == rk:
                    r1 += 1
                runs.append((r0, r1 + 1, rk, idx))
                r0 = r1 + 1
            Lp = p1 - p0
            for (a, b, rk, idx) in runs:
                Lr = b - a
                for hh in range(2):
                    gh = hg * 2 + hh
                    C.dma("sync",
                          kvr[slot_buf][:, hh * Lp * 128 + (a - p0) * 128:
                                        hh * Lp * 128 + (b - p0) * 128],
                          scr["kgb"].ap()
                          .rearrange("(r c p) t -> r c p t", r=2, p=128)
                          [rk, gh, :, idx:idx + Lr * 128], sem)
                C.dma("sync",
                      kvr[slot_buf][:, 2048 + (a - p0) * 256:
                                    2048 + (a - p0) * 256 + Lr * 256]
                      .rearrange("p (l x) -> p l x", l=Lr),
                      scr["vgb"].ap()
                      .rearrange("(r l p) d -> r l p d", r=2, p=128)
                      [rk, idx // 128: idx // 128 + Lr, :,
                       (hg * 2) * 128:(hg * 2 + 2) * 128]
                      .rearrange("l p d -> p l d"), sem)
            return sem.n

        kvt = [None, None]
        kvt[0] = load_piece(0, 0, 0)
        for hg in range(8):
            units = []
            for pi, (p0, p1) in enumerate(pieces):
                for itl in range(p1 - p0):
                    for hh in range(2):
                        units.append((p0 + itl, hh, pi, pi % 2, itl, p1 - p0))
            ss_t3 = {}

            def emit_qk3(ui, u):
                it, hh, pi, sbuf_i, itl, Lp_ = u
                gh = hg * 2 + hh
                sti = (it * 2 + hh) % 4
                gd.acquire("tensor", ("st", sti))
                qh = qsb[:, gh * 512:(gh + 1) * 512]
                C.op("tensor", lambda e, sti=sti, sbuf_i=sbuf_i, itl=itl,
                     hh=hh, qh=qh, Lp_=Lp_:
                     e.matmul(bank(sti),
                              kvr[sbuf_i][:, hh * Lp_ * 128 + itl * 128:
                                          hh * Lp_ * 128 + (itl + 1) * 128],
                              qh, start=True, stop=False))
                C.op("tensor", lambda e, sti=sti, gh=gh:
                     e.matmul(bank(sti), ehmat[:, ts(gh)], crow_sb[:, :],
                              start=False, stop=True), sem=sm)
                kbv = kb_sb[:, it * 16 + gh: it * 16 + gh + 1]
                if it in MASKPOS[slot]:
                    mj = MASKPOS[slot].index(it)
                    C.wait("vector", sm, sm.n)
                    gd.acquire("vector", ("ssb", it % 2))
                    C.op("vector", lambda e, sti=sti, mj=mj, it=it:
                         e.tensor_tensor(s_sb[it % 2], bank(sti),
                                         masks8[:, mj * 512:(mj + 1) * 512],
                                         ALU.add), sem=sv)
                    gd.use(("st", sti), sv)
                    C.wait("scalar", sv, sv.n)
                    gd.acquire("scalar", ("p", (it * 2 + hh) % 4))
                    ss_t3[ui] = C.op("scalar", lambda e, it=it, hh=hh, kbv=kbv:
                                     e.activation(p_sb[(it * 2 + hh) % 4],
                                                  s_sb[it % 2],
                                                  AF.Exp, bias=kbv, scale=ISQD),
                                     sem=ss)
                    gd.use(("ssb", it % 2), ss)
                else:
                    C.wait("scalar", sm, sm.n)
                    gd.acquire("scalar", ("p", (it * 2 + hh) % 4))
                    ss_t3[ui] = C.op("scalar", lambda e, sti=sti, it=it, hh=hh,
                                     kbv=kbv:
                                     e.activation(p_sb[(it * 2 + hh) % 4],
                                                  bank(sti),
                                                  AF.Exp, bias=kbv, scale=ISQD),
                                     sem=ss)
                    gd.use(("st", sti), ss)

            def emit_pv3(ui, u, last_of_piece):
                it, hh, pi, sbuf_i, itl, Lp_ = u
                pb = p_sb[(it * 2 + hh) % 4]
                if it == 0:
                    gd.acquire("tensor", ("bank", 4 + hh))
                    gd.acquire("tensor", ("bank", 6 + hh))
                C.wait("tensor", ss, ss_t3[ui])
                C.op("tensor", lambda e, hh=hh, sbuf_i=sbuf_i, itl=itl, pb=pb,
                     it=it, Lp_=Lp_, ktn=ktn:
                     e.matmul(bank(4 + hh),
                              kvr[sbuf_i][:, 2048 + itl * 256 + hh * 128:
                                          2048 + itl * 256 + (hh + 1) * 128],
                              pb, start=(it == 0), stop=(it == ktn - 1)))
                C.op("tensor", lambda e, hh=hh, pb=pb, it=it, ktn=ktn:
                     e.matmul(bank(6 + hh), onesb[:, :], pb,
                              start=(it == 0), stop=(it == ktn - 1)), sem=sm)
                gd.use(("p", (it * 2 + hh) % 4), sm)
                if last_of_piece:
                    gd.use(("kv", sbuf_i), sm)

            for ui, u in enumerate(units):
                it, hh, pi, sbuf_i, itl, Lp_ = u
                if hh == 0 and itl == 0:
                    C.wait("tensor", skv[sbuf_i], kvt[sbuf_i])
                emit_qk3(ui, u)
                if ui >= 2:
                    up = units[ui - 2]
                    emit_pv3(ui - 2, up, up[2] != units[ui - 1][2])
                if hh == 1 and itl == 0:
                    if pi + 1 < len(pieces):
                        kvt[(pi + 1) % 2] = load_piece(pi + 1, (pi + 1) % 2, hg)
                    elif hg + 1 < 8:
                        kvt[0] = load_piece(0, 0, hg + 1)
            emit_pv3(len(units) - 2, units[-2], units[-2][2] != units[-1][2])
            emit_pv3(len(units) - 1, units[-1], True)
            for hh in range(2):
                gh = hg * 2 + hh
                C.wait("scalar", sm, sm.n)
                gd.acquire("scalar", ("rec",))
                C.op("scalar", lambda e, hh=hh: e.activation(lnt, bank(6 + hh), AF.Ln))
                t_rec = C.op("scalar", lambda e:
                             e.activation(rec_sb, lnt, AF.Exp, scale=-1.0), sem=ss)
                gd.use(("bank", 6 + hh), ss)
                C.wait("vector", ss, t_rec)
                gd.acquire("vector", ("oall", gh))
                C.op("vector", lambda e, gh=gh, hh=hh:
                     e.tensor_tensor(o_all[:, gh * 512:(gh + 1) * 512], bank(4 + hh),
                                     rec_sb, ALU.mult), sem=sv)
                gd.use(("bank", 4 + hh), sv)
                gd.use(("rec",), sv)
        gd.use(("qsb",), sm)
        gd.use(("masks8",), sv)
        gd.use(("kb",), ss)
        if slot + 1 < NTC:
            ktn2 = KTN[slot + 1]
            gd.acquire("scalar", ("qsb",))
            vq_t = C.dma("scalar", qsb.rearrange("p (oc t) -> p oc t", oc=16),
                         scr["qb_scr"].ap().rearrange("(oc p) t -> p oc t", p=128)
                         [:, :, (slot + 1) * CH:(slot + 2) * CH], sqp3)
            gd.acquire("scalar", ("masks8",))
            C.dma("scalar", masks8.rearrange("p (j x) -> p j x", j=8),
                  ein["mdyn"].ap()[slot + 1].rearrange("j p x -> p j x"), sms)
            gd.acquire("scalar", ("kb",))
            C.dma("scalar",
                  kb_sb[:, 0:ktn2 * 16].rearrange("p (i h) -> p i h", i=ktn2),
                  ein["kbias"].ap()[slot + 1, 0:ktn2].rearrange("i p h -> p i h"),
                  sms)
            vms_t = sms.n
        C.wait("tensor", sv, sv.n)
        C.wait("sync", sxs, sxs.n)
        gd.acquire("sync", ("xcur",))
        vx = C.dma("sync", xcur.rearrange("p (oc t) -> p oc t", oc=16),
                   scr["xspill"].ap().rearrange("oc p t -> p oc t")
                   [:, :, slot * CH:(slot + 1) * CH], sq_)
        C.barrier([(sq_, vx)], engines=("vector",))
        def wov3_load(oc):
            sl = oc % 2
            gd.acquire("sync", ("wov", sl))
            for hf in range(2):
                C.dma("sync",
                      wov[sl][:, hf * 1024:(hf + 1) * 1024]
                      .rearrange("p (c i) -> p c i", c=8),
                      scr["bWob"].ap().rearrange("(c p) o -> c p o", p=128)
                      [hf * 8:(hf + 1) * 8, :, oc * 128:(oc + 1) * 128]
                      .rearrange("c p o -> p c o"), sld[sl])
            return sld[sl].n

        wovt3 = [None, None]
        wovt3[0] = wov3_load(0)
        for oc in range(NFC):
            sl = oc % 2
            if oc + 1 < NFC:
                wovt3[(oc + 1) % 2] = wov3_load(oc + 1)
            C.wait("tensor", sld[sl], wovt3[sl])
            pb_i = 2 + sl
            gd.acquire("tensor", ("bank", pb_i))
            for dc in range(NFC):
                C.op("tensor", lambda e, pb_i=pb_i, sl=sl, dc=dc:
                     e.matmul(bank(pb_i), wov[sl][:, ts(dc)],
                              o_all[:, dc * 512:(dc + 1) * 512],
                              start=(dc == 0), stop=(dc == NFC - 1)),
                     sem=(sm if dc == NFC - 1 else None))
            gd.use(("wov", sl), sm)
            if oc == NFC - 1:
                for hh2 in range(NH):
                    gd.use(("oall", hh2), sm)
            C.wait("vector", sm, sm.n)
            xslice = xcur[:, oc * 512:(oc + 1) * 512]
            C.op("vector", lambda e, pb_i=pb_i, xslice=xslice, oc=oc:
                 e.scalar_tensor_tensor(xslice, bank(pb_i),
                                        bias_sb[:, 48 + oc: 49 + oc], xslice,
                                        ALU.add, ALU.add), sem=sv)
            gd.use(("bank", pb_i), sv)
            gd.use(("st", pb_i), sv)
        C.wait("sync", sv, sv.n)
        for oc in range(NFC):
            C.dma("sync", scr["xspill"][oc, :, slot * CH:(slot + 1) * CH],
                  xcur[:, oc * 512:(oc + 1) * 512], sxs)
        gd.use(("xcur",), sxs)
    sxr = C.sem("p3xr")
    C.wait("sync", sxs, sxs.n)
    for fc in range(NFC):
        C.dma("sync", S["x_fm"][:, fc * TOK:(fc + 1) * TOK], scr["xspill"][fc, :, :],
              sxr)
    return [(sxr, sxr.n), (sm, sm.n), (sv, sv.n), (ss, ss.n)]


def phase_P9(S, fence):
    C, bank = S["C"], S["bank"]
    x_fm, xout = S["x_fm"], S["xout"]
    identf = S["cb"]["identf"]
    sd = [C.sem(f"p9d{i}") for i in range(2)]
    sm = C.sem("p9m")
    sv = C.sem("p9v")
    ot = [S["arAf"][:, 0:2048], S["arAf"][:, 2048:4096]]
    C.barrier(fence, engines=("tensor", "vector", "scalar", "sync", "gpsimd"))
    gi = 0
    for tt in range(16):
        bi = tt % 2
        buf = ot[bi]
        for g in range(4):
            bk = bank(gi % 2)
            if gi >= 2:
                C.wait("tensor", sv, gi - 1)
            for f4 in range(4):
                fc = g * 4 + f4
                C.op("tensor", lambda e, bk=bk, fc=fc, f4=f4, tt=tt:
                     e.transpose(bk[:, ts(f4)],
                                 x_fm[:, fc * TOK + tt * 128: fc * TOK + (tt + 1) * 128],
                                 identf[:, :]),
                     sem=(sm if f4 == 3 else None))
            C.wait("vector", sm, sm.n)
            if tt >= 2 and g == 0:
                C.wait("vector", sd[bi], 16 * (tt // 2))
            C.op("vector", lambda e, bk=bk, buf=buf, g=g:
                 e.tensor_copy(buf[:, g * 512:(g + 1) * 512], bk), sem=sv, drain=True)
            gi += 1
        C.wait("sync", sv, sv.n)
        C.dma("sync", xout[ts(tt), :], buf, sd[bi])
    C.wait("sync", sd[0], sd[0].n)
    C.wait("sync", sd[1], sd[1].n)


def kernel(**inputs):
    xfull = np.ascontiguousarray(np.asarray(inputs["L0_attn_a_x"], dtype=np.float32))
    W = {k: np.asarray(inputs[k], dtype=np.float32) for k in
         ("L0_attn_a_Wq", "L0_attn_a_Wk", "L0_attn_a_Wv", "L0_attn_a_Wo",
          "L0_attn_b_Wq", "L0_attn_b_Wk", "L0_attn_b_Wv", "L0_attn_b_Wo",
          "L0_ffn_w1", "L0_ffn_w2", "L0_ffn_w3",
          "L0_attn_b_bq", "L0_attn_b_bk", "L0_attn_b_bv", "L0_attn_b_bo")}
    fw1p = np.zeros((6144, D), dtype=np.float32)
    fw1p[:HID] = W["L0_ffn_w1"]
    fw2p = np.zeros((6144, D), dtype=np.float32)
    fw2p[:HID] = W["L0_ffn_w2"]
    fw3p = np.zeros((D, 6144), dtype=np.float32)
    fw3p[:, :HID] = W["L0_ffn_w3"]
    nc = build_nc(_PHASES)
    in_maps = []
    for c in range(8):
        b = c // 2
        pos = np.concatenate([np.arange(t0, t0 + CH) for t0 in chunk_starts(c)])
        m = {"xin": np.ascontiguousarray(xfull[b, pos, :])}
        m["aWq"] = np.ascontiguousarray(W["L0_attn_a_Wq"][256 * c:256 * (c + 1), :])
        m["aWk"] = np.ascontiguousarray(W["L0_attn_a_Wk"][128 * (c % 4):128 * (c % 4 + 1), :])
        m["aWv"] = np.ascontiguousarray(W["L0_attn_a_Wv"][128 * (c % 4):128 * (c % 4 + 1), :])
        m["aWo"] = np.ascontiguousarray(W["L0_attn_a_Wo"][256 * c:256 * (c + 1), :])
        for k, nm in (("bWq", "L0_attn_b_Wq"), ("bWk", "L0_attn_b_Wk"),
                      ("bWv", "L0_attn_b_Wv"), ("bWo", "L0_attn_b_Wo")):
            m[k] = np.ascontiguousarray(W[nm][256 * c:256 * (c + 1), :])
        m["fw1"] = np.ascontiguousarray(fw1p[768 * c:768 * (c + 1), :])
        m["fw2"] = np.ascontiguousarray(fw2p[768 * c:768 * (c + 1), :])
        m["fw3"] = np.ascontiguousarray(fw3p[:, 768 * c:768 * (c + 1)])
        for k, nm in (("bbq", "L0_attn_b_bq"), ("bbk", "L0_attn_b_bk"),
                      ("bbv", "L0_attn_b_bv"), ("bbo", "L0_attn_b_bo")):
            m[k] = np.ascontiguousarray(W[nm].reshape(16, 128))
        hc = host_consts(c)
        for k in ("ropecos", "ropesin", "rotT", "identf", "identb", "onesb", "crow",
                  "ehmat", "masks", "kbias", "kvoff", "self4", "mdyn"):
            m[k] = hc[k]
        in_maps.append(m)
    res = run_bass_kernel_spmd(nc, in_maps, core_ids=list(range(8)))
    out = np.empty((B, S, D), dtype=np.float32)
    for c in range(8):
        b = c // 2
        pos = np.concatenate([np.arange(t0, t0 + CH) for t0 in chunk_starts(c)])
        out[b, pos, :] = res.results[c]["xout"]
    return out
